# version 04
# baseline (speedup 1.0000x reference)
import math
from contextlib import ExitStack

import numpy as np
import concourse.bass as bass
import concourse.mybir as mybir
from concourse.bass_utils import run_bass_kernel_spmd

F32 = mybir.dt.float32
BF16 = mybir.dt.bfloat16
I8 = mybir.dt.int8
AF = mybir.ActivationFunctionType
ALU = mybir.AluOpType
AX = mybir.AxisListType

B, T, D = 4, 4096, 1024
RPC = T // 2
NIC = RPC // 128
NFC = D // 128
NW = RPC // 512
H, DH, BLK = 8, 128, 64
NJT = T // 128
RANK, DFF = 256, 4096
EPS_LN = 1e-5

ENGS = ("sync", "scalar", "vector", "tensor", "gpsimd")


def host_consts(parity):
    i_glob = np.arange(RPC, dtype=np.float64) + parity * RPC

    c = {}
    c["ident"] = np.eye(128, dtype=np.float32)
    c["identh"] = (0.5 * np.eye(128)).astype(np.float32)
    import ml_dtypes
    c["identb"] = np.eye(128).astype(ml_dtypes.bfloat16)
    c["onescol"] = np.ones((128, 1), np.float32)
    c["ones1"] = np.ones((1, 128), np.float32)

    half = DH // 2
    inv = 1.0 / 10000.0 ** (np.arange(half, dtype=np.float64) / half)
    ang = i_glob[None, :] * inv[:, None]
    cos, sin = np.cos(ang), np.sin(ang)
    c["ctab"] = np.concatenate([cos, cos], 0).astype(np.float32)
    c["stab"] = np.concatenate([-sin, sin], 0).astype(np.float32)

    t = np.linspace(-1.0, 1.0, BLK)
    W = np.stack([np.ones_like(t), t], 1)
    Wa = W / (np.linalg.norm(W, axis=0, keepdims=True) + 1e-6)

    wtr = np.zeros((128, NIC * 64), np.float64)
    for ic in range(NIC):
        for p in range(128):
            il = ic * 128 + p
            nl, a = il // 64, il % 64
            for kq in range(2):
                cl = 2 * nl + kq
                wtr[p, ic * 64 + cl] = Wa[a, kq]
    c["wtrows"] = wtr.astype(np.float32)

    wtT = np.zeros((64, RPC), np.float64)
    for il in range(RPC):
        nl, a = il // 64, il % 64
        wtT[2 * nl, il] = Wa[a, 0]
        wtT[2 * nl + 1, il] = Wa[a, 1]
    c["wtT"] = wtT.astype(np.float32)

    wf = np.zeros((65, 16 * 128), np.float64)
    for tt_ in range(16):
        for jj in range(128):
            blkk, a = jj // 64, jj % 64
            for kk in range(2):
                wf[4 * tt_ + 2 * blkk + kk, tt_ * 128 + jj] = Wa[a, kk]
        wf[64, tt_ * 128:tt_ * 128 + 128] = 1.0
    c["wfar"] = wf.astype(np.float32)

    u = Wa.sum(0)
    d2 = (Wa ** 2).sum(0)
    c01 = (Wa[:, 0] * Wa[:, 1]).sum()
    P1 = np.vstack([np.zeros(2), np.cumsum(Wa, 0)])
    PG = np.vstack([np.zeros(2), np.cumsum(Wa ** 2, 0)])
    PC = np.concatenate([[0.0], np.cumsum(Wa[:, 0] * Wa[:, 1])])
    Mu = np.zeros((128, RPC), np.float64)
    Gm = np.zeros((128, RPC), np.float64)
    Px = np.zeros((128, RPC), np.float64)
    ig = i_glob.astype(np.int64)
    n_i, a_i = ig // 64, ig % 64
    for cp in range(128):
        m, kk = cp // 2, cp % 2
        full = (m <= n_i - 2).astype(np.float64)
        bnd = (m == n_i - 1).astype(np.float64)
        Mu[cp] = u[kk] * full + P1[a_i, kk] * bnd
        Gm[cp] = d2[kk] * full + PG[a_i, kk] * bnd
        if kk == 0:
            Px[cp] = 2.0 * c01 * full + 2.0 * PC[a_i] * bnd
    c["muT"] = Mu.astype(np.float32)
    c["gmT"] = Gm.astype(np.float32)
    c["pxT"] = Px.astype(np.float32)
    c["ninv"] = (1.0 / (ig + 2.0))[None, :].astype(np.float32)
    cm = np.zeros((128, 5 * 512), np.float32)
    fi = np.arange(512)
    for o in range(4):
        keep = (fi[None, :] - 128 * o) >= np.arange(128)[:, None]
        cm[:, o * 512:(o + 1) * 512] = np.where(keep, 0.0, -1e30)
    c["cmask"] = cm
    bm = np.zeros((128, 4 * 192), np.float32)
    lj = np.arange(192)[None, :]
    p_ = np.arange(128)[:, None]
    inband = ((p_ - lj + 64 >= 0) & (p_ - lj + 64 <= 64))
    ok0 = inband & (lj >= 64 - 0) if False else inband
    bm[:, 0:192] = inband.astype(np.float32)
    bm[:, 192:384] = np.where(inband, 0.0, -1e30)
    in0 = inband & ((lj - 64 + 0) >= 0) if parity == 0 else inband
    in0 = inband & (lj >= 64) if parity == 0 else inband
    bm[:, 384:576] = in0.astype(np.float32)
    bm[:, 576:768] = np.where(in0, 0.0, -1e30)
    c["bmask"] = bm
    fm = np.zeros((128, 5 * 512), np.float32)
    fi = np.arange(512)[None, :]
    for k in range(5):
        ojt = -64 + 128 * k
        band = (p_ - fi + ojt) >= 0
        fm[:, k * 512:(k + 1) * 512] = np.where(band, -1e30, 0.0)
    c["fmask"] = fm
    return c


class Prog:
    def __init__(self, dbg_names=()):
        self.nc = bass.Bass("TRN2", target_bir_lowering=False)
        self.stack = ExitStack()
        self.tape = {e: [] for e in ENGS}
        self._sub = None
        self.cnt = {}
        self.sems = {}
        self.dbg_names = set(dbg_names)
        self.dbg = {}
        self.slot_eng = {}
        self.sfx = ""
        self.pcnt = {"E": {}, "O": {}}
        self.parity_cache = {}
        for s in ("qs", "qa", "qg", "pe", "act", "dve", "gps", "cc", "sy"):
            self.sems[s] = self.stack.enter_context(self.nc.semaphore(s))
            self.cnt[s] = 0

    def sb(self, name, p, w, dtype=F32):
        return self.stack.enter_context(self.nc.sbuf_tensor(name, [p, w], dtype))

    def psum(self, name):
        return self.stack.enter_context(self.nc.psum_tensor(name, [128, 512], F32))

    def dram(self, name, shape, dtype=F32, kind="Internal", addr_space="Local"):
        return self.nc.dram_tensor(name, list(shape), dtype, kind=kind,
                                   addr_space=addr_space)

    def dbg_out(self, name, shape):
        if name in self.dbg_names:
            self.dbg[name] = self.dram("dbg_" + name, shape, F32, kind="ExternalOutput").ap()
            return self.dbg[name]
        return None

    def key(self, sem):
        if self.sfx and sem not in ("sy", "cc"):
            return sem + self.sfx
        return sem

    def ct(self, sem):
        return self.cnt.get(self.key(sem), 0)

    def _sem(self, name):
        if name not in self.sems:
            self.sems[name] = self.stack.enter_context(self.nc.semaphore(name))
        return self.sems[name]

    def _emit(self, eng, fn):
        (self._sub if self._sub is not None else self.tape)[eng].append(fn)

    ENG_SEM = {"vector": "dve", "scalar": "act", "gpsimd": "gps"}

    def op(self, eng, fn, inc=None, n=1):
        es = self.ENG_SEM.get(eng)
        if es is not None:
            if inc is None:
                inc = es
            if inc == es:
                self.wait(eng, es, self.ct(es))
        if inc is None:
            self._emit(eng, fn)
            return None
        k = self.key(inc)
        def g(e, fn=fn, s=self._sem(k), n=n):
            fn(e).then_inc(s, n)
        self._emit(eng, g)
        self.cnt[k] = self.cnt.get(k, 0) + n
        return self.cnt[k]

    def dma(self, eng, out, in_, slot=None, **kw):
        sem = slot if slot is not None else {"sync": "qs", "scalar": "qa", "gpsimd": "qg"}[eng]
        sem = self.key(sem)
        self.slot_eng[sem] = eng
        def g(e, out=out, in_=in_, s=self._sem(sem), kw=kw):
            e.dma_start(out=out, in_=in_, **kw).then_inc(s, 16)
        self._emit(eng, g)
        self.cnt[sem] = self.cnt.get(sem, 0) + 16
        return self.cnt[sem]

    def wait(self, eng, sem, n):
        def g(e, s=self._sem(self.key(sem)), n=n):
            e.wait_ge(s, n)
        self._emit(eng, g)

    def wait_now(self, eng, sem):
        self.wait(eng, sem, self.ct(sem))

    def barrier(self, engines=ENGS):
        for sem, eng in [("qs", "sync"), ("qa", "scalar"), ("qg", "gpsimd")] + \
                list(self.slot_eng.items()):
            if eng in engines:
                self.wait(eng, sem, self.cnt.get(sem, 0))
        for e, es in (("vector", "dve"), ("scalar", "act"),
                      ("gpsimd", "gps"), ("tensor", "pe")):
            if e in engines:
                self.wait(e, es, self.cnt.get(es, 0))
        for e in engines:
            def g(eng, s=self.sems["sy"]):
                eng.nop().then_inc(s, 1)
            self._emit(e, g)
            self.cnt["sy"] += 1
        tgt = self.cnt["sy"]
        for e in engines:
            self.wait(e, "sy", tgt)

    def parity(self, fn_even, fn_odd):
        base_cnt = dict(self.cnt)
        base_slots = dict(self.slot_eng)
        tapes = []
        for sfx, fn in (("E", fn_even), ("O", fn_odd)):
            self.sfx = sfx
            self.cnt = dict(base_cnt)
            self.cnt.update(self.pcnt[sfx])
            self.slot_eng = dict(base_slots)
            self._sub = {e: [] for e in ENGS}
            fn()
            for e in ENGS:
                es = self.ENG_SEM.get(e)
                if es is not None:
                    self.wait(e, es, self.ct(es))
                for sem, eng2 in list(self.slot_eng.items()):
                    if eng2 == e and sem.endswith(sfx):
                        def gg(eng_, s=self._sem(sem), n=self.cnt.get(sem, 0)):
                            eng_.wait_ge(s, n)
                        self._emit(e, gg)
                def gsy(eng_, s=self.sems["sy"]):
                    eng_.nop().then_inc(s, 1)
                self._emit(e, gsy)
            self.pcnt[sfx] = {k: v for k, v in self.cnt.items() if k.endswith(sfx)}
            tapes.append(self._sub)
        self._sub = None
        self.sfx = ""
        self.cnt = dict(base_cnt)
        self.slot_eng = dict(base_slots)
        self.cnt["sy"] = base_cnt.get("sy", 0) + len(ENGS)
        tgt = self.cnt["sy"]
        for e in ENGS:
            t0, t1 = tapes[0][e], tapes[1][e]
            def mk(t0=t0, t1=t1, ename=e):
                def run(eng):
                    if ename not in self.parity_cache:
                        self.parity_cache[ename] = eng.partition_id() % 2
                    par = self.parity_cache[ename]
                    for body in eng.Switch(par, 2):
                        for f in (t0 if body == 0 else t1):
                            f(eng)
                return run
            self.tape[e].append(mk())
        for e in ENGS:
            self.wait(e, "sy", tgt)

    def finalize(self):
        nc = self.nc
        with nc.Block() as block:
            @block.sync
            def _(sync):
                for f in self.tape["sync"]:
                    f(sync)

            @block.scalar
            def _(scalar):
                for f in self.tape["scalar"]:
                    f(scalar)

            @block.vector
            def _(vector):
                for f in self.tape["vector"]:
                    f(vector)

            @block.tensor
            def _(tensor):
                for f in self.tape["tensor"]:
                    f(tensor)

            @block.gpsimd
            def _(gpsimd):
                for f in self.tape["gpsimd"]:
                    f(gpsimd)
        self.stack.close()
        return nc


def build_program(upto="all", dbg_names=(), p5_trunc=0):
    P = Prog(dbg_names)

    def einp(name, shape, dtype=F32):
        return P.dram(name, shape, dtype, kind="ExternalInput").ap()

    x_in = einp("x", [RPC, D])
    Wq = einp("Wq", [D, D]); bq = einp("bq", [128, 8])
    Wk = einp("Wk", [D, D]); bk = einp("bk", [128, 8])
    Wv = einp("Wv", [D, D]); bv = einp("bv", [1, D])
    U1 = einp("U1", [D, RANK]); V1 = einp("V1", [RANK, DFF]); b1 = einp("b1", [128, 32])
    U2 = einp("U2", [DFF, RANK]); V2 = einp("V2", [RANK, D]); b2 = einp("b2", [1, D])
    WQ1 = einp("WQ1", [D, D]); WK1 = einp("WK1", [D, D])
    WV1 = einp("WV1", [D, D]); WO1 = einp("WO1", [D, D])
    lnw = einp("lnw", [128, 8]); lnb = einp("lnb", [128, 8])
    A1 = einp("A1", [H, 128, 128])
    knT = einp("knT", [128, H])
    W8 = einp("W8", [D, D], I8)
    wscale = einp("wscale", [1, D])
    fbias = einp("fbias", [1, D])
    ident = einp("ident", [128, 128]); identh = einp("identh", [128, 128])
    identb = einp("identb", [128, 128], BF16)
    onescol = einp("onescol", [128, 1]); ones1 = einp("ones1", [1, 128])
    ctab_i = einp("ctab", [128, RPC]); stab_i = einp("stab", [128, RPC])
    wtrows_i = einp("wtrows", [128, NIC * 64])
    wtT_i = einp("wtT", [64, RPC])
    wfar_i = einp("wfar", [65, 16 * 128])
    muT_i = einp("muT", [128, RPC]); gmT_i = einp("gmT", [128, RPC]); pxT_i = einp("pxT", [128, RPC])
    ninv_i = einp("ninv", [1, RPC])
    cmask_i = einp("cmask", [128, 5 * 512])

    out = P.dram("out", [RPC, D], F32, kind="ExternalOutput").ap()

    d_hq = P.dram("hq", [NFC * 128, RPC])
    d_hkown = P.dram("hkown", [NFC * 128, RPC])
    d_hvown = P.dram("hvown", [RPC, D])
    d_e1kout = P.dram("e1kout", [8 * NFC * 128, RPC])
    d_e1vout = P.dram("e1vout", [8 * RPC, D])
    d_hkall = P.dram("hkall", [2 * NFC * 128, T // 2])
    d_hvall = P.dram("hvall", [T, D])
    d_hraw = P.dram("hraw", [104 * 128, 512])
    d_e2in = P.dram("e2in", [128, 64])
    d_e2out = P.dram("e2out", [2 * 128, 64])
    d_hq1 = P.dram("hq1", [H * 128, RPC])
    d_hk1 = P.dram("hk1", [H * 128, RPC])
    d_e3in = P.dram("e3in", [2305, 1024])
    d_e3outp = P.dram("e3outp", [2 * 2304, 1024])
    d_e3max = P.dram("e3max", [8, 1024], addr_space="Shared")
    d_hv1all = P.dram("hv1all", [T, D])
    d_hy = P.dram("hy", [H * 128, RPC])
    d_hxq = P.dram("hxq", [NFC * 128, RPC], BF16)

    XT = P.sb("XT", 128, NIC * 1024)
    XNT = P.sb("XNT", 128, NFC * RPC)
    IDT = P.sb("IDT", 128, 128)
    IDTH = P.sb("IDTH", 128, 128)
    ONESC = P.sb("ONESC", 128, 1)
    ONES1 = P.sb("ONES1", 1, 128)
    BQ = P.sb("BQ", 128, 8); BK = P.sb("BK", 128, 8)
    B1 = P.sb("B1", 128, 32)
    LNW = P.sb("LNW", 128, 8); LNB = P.sb("LNB", 128, 8)
    KNT = P.sb("KNT", 128, H)
    M0 = P.sb("M0", 128, NJT)
    S0 = P.sb("S0", 128, NJT)
    EB0 = P.sb("EB0", 128, NJT)
    PW = 14336
    POOL = P.sb("POOL", 128, PW)
    PB = [P.psum(f"pb{i}") for i in range(8)]

    def pool(o, w):
        assert o + w <= PW, (o, w)
        return POOL[:, o:o + w]

    def prow(o, w):
        assert o + w <= PW, (o, w)
        return POOL[0:1, o:o + w]

    def mm(lhsT, rhs, outp, start, stop, inc=None, n=1):
        return P.op("tensor", lambda e, a=lhsT, b=rhs, o=outp, st=start, sp=stop:
                    e.matmul(o, a, b, start=st, stop=sp), inc=inc, n=n)

    def transp(inp, outp, inc="pe", half=False, np_=128):
        idt = IDTH if half else IDT
        return P.op("tensor", lambda e, i=inp, o=outp, d=idt, n_=np_:
                    e.transpose(o, i, d[0:n_, 0:n_]), inc=inc)

    def act(outp, inp, func, bias=0.0, scale=1.0, accum=None, inc="act"):
        return P.op("scalar", lambda e, o=outp, i=inp, f=func, b=bias, s=scale, a=accum:
                    e.activation(o, i, f, bias=b, scale=s, accum_out=a), inc=inc)

    def tt(outp, a, b_, opx, inc="dve", eng="vector"):
        return P.op(eng, lambda e, o=outp, x=a, y=b_, op=opx:
                    e.tensor_tensor(o, x, y, op), inc=inc)

    def ts(outp, a, s1, s2, op0, op1=ALU.bypass, inc="dve", eng="vector"):
        return P.op(eng, lambda e, o=outp, x=a, u=s1, v=s2, p0=op0, p1=op1:
                    e.tensor_scalar(o, x, u, v, p0, p1), inc=inc)

    def stt(outp, in0, scal, in1, op0, op1, inc="dve", eng="vector"):
        return P.op(eng, lambda e, o=outp, x=in0, s=scal, y=in1, p0=op0, p1=op1:
                    e.scalar_tensor_tensor(o, x, s, y, p0, p1), inc=inc)

    def rsum(outp, inp, inc="dve", eng="vector"):
        return P.op(eng, lambda e, o=outp, i=inp: e.reduce_sum(o, i, axis=AX.X), inc=inc)

    def rmax(outp, inp, absval=False, inc="dve", eng="vector"):
        return P.op(eng, lambda e, o=outp, i=inp, a=absval:
                    e.reduce_max(o, i, axis=AX.X, apply_absolute_value=a), inc=inc)

    def recip(outp, inp, inc="dve"):
        return P.op("vector", lambda e, o=outp, i=inp: e.reciprocal(o, i), inc=inc)

    def copy_ev(outp, inp, inc="dve", eng="vector"):
        return P.op(eng, lambda e, o=outp, i=inp: e.tensor_copy(o, i), inc=inc)

    def bcast_row(row_ap, dest_sb, width, bank):
        for w0 in range(0, width, 512):
            w = min(512, width - w0)
            c1 = mm(ONES1[:, :], row_ap[:, w0:w0 + w], bank[:, 0:w], True, True, inc="pe")
            P.wait("vector", "pe", c1)
            cd = copy_ev(dest_sb[:, w0:w0 + w], bank[:, 0:w])
            P.wait("tensor", "dve", cd)

    for ap_, t in [(ident, IDT), (identh, IDTH), (onescol, ONESC), (ones1, ONES1),
                   (bq, BQ), (bk, BK), (b1, B1), (lnw, LNW), (lnb, LNB), (knT, KNT)]:
        P.dma("sync", t[:, :], ap_)
    for ic in range(NIC):
        P.dma("sync", XT[:, ic * 1024:(ic + 1) * 1024],
              x_in[ic * 128:(ic + 1) * 128, :])
    P.wait_now("tensor", "qs")
    P.wait_now("vector", "qs")
    P.wait_now("scalar", "qs")
    P.barrier()

    def ln_into_xnt():
        mcol, s2col = pool(0, NIC), pool(16, NIC)
        msqc, varc, rstdc = pool(32, NIC), pool(48, NIC), pool(64, NIC)
        sq, xn_t = pool(80, 1024), pool(1104, 1024)
        prev_r2 = [0]
        for ic in range(NIC):
            xc = XT[:, ic * 1024:(ic + 1) * 1024]
            rsum(mcol[:, ic:ic + 1], xc)
            P.wait("scalar", "dve", prev_r2[0])
            c_sq = act(sq, xc, AF.Square)
            P.wait("vector", "act", c_sq)
            prev_r2[0] = rsum(s2col[:, ic:ic + 1], sq)
        c_m = ts(mcol, mcol, 1.0 / 1024.0, None, ALU.mult)
        P.wait("scalar", "dve", c_m)
        c_vv = act(msqc, mcol, AF.Square)
        P.wait("vector", "act", c_vv)
        ts(s2col, s2col, 1.0 / 1024.0, None, ALU.mult)
        ts(msqc, msqc, -1.0, None, ALU.mult)
        tt(varc, s2col, msqc, ALU.add)
        c_v = ts(varc, varc, EPS_LN, None, ALU.add)
        P.wait("scalar", "dve", c_v)
        c_sa = act(rstdc, varc, AF.Sqrt)
        P.wait("vector", "act", c_sa)
        c_rc = recip(rstdc, rstdc)
        dbg_st = P.dbg_out("stats0", [128, 10])
        if dbg_st is not None:
            P.wait("sync", "dve", c_rc)
            P.dma("sync", dbg_st[:, 0:2], mcol[:, 14:16])
            P.dma("sync", dbg_st[:, 2:4], varc[:, 14:16])
            P.dma("sync", dbg_st[:, 4:6], rstdc[:, 14:16])
            P.dma("sync", dbg_st[:, 6:8], s2col[:, 14:16])
            P.dma("sync", dbg_st[:, 8:10], msqc[:, 14:16])
        tpose_dve = []
        for ic in range(NIC):
            xc = XT[:, ic * 1024:(ic + 1) * 1024]
            c_xn = ts(xn_t, xc, mcol[:, ic:ic + 1], rstdc[:, ic:ic + 1],
                      ALU.subtract, ALU.mult)
            P.wait("tensor", "dve", c_xn)
            for fc in range(NFC):
                bank = PB[6 + (fc % 2)]
                gi = ic * NFC + fc
                if gi >= 2:
                    P.wait("tensor", "dve", tpose_dve[gi - 2])
                c_t = transp(xn_t[:, fc * 128:(fc + 1) * 128], bank[:, 0:128])
                P.wait("vector", "pe", c_t)
                cd = copy_ev(XNT[:, fc * RPC + ic * 128: fc * RPC + (ic + 1) * 128],
                             bank[:, 0:128])
                tpose_dve.append(cd)

    ln_into_xnt()
    dxn = P.dbg_out("xnT", [NFC * 128, RPC])
    if dxn is not None:
        P.wait_now("sync", "dve")
        for fc in range(NFC):
            P.dma("sync", dxn[fc * 128:(fc + 1) * 128, :],
                  XNT[:, fc * RPC:(fc + 1) * RPC])
    P.barrier()

    if upto == "ln":
        for ic in range(NIC):
            P.dma("sync", out[ic * 128:(ic + 1) * 128, :],
                  XT[:, ic * 1024:(ic + 1) * 1024])
        P.wait_now("sync", "qs")
        return P

    wring = [pool(128 * i, 128) for i in range(4)]
    evacs = [pool(512, 512), pool(1024, 512)]
    vtile0 = pool(1536, 1024)
    vw = [pool(2560, 1024), pool(3584, 1024)]
    BVB = pool(4608, 1024)
    bvrow = prow(5632, 1024)
    c_bv = P.dma("sync", bvrow, bv, slot="rowld")
    P.wait("tensor", "rowld", c_bv)
    bcast_row(bvrow, BVB, 1024, PB[7])

    def proj_fmaj(W_ap, bias_sb, dest):
        mm_mark = {}
        ev_dma = []
        prev_evac = [0]
        k = 0
        for do in range(NFC):
            for di in range(NFC):
                slot = k % 4
                if k >= 4:
                    P.wait("sync", "pe", mm_mark[k - 4])
                wc = P.dma("sync", wring[slot], W_ap[di * 128:(di + 1) * 128,
                                                     do * 128:(do + 1) * 128],
                           slot=f"wr{slot}")
                P.wait("tensor", f"wr{slot}", wc)
                if di == 0:
                    P.wait("tensor", "act", prev_evac[0])
                for iw in range(NW):
                    mm(wring[slot], XNT[:, di * RPC + iw * 512: di * RPC + (iw + 1) * 512],
                       PB[iw][:, :], di == 0, di == NFC - 1,
                       inc="pe" if iw == NW - 1 else None)
                mm_mark[k] = P.ct("pe")
                k += 1
            pec = P.ct("pe")
            for iw in range(NW):
                ev = evacs[iw % 2]
                P.wait("scalar", "pe", pec)
                if len(ev_dma) >= 2:
                    P.wait("scalar", f"ev{iw % 2}", ev_dma[-2][1])
                ca = act(ev, PB[iw][:, :], AF.Identity, bias=bias_sb[:, do:do + 1])
                P.wait("scalar", "act", ca)
                prev_evac[0] = ca
                ev_dma.append((iw % 2,
                    P.dma("scalar", dest[do * 128:(do + 1) * 128, iw * 512:(iw + 1) * 512], ev,
                          slot=f"ev{iw % 2}")))
        P.wait_now("sync", "ev0")
        P.wait_now("sync", "ev1")

    proj_fmaj(Wq, BQ, d_hq.ap())
    proj_fmaj(Wk, BK, d_hkown.ap())

    mm_mark = {}
    ev_dma = []
    prev_stt = [0]
    for ic in range(NIC):
        for di in range(NFC):
            kk = ic * NFC + di
            slot = kk % 2
            if kk >= 2:
                P.wait("sync", "pe", mm_mark[kk - 2])
            wc = P.dma("sync", vw[slot], Wv[di * 128:(di + 1) * 128, :], slot=f"vw{slot}")
            P.wait("tensor", f"vw{slot}", wc)
            if di == 0:
                P.wait("tensor", "dve", prev_stt[0])
            for hf in range(2):
                mm(XNT[:, di * RPC + ic * 128: di * RPC + (ic + 1) * 128],
                   vw[slot][:, hf * 512:(hf + 1) * 512], PB[4 + hf][:, :],
                   di == 0, di == NFC - 1, inc="pe" if hf == 1 else None)
            mm_mark[kk] = P.ct("pe")
        P.wait_now("vector", "pe")
        if len(ev_dma) >= 1:
            P.wait("vector", "vst", ev_dma[-1])
        for hf in range(2):
            cs = stt(vtile0[:, hf * 512:(hf + 1) * 512], PB[4 + hf][:, :], 1.0,
                     BVB[:, hf * 512:(hf + 1) * 512], ALU.mult, ALU.add)
        prev_stt[0] = cs
        P.wait_now("sync", "dve")
        ev_dma.append(P.dma("sync", d_hvown.ap()[ic * 128:(ic + 1) * 128, :], vtile0,
                            slot="vst"))
    P.wait_now("sync", "vst")

    dq = P.dbg_out("q0", [NFC * 128, RPC])
    if dq is not None:
        P.dma("sync", dq, d_hq.ap())
    dk = P.dbg_out("k0", [NFC * 128, RPC])
    if dk is not None:
        P.dma("sync", dk, d_hkown.ap())
    dv = P.dbg_out("v0", [RPC, D])
    if dv is not None:
        P.dma("sync", dv, d_hvown.ap())
    P.barrier()

    B2B = pool(11776, 1024)
    b2row = prow(12800, 1024)
    c_b2 = P.dma("sync", b2row, b2, slot="rowld")
    P.wait("tensor", "rowld", c_b2)
    bcast_row(b2row, B2B, 1024, PB[7])

    u1big = pool(0, 2048)
    PF = [pool(2048, RPC), pool(4096, RPC)]
    for di in range(NFC):
        c_u1 = P.dma("sync", u1big[:, di * 256:(di + 1) * 256],
                     U1[di * 128:(di + 1) * 128, :], slot="u1")
    P.wait("tensor", "u1", c_u1)
    for rc in range(2):
        for iw in range(NW):
            for di in range(NFC):
                mm(u1big[:, di * 256 + rc * 128: di * 256 + rc * 128 + 128],
                   XNT[:, di * RPC + iw * 512: di * RPC + (iw + 1) * 512],
                   PB[rc * 4 + iw][:, :], di == 0, di == NFC - 1,
                   inc="pe" if di == NFC - 1 else None)
    pec = P.ct("pe")
    P.wait("vector", "pe", pec)
    for rc in range(2):
        for iw in range(NW):
            copy_ev(PF[rc][:, iw * 512:(iw + 1) * 512], PB[rc * 4 + iw][:, :])
    c_pf = P.ct("dve")
    P.wait("tensor", "dve", c_pf)
    for ic in range(NIC):
        tt(XT[:, ic * 1024:(ic + 1) * 1024], XT[:, ic * 1024:(ic + 1) * 1024],
           B2B[:, :], ALU.add)

    QF = [pool(6144, RPC), pool(8192, RPC)]
    htile = pool(10240, 512)
    v1s = [pool(10752, 128), pool(10880, 128), pool(11008, 128), pool(11136, 128)]
    u2s = [pool(11264, 256), pool(11520, 256)]
    wq_marks = []
    for iw in range(NW):
        for fc in range(32):
            it = iw * 32 + fc
            sv = [v1s[(2 * it) % 4], v1s[(2 * it + 1) % 4]]
            su = u2s[it % 2]
            if it >= 2:
                P.wait("sync", "pe", wq_marks[it - 2])
            sl4 = (2 * it) % 4
            c_a = P.dma("sync", sv[0], V1[0:128, fc * 128:(fc + 1) * 128], slot=f"hv{sl4}")
            c_b = P.dma("sync", sv[1], V1[128:256, fc * 128:(fc + 1) * 128], slot=f"hv{(sl4 + 1) % 4}")
            c_w = P.dma("sync", su, U2[fc * 128:(fc + 1) * 128, :], slot=f"hu{it % 2}")
            P.wait("tensor", f"hv{sl4}", c_a)
            P.wait("tensor", f"hv{(sl4 + 1) % 4}", c_b)
            P.wait("tensor", f"hu{it % 2}", c_w)
            for rc in range(2):
                mm(sv[rc], PF[rc][:, iw * 512:(iw + 1) * 512], PB[0][:, :],
                   rc == 0, rc == 1, inc="pe" if rc == 1 else None)
            c_h = P.ct("pe")
            P.wait("scalar", "pe", c_h)
            c_g = act(htile, PB[0][:, :], AF.Gelu_apprx_tanh, bias=B1[:, fc:fc + 1])
            P.wait("tensor", "act", c_g)
            for rc in range(2):
                mm(su[:, rc * 128:(rc + 1) * 128], htile, PB[1 + rc][:, :],
                   fc == 0, fc == 31, inc="pe" if rc == 1 else None)
            wq_marks.append(P.ct("pe"))
        P.wait_now("vector", "pe")
        for rc in range(2):
            copy_ev(QF[rc][:, iw * 512:(iw + 1) * 512], PB[1 + rc][:, :])
        P.wait("tensor", "dve", P.ct("dve"))
    v2a = pool(0, 1024)
    v2b = pool(1024, 1024)
    c1 = P.dma("sync", v2a, V2[0:128, :], slot="v2l")
    c2 = P.dma("sync", v2b, V2[128:256, :], slot="v2l")
    P.wait("tensor", "v2l", c2)
    for ic in range(NIC):
        for hf in range(2):
            mm(QF[0][:, ic * 128:(ic + 1) * 128], v2a[:, hf * 512:(hf + 1) * 512],
               PB[4 + hf][:, :], True, False)
            mm(QF[1][:, ic * 128:(ic + 1) * 128], v2b[:, hf * 512:(hf + 1) * 512],
               PB[4 + hf][:, :], False, True, inc="pe" if hf == 1 else None)
        P.wait_now("vector", "pe")
        for hf in range(2):
            tt(XT[:, ic * 1024 + hf * 512: ic * 1024 + (hf + 1) * 512],
               XT[:, ic * 1024 + hf * 512: ic * 1024 + (hf + 1) * 512],
               PB[4 + hf][:, :], ALU.add)
        P.wait("tensor", "dve", P.ct("dve"))
    P.barrier()

    if upto == "p3":
        for ic in range(NIC):
            P.dma("sync", out[ic * 128:(ic + 1) * 128, :],
                  XT[:, ic * 1024:(ic + 1) * 1024])
        P.wait_now("sync", "qs")
        return P

    PAIRS = [[0, 1], [2, 3], [4, 5], [6, 7]]

    def collective(kind, op, ins_ap, outs_ap, groups=None):
        gr = groups if groups is not None else [list(range(8))]
        def g(e, k=kind, o=op, i=ins_ap, u=outs_ap, s=P.sems["cc"], gr=gr):
            e.collective_compute(k, o, replica_groups=gr,
                                 ins=[i], outs=[u]).then_inc(s, 1)
        P._emit("gpsimd", g)
        P.cnt["cc"] = P.cnt.get("cc", 0) + 1
        return P.cnt["cc"]

    c_kag = collective("AllGather", ALU.bypass, d_hkown.ap(), d_hkall.ap(),
                       groups=PAIRS)
    c_vag = collective("AllGather", ALU.bypass, d_hvown.ap(), d_hvall.ap(),
                       groups=PAIRS)
    P.wait("gpsimd", "cc", c_vag)
    P.barrier()

    if upto == "p4":
        for ic in range(NIC):
            P.dma("sync", out[ic * 128:(ic + 1) * 128, :],
                  XT[:, ic * 1024:(ic + 1) * 1024])
        P.wait_now("sync", "qs")
        return P

    NWIN = 8

    def l0_windows(parity):
        r = range(4 * parity, 4 * parity + 4)
        if p5_trunc:
            r = list(r)[:1]
        return r

    def l0_tiles(parity):
        outl = []
        lin = 0
        for giw in l0_windows(parity):
            for jt in range(4 * giw + 4):
                outl.append((giw, jt, lin))
                lin += 1
        return outl

    def p5_stats(parity):
        def self_cnt_rw(lin):
            return P.ct(f"rw{lin % 3}")
        CMASK = pool(8736, 2560)
        ccm = P.dma("sync", CMASK, cmask_i, slot="cml")
        P.wait("vector", "cml", ccm)
        qwin = pool(0, 4096)
        kslot = [pool(4096, 1024), pool(5120, 1024)]
        raws = [pool(6144, 512), pool(6656, 512), pool(7168, 512)]
        wmax = pool(7680, 256)
        P.op("gpsimd", lambda e, w=wmax: e.memset(w, -1e30), inc="gps")
        P.op("gpsimd", lambda e, w=M0[:, :]: e.memset(w, -1e30), inc="gps")
        P.op("gpsimd", lambda e, w=S0[:, :]: e.memset(w, 0.0), inc="gps")
        P.wait_now("vector", "gps")
        P.wait_now("sync", "gps")
        tiles = l0_tiles(parity)
        k_marks = {}
        raw_dma = {}
        raw_act = {}
        raw_rmax = {}
        for giw in l0_windows(parity):
            iwl = giw - 4 * parity
            cq = P.dma("sync", qwin.rearrange("p (a m) -> p a m", a=NFC),
                       d_hq.ap()[:, iwl * 512:(iwl + 1) * 512]
                       .rearrange("(a p) m -> p a m", p=128), slot="qw")
            P.wait("tensor", "qw", cq)
            wslot = iwl
            for gjt in range(4 * giw + 4):
                giw_, jt, lin = [t for t in tiles if t[0] == giw and t[1] == gjt][0]
                ks = kslot[lin % 2]
                if lin >= 2:
                    P.wait("sync", "pe", k_marks[lin - 2])
                half, jl = jt // 16, jt % 16
                ck = P.dma("sync", ks.rearrange("p (a m) -> p a m", a=NFC),
                           d_hkall.ap()[half * 1024:(half + 1) * 1024,
                                        jl * 128:(jl + 1) * 128]
                           .rearrange("(a p) m -> p a m", p=128), slot=f"kl{lin % 2}")
                P.wait("tensor", f"kl{lin % 2}", ck)
                bank = PB[lin % 2]
                if lin >= 2:
                    P.wait("tensor", "dve", raw_act[lin - 2])
                for fc in range(NFC):
                    mm(ks[:, fc * 128:(fc + 1) * 128], qwin[:, fc * 512:(fc + 1) * 512],
                       bank[:, :], fc == 0, fc == NFC - 1,
                       inc="pe" if fc == NFC - 1 else None)
                k_marks[lin] = P.ct("pe")
                if p5_trunc == 7:
                    raw_act[lin] = P.ct("act")
                    raw_dma[lin] = P.ct("qs")
                    raw_rmax[lin] = P.ct("dve")
                    continue
                rt = raws[lin % 3]
                if lin >= 3:
                    P.wait("vector", f"rw{lin % 3}", raw_dma[lin - 3])
                P.wait("vector", "pe", k_marks[lin])
                mo = jt - 4 * giw + 4 if jt < 4 * giw else jt - 4 * giw
                mo = (jt - 4 * giw) if jt >= 4 * giw else 4
                ca = stt(rt, bank[:, :], 1.0, CMASK[:, mo * 512:(mo + 1) * 512],
                         ALU.mult, ALU.add)
                raw_act[lin] = ca
                P.wait("sync", "dve", ca)
                raw_dma[lin] = P.dma("sync", d_hraw.ap()[lin * 128:(lin + 1) * 128, :], rt,
                                     slot=f"rw{lin % 3}")
                raw_rmax[lin] = rmax(wmax[:, jt * 8 + wslot: jt * 8 + wslot + 1], rt)
        P.wait_now("sync", "dve")
        for jt in range(NJT):
            if parity == 0 and jt >= 16:
                continue
            rmax(M0[:, jt:jt + 1], wmax[:, jt * 8:(jt + 1) * 8])
        c_all = P.ct("dve")
        if p5_trunc == 3:
            return
        wsum = pool(7936, 256)
        burn = pool(8192, 512)
        negM = pool(8704, 32)
        P.op("gpsimd", lambda e, w=wsum: e.memset(w, 0.0), inc="gps")
        P.wait_now("vector", "gps")
        c_nm = ts(negM, M0[:, :], -1.0, None, ALU.mult)
        P.wait("scalar", "dve", c_nm)
        P.wait_now("scalar", "gps")
        rd_marks = {}
        for sl in range(3):
            if P.ct(f"rw{sl}"):
                P.wait("sync", f"rw{sl}", P.ct(f"rw{sl}"))
        P.wait_now("sync", "dve")
        for giw in l0_windows(parity):
            iwl = giw - 4 * parity
            wslot = iwl
            for gjt in range(4 * giw + 4):
                giw_, jt, lin = [t for t in tiles if t[0] == giw and t[1] == gjt][0]
                rt = raws[lin % 3]
                if lin >= 3:
                    P.wait("sync", "act", rd_marks[lin - 3])
                crd = P.dma("sync", rt, d_hraw.ap()[lin * 128:(lin + 1) * 128, :],
                            slot=f"rw{lin % 3}")
                P.wait("scalar", f"rw{lin % 3}", crd)
                rd_marks[lin] = act(burn, rt, AF.Exp, bias=negM[:, jt:jt + 1],
                                    accum=wsum[:, jt * 8 + wslot: jt * 8 + wslot + 1])
        c_exp = P.ct("act")
        P.wait("vector", "act", c_exp)
        if p5_trunc == 2:
            return
        for jt in range(NJT):
            if parity == 0 and jt >= 16:
                continue
            rsum(S0[:, jt:jt + 1], wsum[:, jt * 8:(jt + 1) * 8])

    P.parity(lambda: p5_stats(0), lambda: p5_stats(1))
    P.barrier()

    if upto == "p5":
        for ic in range(NIC):
            P.dma("sync", out[ic * 128:(ic + 1) * 128, :],
                  XT[:, ic * 1024:(ic + 1) * 1024])
        P.wait_now("sync", "qs")
        return P

    P.dma("gpsimd", d_e2in.ap()[:, 0:32], M0[:, :])
    cqg = P.dma("gpsimd", d_e2in.ap()[:, 32:64], S0[:, :])
    P.wait("gpsimd", "qg", cqg)

    def g_e2(eng):
        eng.collective_compute("AllGather", ALU.bypass,
                               replica_groups=[[0, 1], [2, 3], [4, 5], [6, 7]],
                               ins=[d_e2in.ap()], outs=[d_e2out.ap()]
                               ).then_inc(P.sems["cc"], 1)
    P._emit("gpsimd", g_e2)
    P.cnt["cc"] = P.cnt.get("cc", 0) + 1
    P.wait("gpsimd", "cc", P.cnt["cc"])
    Ma = pool(8736, 32); Sa = pool(8768, 32)
    Mb = pool(8800, 32); Sb = pool(8832, 32)
    tca = pool(8864, 32); tcb = pool(8896, 32)
    P.dma("gpsimd", Ma, d_e2out.ap()[0:128, 0:32])
    P.dma("gpsimd", Sa, d_e2out.ap()[0:128, 32:64])
    P.dma("gpsimd", Mb, d_e2out.ap()[128:256, 0:32])
    ce2 = P.dma("gpsimd", Sb, d_e2out.ap()[128:256, 32:64])
    P.wait("vector", "qg", ce2)
    tt(M0[:, :], Ma, Mb, ALU.max)
    c1 = ts(M0[:, :], M0[:, :], -1.0, None, ALU.mult)
    tt(tca, Ma, M0[:, :], ALU.add)
    c2 = tt(tcb, Mb, M0[:, :], ALU.add)
    P.wait("scalar", "dve", c2)
    ca1 = act(tca, tca, AF.Exp)
    ca2 = act(tcb, tcb, AF.Exp)
    P.wait("vector", "act", ca2)
    tt(tca, tca, Sa, ALU.mult)
    tt(tcb, tcb, Sb, ALU.mult)
    tt(S0[:, :], tca, tcb, ALU.add)
    c3 = ts(S0[:, :], S0[:, :], 32.0, None, ALU.mult)
    P.wait("scalar", "dve", c3)
    c4 = act(EB0[:, :], S0[:, :], AF.Ln)
    P.wait("vector", "act", c4)
    c5 = ts(EB0[:, :], EB0[:, :], -1.0, None, ALU.mult)
    ceb = tt(EB0[:, :], EB0[:, :], M0[:, :], ALU.add)
    P.wait("scalar", "dve", ceb)
    P.wait("tensor", "dve", ceb)
    P.barrier()

    if upto == "p6":
        for ic in range(NIC):
            P.dma("sync", out[ic * 128:(ic + 1) * 128, :],
                  XT[:, ic * 1024:(ic + 1) * 1024])
        P.wait_now("sync", "qs")
        return P

    def p7_rows(parity):
        vslot = [pool(4096, 1024), pool(5120, 1024)]
        praw = [pool(6144, 512), pool(6656, 512)]
        ptile = [pool(7168, 512), pool(7680, 512)]
        readF = pool(0, 4096)
        tiles = l0_tiles(parity)
        pv_marks = {}
        exp_marks = {}
        for giw in l0_windows(parity):
            iwl = giw - 4 * parity
            J = 4 * giw + 4
            for gjt in range(J):
                giw_, jt, lin = [t for t in tiles if t[0] == giw and t[1] == gjt][0]
                sl = lin % 2
                if lin >= 2:
                    P.wait("sync", "pe", pv_marks[lin - 2])
                crd = P.dma("sync", praw[sl], d_hraw.ap()[lin * 128:(lin + 1) * 128, :],
                            slot=f"p7r{sl}")
                cv = P.dma("sync", vslot[sl].rearrange("p (a m) -> p a m", a=NFC),
                           d_hvall.ap()[jt * 128:(jt + 1) * 128, :]
                           .rearrange("p (a m) -> p a m", a=NFC), slot=f"p7v{sl}")
                P.wait("scalar", f"p7r{sl}", crd)
                P.wait("tensor", f"p7v{sl}", cv)
                if lin >= 2:
                    P.wait("scalar", "pe", pv_marks[lin - 2])
                ce = act(ptile[sl], praw[sl], AF.Exp, bias=EB0[:, jt:jt + 1])
                exp_marks[lin] = ce
                P.wait("tensor", "act", ce)
                for dc in range(NFC):
                    mm(vslot[sl][:, dc * 128:(dc + 1) * 128], ptile[sl],
                       PB[dc][:, :], gjt == 0, gjt == J - 1,
                       inc="pe" if dc == NFC - 1 else None)
                pv_marks[lin] = P.ct("pe")
            P.wait_now("vector", "pe")
            for dc in range(NFC):
                stt(readF[:, dc * 512:(dc + 1) * 512], PB[dc][:, :], 1.0,
                    XNT[:, dc * RPC + iwl * 512: dc * RPC + (iwl + 1) * 512],
                    ALU.mult, ALU.add)
            cev = P.ct("dve")
            P.wait("tensor", "dve", cev)
            tmarks = []
            for dc in range(NFC):
                for sub in range(4):
                    gi = dc * 4 + sub
                    bank = PB[gi % 2]
                    if gi >= 2:
                        P.wait("tensor", "dve", tmarks[gi - 2])
                    c_t = transp(readF[:, dc * 512 + sub * 128: dc * 512 + (sub + 1) * 128],
                                 bank[:, 0:128])
                    P.wait("vector", "pe", c_t)
                    ic = iwl * 4 + sub
                    cd = tt(XT[:, ic * 1024 + dc * 128: ic * 1024 + (dc + 1) * 128],
                            XT[:, ic * 1024 + dc * 128: ic * 1024 + (dc + 1) * 128],
                            bank[:, 0:128], ALU.add)
                    tmarks.append(cd)
            P.wait("tensor", "dve", P.ct("dve"))

    P.parity(lambda: p7_rows(0), lambda: p7_rows(1))
    P.barrier()

    M2C = P.sb("M2C", 128, NIC)
    R2C = P.sb("R2C", 128, NIC)
    NR2M2 = P.sb("NR2M2", 128, NIC)
    XSB = P.sb("XSB", 128, 16)

    def ln1_into_xnt():
        mcol, s2col = pool(0, NIC), pool(16, NIC)
        msqc, varc, rstdc = pool(32, NIC), pool(48, NIC), pool(64, NIC)
        sq, xn_t = pool(80, 1024), pool(1104, 1024)
        amax = pool(2128, NIC)
        prev_r2 = [0]
        for ic in range(NIC):
            xc = XT[:, ic * 1024:(ic + 1) * 1024]
            rsum(mcol[:, ic:ic + 1], xc)
            P.wait("scalar", "dve", prev_r2[0])
            c_sq = act(sq, xc, AF.Square)
            P.wait("vector", "act", c_sq)
            prev_r2[0] = rsum(s2col[:, ic:ic + 1], sq)
        c_m = ts(mcol, mcol, 1.0 / 1024.0, None, ALU.mult)
        P.wait("scalar", "dve", c_m)
        c_vv = act(msqc, mcol, AF.Square)
        P.wait("vector", "act", c_vv)
        ts(s2col, s2col, 1.0 / 1024.0, None, ALU.mult)
        ts(msqc, msqc, -1.0, None, ALU.mult)
        tt(varc, s2col, msqc, ALU.add)
        c_v = ts(varc, varc, EPS_LN, None, ALU.add)
        P.wait("scalar", "dve", c_v)
        c_sa = act(rstdc, varc, AF.Sqrt)
        P.wait("vector", "act", c_sa)
        recip(rstdc, rstdc)
        tpose_dve = []
        prev2 = [0]
        for ic in range(NIC):
            xc = XT[:, ic * 1024:(ic + 1) * 1024]
            c_xn = ts(xn_t, xc, mcol[:, ic:ic + 1], rstdc[:, ic:ic + 1],
                      ALU.subtract, ALU.mult)
            rsum(M2C[:, ic:ic + 1], xn_t)
            rmax(amax[:, ic:ic + 1], xn_t, absval=True)
            P.wait("scalar", "dve", c_xn)
            c_s2 = act(sq, xn_t, AF.Square)
            P.wait("vector", "act", c_s2)
            prev2[0] = rsum(R2C[:, ic:ic + 1], sq)
            P.wait("tensor", "dve", prev2[0])
            for fc in range(NFC):
                bank = PB[6 + (fc % 2)]
                gi = ic * NFC + fc
                if gi >= 2:
                    P.wait("tensor", "dve", tpose_dve[gi - 2])
                c_t = transp(xn_t[:, fc * 128:(fc + 1) * 128], bank[:, 0:128])
                P.wait("vector", "pe", c_t)
                cd = copy_ev(XNT[:, fc * RPC + ic * 128: fc * RPC + (ic + 1) * 128],
                             bank[:, 0:128])
                tpose_dve.append(cd)
        c1 = ts(M2C[:, :], M2C[:, :], 1.0 / 1024.0, None, ALU.mult)
        P.wait("scalar", "dve", c1)
        c2 = act(msqc, M2C[:, :], AF.Square)
        P.wait("vector", "act", c2)
        ts(R2C[:, :], R2C[:, :], 1.0 / 1024.0, None, ALU.mult)
        ts(msqc, msqc, -1.0, None, ALU.mult)
        tt(varc, R2C[:, :], msqc, ALU.add)
        c3 = ts(varc, varc, EPS_LN, None, ALU.add)
        P.wait("scalar", "dve", c3)
        c4 = act(R2C[:, :], varc, AF.Sqrt)
        P.wait("vector", "act", c4)
        recip(R2C[:, :], R2C[:, :])
        tt(NR2M2[:, :], R2C[:, :], M2C[:, :], ALU.mult)
        ts(NR2M2[:, :], NR2M2[:, :], -1.0, None, ALU.mult)
        c5 = rmax(XSB[:, 0:1], amax)
        P.wait("tensor", "dve", P.ct("dve"))
        c_tm = transp(XSB[:, 0:1], PB[6][0:1, 0:128])
        P.wait("scalar", "pe", c_tm)
        c_ev = act(POOL[0:1, 3000:3128], PB[6][0:1, 0:128], AF.Identity)
        P.wait("vector", "act", c_ev)
        c6 = rmax(XSB[0:1, 1:2], POOL[0:1, 3000:3128])
        P.wait("sync", "dve", c6)

    ln1_into_xnt()
    P.barrier()
    dx1 = P.dbg_out("xn1T", [NFC * 128, RPC])
    if dx1 is not None:
        for fc in range(NFC):
            P.dma("sync", dx1[fc * 128:(fc + 1) * 128, :],
                  XNT[:, fc * RPC:(fc + 1) * RPC])
        P.wait_now("sync", "qs")

    if upto == "p8":
        for ic in range(NIC):
            P.dma("sync", out[ic * 128:(ic + 1) * 128, :],
                  XT[:, ic * 1024:(ic + 1) * 1024])
        P.wait_now("sync", "qs")
        return P

    CTABp = pool(0, RPC)
    STABp = pool(2048, RPC)
    c_ct = P.dma("sync", CTABp, ctab_i, slot="rowld")
    c_ct = P.dma("sync", STABp, stab_i, slot="rowld")
    P.wait("vector", "rowld", c_ct)
    qraw = pool(4096, RPC)
    qswap = pool(6144, RPC)
    qrope = pool(8192, RPC)
    wring2 = [pool(10240, 128), pool(10368, 128)]
    tpsum = PB[6]

    def qk_proj(W_ap, dest_dram, swslot):
        for h in range(H):
            for di in range(NFC):
                sl = di % 2
                if di >= 2:
                    P.wait("sync", "pe", wt_marks[sl])
                cw = P.dma("sync", wring2[sl], W_ap[h * 128:(h + 1) * 128,
                                                    di * 128:(di + 1) * 128],
                           slot=f"w2{sl}")
                P.wait("tensor", f"w2{sl}", cw)
                P.wait("tensor", "dve", wt_last[0])
                c_t = transp(wring2[sl], tpsum[:, 0:128])
                P.wait("vector", "pe", c_t)
                wt_last[0] = copy_ev(wring2[sl], tpsum[:, 0:128])
                P.wait("tensor", "dve", wt_last[0])
                for iw in range(NW):
                    mm(wring2[sl], XNT[:, di * RPC + iw * 512: di * RPC + (iw + 1) * 512],
                       PB[iw][:, :], di == 0, di == NFC - 1,
                       inc="pe" if iw == NW - 1 else None)
                wt_marks[sl] = P.ct("pe")
            cmm = P.ct("pe")
            P.wait("scalar", "pe", cmm)
            for iw in range(NW):
                act(qraw[:, iw * 512:(iw + 1) * 512], PB[iw][:, :], AF.Identity)
            cev = P.ct("act")
            P.wait("gpsimd", "act", cev)
            cs1 = P.dma("gpsimd", qswap[0:64, :], qraw[64:128, :], slot=swslot)
            cs2 = P.dma("gpsimd", qswap[64:128, :], qraw[0:64, :], slot=swslot)
            P.wait("vector", swslot, cs2)
            P.wait("vector", "act", cev)
            tt(qrope, qraw, CTABp, ALU.mult)
            tt(qraw, qswap, STABp, ALU.mult)
            c_rope = tt(qrope, qrope, qraw, ALU.add)
            P.wait("scalar", "dve", c_rope)
            c_sq = act(qswap, qrope, AF.Square)
            P.wait("tensor", "act", c_sq)
            for iw in range(NW):
                mm(ONESC[:, :], qswap[:, iw * 512:(iw + 1) * 512],
                   PB[iw][0:1, :], True, True, inc="pe" if iw == NW - 1 else None)
            cnm = P.ct("pe")
            P.wait("scalar", "pe", cnm)
            for iw in range(NW):
                act(qraw[0:1, iw * 512:(iw + 1) * 512], PB[iw][0:1, :], AF.Sqrt)
            cno = P.ct("act")
            P.wait("vector", "act", cno)
            ts(qraw[0:1, :], qraw[0:1, :], 1e-6, None, ALU.add)
            c_rc = recip(qraw[0:1, :], qraw[0:1, :])
            P.wait("tensor", "dve", c_rc)
            for iw in range(NW):
                mm(ONES1[:, :], qraw[0:1, iw * 512:(iw + 1) * 512],
                   PB[iw][:, :], True, True, inc="pe" if iw == NW - 1 else None)
            cbb = P.ct("pe")
            P.wait("vector", "pe", cbb)
            for iw in range(NW):
                tt(qrope[:, iw * 512:(iw + 1) * 512], qrope[:, iw * 512:(iw + 1) * 512],
                   PB[iw][:, :], ALU.mult)
            chat = P.ct("dve")
            P.wait("sync", "dve", chat)
            cst = P.dma("sync", dest_dram[h * 128:(h + 1) * 128, :], qrope, slot="qkst")
            qk_st.append(cst)
            P.wait("tensor", "act", cev)
            P.wait("tensor", "dve", chat)
            P.wait("vector", "qkst", cst)

    wt_marks = {0: 0, 1: 0}
    wt_evac = {0: 0, 1: 0}
    wt_last = [0]
    wt_tp = {0: 0, 1: 0}
    qk_st = []
    qk_proj(WQ1, d_hq1.ap(), "sw0")
    qk_proj(WK1, d_hk1.ap(), "sw1")
    P.barrier()

    dq1 = P.dbg_out("q1", [H * 128, RPC])
    if dq1 is not None:
        P.dma("sync", dq1, d_hq1.ap())
        P.wait_now("sync", "qs")
    dk1 = P.dbg_out("k1", [H * 128, RPC])
    if dk1 is not None:
        P.dma("sync", dk1, d_hk1.ap())
        P.wait_now("sync", "qs")

    d_hv1own = P.dram("hv1own", [RPC, D])
    wvt = pool(10496, 1024)
    wvrow = prow(11520, 1024)
    wbrow = prow(12544, 1024)
    wbbc = pool(4096, 128)
    bpbc = pool(4224, 128)
    vtile = pool(4352, 128)
    t1c = pool(4480, 128)
    for do in range(NFC):
        for di in range(NFC):
            sl = di % 2
            P.wait("sync", "pe", wt_tp[sl])
            cw = P.dma("sync", wring2[sl], WV1[do * 128:(do + 1) * 128,
                                               di * 128:(di + 1) * 128],
                       slot=f"w2{sl}")
            P.wait("tensor", f"w2{sl}", cw)
            P.wait("tensor", "dve", wt_last[0])
            c_t = transp(wring2[sl], tpsum[:, 0:128])
            wt_tp[sl] = c_t
            P.wait("vector", "pe", c_t)
            cpl = copy_ev(wvt[:, di * 128:(di + 1) * 128], tpsum[:, 0:128])
            wt_last[0] = cpl
            P.wait("tensor", "dve", cpl)
            cbm = mm(LNB[:, di:di + 1], wvt[:, di * 128:(di + 1) * 128],
                     PB[7][0:1, 0:128], di == 0, di == NFC - 1, inc="pe")
            P.wait("vector", "pe", cbm)
            cs = ts(wvt[:, di * 128:(di + 1) * 128], wvt[:, di * 128:(di + 1) * 128],
                    LNW[:, di:di + 1], None, ALU.mult)
            P.wait("tensor", "dve", cs)
            cwb = mm(ONESC[:, :], wvt[:, di * 128:(di + 1) * 128],
                     PB[5][0:1, 0:128], di == 0, di == NFC - 1, inc="pe")
        P.wait("scalar", "pe", P.ct("pe"))
        act(wvrow[:, do * 128:(do + 1) * 128], PB[7][0:1, 0:128], AF.Identity)
        cwv = act(wbrow[:, do * 128:(do + 1) * 128], PB[5][0:1, 0:128], AF.Identity)
        P.wait("tensor", "act", cwv)
        mm(ONES1[:, :], wvrow[:, do * 128:(do + 1) * 128], PB[7][:, 0:128], True, True)
        c_b = mm(ONES1[:, :], wbrow[:, do * 128:(do + 1) * 128], PB[5][:, 0:128],
                 True, True, inc="pe")
        P.wait("vector", "pe", c_b)
        copy_ev(bpbc, PB[7][:, 0:128])
        c_bb = copy_ev(wbbc, PB[5][:, 0:128])
        P.wait("tensor", "dve", c_bb)
        for ic in range(NIC):
            bank = PB[ic % 4]
            for di in range(NFC):
                mm(XNT[:, di * RPC + ic * 128: di * RPC + (ic + 1) * 128],
                   wvt[:, di * 128:(di + 1) * 128], bank[:, 0:128],
                   di == 0, di == NFC - 1, inc="pe" if di == NFC - 1 else None)
            cpv = P.ct("pe")
            P.wait("vector", "pe", cpv)
            stt(t1c, wbbc, NR2M2[:, ic:ic + 1], bpbc, ALU.mult, ALU.add)
            cvt = stt(vtile, bank[:, 0:128], R2C[:, ic:ic + 1], t1c, ALU.mult, ALU.add)
            P.wait("sync", "dve", cvt)
            cvs = P.dma("sync", d_hv1own.ap()[ic * 128:(ic + 1) * 128,
                                              do * 128:(do + 1) * 128], vtile,
                        slot="v1st")
            P.wait("vector", "v1st", cvs)
            P.wait("tensor", "dve", cvt)
    P.barrier()

    dv1 = P.dbg_out("v1", [RPC, D])
    if dv1 is not None:
        P.dma("sync", dv1, d_hv1own.ap())
        P.wait_now("sync", "qs")

    if upto == "p9":
        for ic in range(NIC):
            P.dma("sync", out[ic * 128:(ic + 1) * 128, :],
                  XT[:, ic * 1024:(ic + 1) * 1024])
        P.wait_now("sync", "qs")
        return P

    bmask_i2 = einp("bmask", [128, 4 * 192])
    fmask_i2 = einp("fmask", [128, 5 * 512])
    QFO = P.sb("QFO", 128, H * 64)
    KFALL = P.sb("KFALL", 128, H * 128)
    VNALL = P.sb("VNALL", 128, H * 128)
    KNH = P.sb("KNH", 128, H)
    VNOWN_R = P.sb("VNOWN_R", 128, H * 64)
    d_bndk = P.dram("bndk", [H * 128, 64])
    d_bndx = P.dram("bndx", [NFC * 128, 64])
    d_hband = P.dram("hband", [H * NIC * 128, 192])
    d_hw8t = P.dram("hw8t", [D, D], BF16)

    qhead = pool(0, RPC)
    qtok = pool(2048, 128)
    WTR = pool(4416, NIC * 64)
    cwt = P.dma("sync", WTR, wtrows_i, slot="rowld")
    P.wait("tensor", "rowld", cwt)
    cftmp = pool(2176, 512)
    nrmsc = pool(2688, 128)
    KFOWN = pool(2816, H * 64)
    VFOWN = pool(3328, H * 64)
    sqv = pool(3840, 64)

    coarse_pe = [0]

    def coarse_for(src_dram, dest_sb, is_v):
        last_pe = coarse_pe
        for h in range(H):
            if not is_v:
                P.wait("sync", "pe", last_pe[0])
                cl = P.dma("sync", qhead, src_dram[h * 128:(h + 1) * 128, :], slot="chl")
                P.wait("tensor", "chl", cl)
            last_ev = [P.ct("dve")]
            for ic in range(NIC):
                if is_v:
                    P.wait("sync", "pe", last_pe[0])
                    cl = P.dma("sync", qtok, src_dram[ic * 128:(ic + 1) * 128,
                                                      h * 128:(h + 1) * 128], slot="chl")
                    P.wait("tensor", "chl", cl)
                    P.wait("tensor", "dve", last_ev[0])
                    srct = qtok
                else:
                    P.wait("tensor", "dve", last_ev[0])
                    c_t = transp(qhead[:, ic * 128:(ic + 1) * 128], PB[6][:, 0:128])
                    P.wait("vector", "pe", c_t)
                    last_ev[0] = copy_ev(qtok, PB[6][:, 0:128])
                    P.wait("tensor", "dve", last_ev[0])
                    srct = qtok
                cmmv = mm(srct, WTR[:, ic * 64:(ic + 1) * 64], PB[7][:, 0:64],
                          ic == 0, ic == NIC - 1, inc="pe")
                if is_v:
                    last_pe[0] = cmmv
                    last_ev[0] = P.ct("dve")
            cqf = P.ct("pe")
            last_pe[0] = cqf
            P.wait("vector", "pe", cqf)
            cev = copy_ev(dest_sb[:, h * 64:(h + 1) * 64], PB[7][:, 0:64])
            P.wait("tensor", "dve", cev)

    coarse_for(d_hq1.ap(), QFO, False)
    coarse_for(d_hk1.ap(), KFOWN, False)
    coarse_for(d_hv1own.ap(), VFOWN, True)
    VNOWN = pool(3904, H * 64)
    P.wait("scalar", "dve", P.ct("dve"))
    for h in range(H):
        c_s = act(sqv, VFOWN[:, h * 64:(h + 1) * 64], AF.Square)
        P.wait("tensor", "act", c_s)
        c_n = mm(ONESC[:, :], sqv[:, 0:64], PB[7][0:1, 0:64], True, True, inc="pe")
        P.wait("scalar", "pe", c_n)
        c_sa = act(nrmsc[0:1, 0:64], PB[7][0:1, 0:64], AF.Sqrt)
        P.wait("vector", "act", c_sa)
        ts(nrmsc[0:1, 0:64], nrmsc[0:1, 0:64], 1e-12, None, ALU.max)
        c_r = recip(nrmsc[0:1, 0:64], nrmsc[0:1, 0:64])
        P.wait("tensor", "dve", c_r)
        c_b = mm(ONES1[:, :], nrmsc[0:1, 0:64], PB[7][:, 0:64], True, True, inc="pe")
        P.wait("vector", "pe", c_b)
        c_v = tt(VNOWN[:, h * 64:(h + 1) * 64], VFOWN[:, h * 64:(h + 1) * 64],
                 PB[7][:, 0:64], ALU.mult)
        cvr = copy_ev(VNOWN_R[:, h * 64:(h + 1) * 64], VNOWN[:, h * 64:(h + 1) * 64])
        P.wait("tensor", "dve", cvr)
    P.wait("scalar", "dve", P.ct("dve"))
    c_s = act(nrmsc[:, 0:8], KNT[:, :], AF.Square)
    P.wait("tensor", "act", c_s)
    c_n = mm(ONESC[:, :], nrmsc[:, 0:8], PB[7][0:1, 0:8], True, True, inc="pe")
    P.wait("scalar", "pe", c_n)
    c_sa = act(nrmsc[0:1, 16:24], PB[7][0:1, 0:8], AF.Sqrt)
    P.wait("vector", "act", c_sa)
    ts(nrmsc[0:1, 16:24], nrmsc[0:1, 16:24], 1e-6, None, ALU.add)
    c_r = recip(nrmsc[0:1, 16:24], nrmsc[0:1, 16:24])
    P.wait("tensor", "dve", c_r)
    c_b = mm(ONES1[:, :], nrmsc[0:1, 16:24], PB[7][:, 0:8], True, True, inc="pe")
    P.wait("vector", "pe", c_b)
    c_kn = tt(KNH[:, :], KNT[:, :], PB[7][:, 0:8], ALU.mult)
    P.wait("sync", "dve", c_kn)
    P.wait("gpsimd", "dve", c_kn)

    P.wait("gpsimd", "gps", P.ct("gps"))
    P.wait("gpsimd", "dve", P.ct("dve"))
    P.dma("gpsimd", d_e3in.ap()[0:2048, :], d_hv1own.ap())
    P.dma("gpsimd", d_e3in.ap()[2048:2112, :].rearrange("a (b c) -> (a b) c", b=16),
          d_hk1.ap()[:, RPC - 64:RPC])
    for fc in range(NFC):
        P.dma("gpsimd",
              d_e3in.ap()[2112 + fc * 8:2112 + (fc + 1) * 8, :]
              .rearrange("a (b c) -> (a b) c", b=16),
              XNT[:, (fc + 1) * RPC - 64:(fc + 1) * RPC])
    P.dma("gpsimd", d_e3in.ap()[2176:2240, :].rearrange("a (b c) -> (a b) c", b=2),
          KFOWN)
    P.dma("gpsimd", d_e3in.ap()[2240:2304, :].rearrange("a (b c) -> (a b) c", b=2),
          VNOWN)
    mrow = POOL[0:1, 5504:6528]
    cms = P.op("gpsimd", lambda e, m=mrow: e.memset(m, 0.0))
    P.wait("vector", "gps", cms)
    cmc = copy_ev(mrow[:, 0:1], XSB[0:1, 1:2])
    P.wait("gpsimd", "dve", cmc)
    ce3s = P.dma("gpsimd", d_e3in.ap()[2304:2305, :], mrow)
    P.wait("gpsimd", "qg", ce3s)
    c_e3 = collective("AllGather", ALU.bypass, d_e3in.ap()[0:2304, :],
                      d_e3outp.ap(), groups=PAIRS)
    c_e3 = collective("AllGather", ALU.bypass, d_e3in.ap()[2304:2305, :],
                      d_e3max.ap())
    P.wait("gpsimd", "cc", c_e3)

    for half in (0, 1):
        P.dma("gpsimd", d_hv1all.ap()[half * 2048:(half + 1) * 2048, :],
              d_e3outp.ap()[half * 2304:half * 2304 + 2048, :])

    def g_e3asm(eng):
        par2 = eng.partition_id() % 2
        for par in eng.Switch(par2, 2):
            pr = 1 - par
            eng.dma_start(out=d_bndk.ap(),
                          in_=d_e3outp.ap()[pr * 2304 + 2048:pr * 2304 + 2112, :]
                          .rearrange("a (b c) -> (a b) c", b=16)
                          ).then_inc(P.sems["qg"], 16)
            eng.dma_start(out=d_bndx.ap(),
                          in_=d_e3outp.ap()[pr * 2304 + 2112:pr * 2304 + 2176, :]
                          .rearrange("a (b c) -> (a b) c", b=16)
                          ).then_inc(P.sems["qg"], 16)
            for src_r, dst_off in ((par, 64 * par), (pr, 64 * (1 - par))):
                eng.dma_start(
                    out=KFALL[:, :].rearrange("p (h c) -> p h c", h=H)[:, :, dst_off:dst_off + 64],
                    in_=d_e3outp.ap()[src_r * 2304 + 2176:src_r * 2304 + 2240, :]
                    .rearrange("a (b c) -> (a b) c", b=2)
                    .rearrange("p (h c) -> p h c", h=H)
                    ).then_inc(P.sems["qg"], 16)
                eng.dma_start(
                    out=VNALL[:, :].rearrange("p (h c) -> p h c", h=H)[:, :, dst_off:dst_off + 64],
                    in_=d_e3outp.ap()[src_r * 2304 + 2240:src_r * 2304 + 2304, :]
                    .rearrange("a (b c) -> (a b) c", b=2)
                    .rearrange("p (h c) -> p h c", h=H)
                    ).then_inc(P.sems["qg"], 16)
    P._emit("gpsimd", g_e3asm)
    P.cnt["qg"] = P.cnt.get("qg", 0) + 16 * 6
    P.wait("sync", "cc", P.ct("cc"))
    for r_ in range(8):
        cmx = P.dma("sync", XSB[0:1, r_:r_ + 1],
                    d_e3max.ap()[r_:r_ + 1, 0:1], slot="mxl")
    P.wait("vector", "mxl", cmx)
    rmax(XSB[0:1, 12:13], XSB[0:1, 0:8])
    ts(XSB[0:1, 12:13], XSB[0:1, 12:13], 1.0 / 127.0, None, ALU.mult)
    ts(XSB[0:1, 12:13], XSB[0:1, 12:13], 1e-5, None, ALU.max)
    copy_ev(XSB[0:1, 13:14], XSB[0:1, 12:13])
    cxs = recip(XSB[0:1, 13:14], XSB[0:1, 13:14])
    P.wait("tensor", "dve", cxs)
    mm(ONES1[:, :], XSB[0:1, 12:14], PB[7][:, 0:2], True, True, inc="pe")
    P.wait("vector", "pe", P.ct("pe"))
    cxb = copy_ev(XSB[:, 14:16], PB[7][:, 0:2])
    P.wait("tensor", "dve", cxb)
    P.barrier()

    if upto == "p10a":
        for ic in range(NIC):
            P.dma("sync", out[ic * 128:(ic + 1) * 128, :],
                  XT[:, ic * 1024:(ic + 1) * 1024])
        P.wait_now("sync", "qs")
        return P
    d_hraug = P.dram("hraug", [H * 130, RPC])
    d_hstatc = P.dram("hstatc", [H * 128, 32])
    d_hstatr = P.dram("hstatr", [H * 4, RPC])
    d_rowb = P.dram("rowb", [4, RPC])

    QH = pool(0, RPC)
    KPAD = pool(2048, 2112)
    XVPAD = pool(4160, 2112)
    SCR = pool(6272, 2048)
    RB = pool(8320, 2048)
    STL = pool(10368, 128)
    BMK = pool(10496, 768)
    CEV = pool(11264, 128)
    ROWB = 11392
    COLS = pool(13440, 64)
    BRS = pool(13504, 192)
    BRS2 = pool(13696, 192)
    UEV = pool(13888, 64)

    def rowA(base, w=RPC):
        return POOL[base:base + 1, ROWB:ROWB + w]

    sS1 = rowA(0)
    sSq = POOL[0:1, 6272:6272 + RPC]
    sNul = POOL[0:1, 2048:2048 + RPC]
    sMu = POOL[0:1, 4160:4160 + RPC]

    cbm = P.dma("sync", BMK, bmask_i2, slot="rowld")
    P.wait("vector", "rowld", cbm)
    P.wait("scalar", "rowld", cbm)

    for h in range(H):
        cq = P.dma("sync", QH, d_hq1.ap()[h * 128:(h + 1) * 128, :], slot="l1a")
        P.dma("sync", KPAD[:, 0:64], d_bndk.ap()[h * 128:(h + 1) * 128, :], slot="l1a")
        ckp = P.dma("sync", KPAD[:, 64:2112], d_hk1.ap()[h * 128:(h + 1) * 128, :],
                    slot="l1a")
        cbx = P.dma("sync", BRS[:, 0:64], d_bndx.ap()[h * 128:(h + 1) * 128, :], slot="l1b")
        xnh = XNT[:, h * RPC:(h + 1) * RPC]
        c_s = act(SCR, xnh, AF.Square)
        P.wait("tensor", "act", c_s)
        for w in range(NW):
            mm(ONESC[:, :], SCR[:, w * 512:(w + 1) * 512], PB[w][0:1, :], True, True,
               inc="pe" if w == NW - 1 else None)
        P.wait("scalar", "l1b", cbx)
        c_s2 = act(BRS2[:, 0:64], BRS[:, 0:64], AF.Square)
        P.wait("tensor", "act", c_s2)
        c_n2 = mm(ONESC[:, :], BRS2[:, 0:64], PB[5][0:1, 0:64], True, True, inc="pe")
        P.wait("scalar", "pe", c_n2)
        bndn = POOL[0:1, 13440:13504]
        for w in range(NW):
            act(rowA(0)[:, w * 512:(w + 1) * 512], PB[w][0:1, :], AF.Sqrt)
        c_q2 = act(bndn, PB[5][0:1, 0:64], AF.Sqrt)
        P.wait("vector", "act", c_q2)
        ts(rowA(0), rowA(0), 1e-12, None, ALU.max)
        ts(bndn, bndn, 1e-12, None, ALU.max)
        recip(rowA(0), rowA(0))
        c_r2 = recip(bndn, bndn)
        P.wait("tensor", "dve", c_r2)
        for w in range(NW):
            mm(ONES1[:, :], rowA(0)[:, w * 512:(w + 1) * 512], PB[w][:, :], True, True,
               inc="pe" if w == NW - 1 else None)
        c_bc2 = mm(ONES1[:, :], bndn, PB[5][:, 0:64], True, True, inc="pe")
        P.wait("vector", "pe", c_bc2)
        for w in range(NW):
            tt(XVPAD[:, 64 + w * 512:64 + (w + 1) * 512], xnh[:, w * 512:(w + 1) * 512],
               PB[w][:, :], ALU.mult)
        c_xv = tt(XVPAD[:, 0:64], BRS[:, 0:64], PB[5][:, 0:64], ALU.mult)
        P.wait("tensor", "dve", c_xv)
        ca_ = P.dma("sync", STL, A1[h, :, :], slot="l1c")
        P.wait("tensor", "l1c", ca_)
        c_t = transp(STL, PB[6][:, 0:128])
        P.wait("vector", "pe", c_t)
        tt(STL, STL, PB[6][:, 0:128], ALU.subtract)
        c_st = ts(STL, STL, 0.5, None, ALU.mult)
        P.wait("tensor", "dve", c_st)
        kfh = KFALL[:, h * 128:(h + 1) * 128]
        vnh = VNALL[:, h * 128:(h + 1) * 128]
        mm(QFO[:, h * 64:(h + 1) * 64], kfh, PB[7][0:64, 0:128], True, False)
        c_u = mm(STL, VNOWN_R[:, h * 64:(h + 1) * 64], PB[5][:, 0:64], True, True,
                 inc="pe")
        P.wait("scalar", "pe", c_u)
        c_ue = act(UEV, PB[5][:, 0:64], AF.Identity)
        P.wait("tensor", "act", c_ue)
        c_c2 = mm(UEV, vnh, PB[7][0:64, 0:128], False, True, inc="pe")
        P.wait("scalar", "pe", c_c2)
        c_ce = act(CEV[0:64, :], PB[7][0:64, 0:128], AF.Identity)
        P.wait("tensor", "act", c_ce)
        P.wait("sync", "pe", P.ct("pe"))
        P.wait("sync", "act", P.ct("act"))
        cwt = P.dma("sync", POOL[0:64, 6272:6272 + RPC], wtT_i, slot="msk")
        P.wait("tensor", "msk", cwt)
        for w in range(NW):
            mm(CEV[0:64, :], POOL[0:64, 6272 + w * 512:6272 + (w + 1) * 512],
               PB[w][:, :], True, True, inc="pe" if w == NW - 1 else None)
        c_rp = P.ct("pe")
        P.wait("vector", "pe", c_rp)
        for w in range(NW):
            copy_ev(RB[:, w * 512:(w + 1) * 512], PB[w][:, :])
        c_rb = P.ct("dve")
        P.wait("tensor", "dve", c_rb)
        for w in range(NW):
            mm(STL, XVPAD[:, 64 + w * 512:64 + (w + 1) * 512], PB[w][:, :], True, True,
               inc="pe" if w == NW - 1 else None)
        c_w1 = P.ct("pe")
        P.wait("vector", "pe", c_w1)
        for w in range(NW):
            copy_ev(SCR[:, w * 512:(w + 1) * 512], PB[w][:, :])
        c_w1e = P.ct("dve")
        P.wait("tensor", "dve", c_w1e)
        bm_dve = {}
        prev_bst = [0]
        P.wait("tensor", "l1a", ckp)
        for icl in range(NIC):
            bank = PB[icl % 2]
            if icl >= 2:
                P.wait("tensor", "dve", bm_dve[icl - 2])
            mm(QH[:, icl * 128:(icl + 1) * 128], KPAD[:, icl * 128:icl * 128 + 192],
               bank[:, 0:192], True, False)
            c_bm = mm(SCR[:, icl * 128:(icl + 1) * 128],
                      XVPAD[:, icl * 128:icl * 128 + 192],
                      bank[:, 0:192], False, True, inc="pe")
            P.wait("vector", "pe", c_bm)
            P.wait("vector", "bst", prev_bst[0])
            mofs = 384 if (icl == 0) else 0
            tt(BRS, bank[:, 0:192], BMK[:, mofs:mofs + 192], ALU.mult)
            rsum(COLS[:, 32 + icl:33 + icl], BRS)
            P.wait("scalar", "dve", P.ct("dve"))
            c_bq = act(BRS2, BRS, AF.Square)
            P.wait("vector", "act", c_bq)
            rsum(COLS[:, 48 + icl:49 + icl], BRS2)
            c_br = stt(BRS, bank[:, 0:192], 1.0, BMK[:, mofs + 192:mofs + 384],
                       ALU.mult, ALU.add)
            bm_dve[icl] = c_br
            P.wait("sync", "dve", c_br)
            prev_bst[0] = P.dma(
                "sync", d_hband.ap()[(h * NIC + icl) * 128:(h * NIC + icl + 1) * 128, :],
                BRS, slot="bst")
        nul_act = {}
        for w in range(NW):
            if w >= 2:
                P.wait("tensor", "act", nul_act[w - 2])
            mm(KNH[:, h:h + 1], QH[:, w * 512:(w + 1) * 512], PB[4 + (w % 2)][0:1, :],
               True, True, inc="pe")
            cnl = P.ct("pe")
            P.wait("scalar", "pe", cnl)
            nul_act[w] = act(sNul[:, w * 512:(w + 1) * 512],
                             PB[4 + (w % 2)][0:1, :], AF.Identity)
        c_nl = P.ct("act")
        P.wait("sync", "act", c_nl)
        P.dma("sync", d_hstatr.ap()[h * 4 + 0:h * 4 + 1, :], sNul, slot="bst")
        P.wait("sync", "pe", P.ct("pe"))
        cmu = P.dma("sync", SCR, muT_i, slot="msk")
        P.wait("vector", "msk", cmu)
        P.wait("vector", "act", c_nl)
        c_t1 = tt(SCR, SCR, RB, ALU.mult)
        P.wait("tensor", "dve", c_t1)
        for w in range(NW):
            mm(ONESC[:, :], SCR[:, w * 512:(w + 1) * 512], PB[w][0:1, :], True, True,
               inc="pe" if w == NW - 1 else None)
        c_s1m = P.ct("pe")
        P.wait("scalar", "pe", c_s1m)
        for w in range(NW):
            act(rowA(0)[:, w * 512:(w + 1) * 512], PB[w][0:1, :], AF.Identity)
        c_s1e = P.ct("act")
        P.wait("sync", "act", c_s1e)
        P.wait("sync", "pe", P.ct("pe"))
        cgm = P.dma("sync", SCR, gmT_i, slot="msk")
        P.wait("vector", "msk", cgm)
        tt(SCR, SCR, RB, ALU.mult)
        c_g2 = tt(SCR, SCR, RB, ALU.mult)
        P.wait("tensor", "dve", c_g2)
        P.wait("tensor", "act", c_s1e)
        for w in range(NW):
            mm(ONESC[:, :], SCR[:, w * 512:(w + 1) * 512], PB[w][0:1, :], True, False,
               inc="pe" if w == NW - 1 else None)
        c_gq = P.ct("pe")
        P.wait("sync", "pe", P.ct("pe"))
        cpx = P.dma("sync", SCR, pxT_i, slot="msk")
        csh = P.dma("sync", QH[0:127, :], RB[1:128, :], slot="shb")
        P.wait("vector", "msk", cpx)
        P.wait("vector", "shb", csh)
        tt(QH[0:127, :], QH[0:127, :], RB[0:127, :], ALU.mult)
        c_x2 = tt(QH[0:127, :], QH[0:127, :], SCR[0:127, :], ALU.mult)
        P.wait("tensor", "dve", c_x2)
        for w in range(NW):
            mm(ONESC[0:127, :], QH[0:127, w * 512:(w + 1) * 512], PB[w][0:1, :],
               False, True, inc="pe" if w == NW - 1 else None)
        c_sqm = P.ct("pe")
        P.wait("scalar", "pe", c_sqm)
        for w in range(NW):
            act(sSq[:, w * 512:(w + 1) * 512], PB[w][0:1, :], AF.Identity)
        c_sqe = P.ct("act")
        cb1 = P.ct("dve")
        P.wait("tensor", "dve", cb1)
        c_tc = transp(COLS[:, 32:48], PB[5][0:16, 0:128])
        c_tc2 = transp(COLS[:, 48:64], PB[6][0:16, 0:128], inc="pe")
        P.wait("vector", "pe", c_tc2)
        P.wait("vector", "bst", P.ct("bst"))
        copy_ev(BRS[0:16, 0:128], PB[5][0:16, 0:128])
        c_fe = copy_ev(BRS2[0:16, 0:128], PB[6][0:16, 0:128])
        P.wait("sync", "dve", c_fe)
        P.dma("sync", d_rowb.ap()[0:1, :].rearrange("a (b c) -> (a b) c", b=16),
              BRS[0:16, 0:128], slot="rb1")
        crb = P.dma("sync", d_rowb.ap()[1:2, :].rearrange("a (b c) -> (a b) c", b=16),
                    BRS2[0:16, 0:128], slot="rb1")
        P.wait("gpsimd", "rb1", crb)
        P.wait("gpsimd", "act", c_sqe)
        P.dma("gpsimd", sS1, d_rowb.ap()[0:1, :], slot="rba", accum_op=ALU.add)
        crl = P.dma("gpsimd", sSq, d_rowb.ap()[1:2, :], slot="rba",
                    accum_op=ALU.add)
        P.wait("vector", "rba", crl)
        tt(sMu, sNul, sNul, ALU.mult)
        tt(sSq, sSq, sMu, ALU.add)
        c_ad = tt(sS1, sS1, sNul, ALU.add)
        P.wait("sync", "dve", c_ad)
        cnv2 = P.dma("sync", sNul, ninv_i, slot="rb2")
        P.wait("vector", "rb2", cnv2)
        tt(sMu, sS1, sNul, ALU.mult)
        tt(sSq, sSq, sNul, ALU.mult)
        tt(sNul, sMu, sMu, ALU.mult)
        ts(sNul, sNul, -1.0, None, ALU.mult)
        tt(sSq, sSq, sNul, ALU.add)
        c_var = ts(sSq, sSq, 1e-6, None, ALU.max)
        P.wait("scalar", "dve", c_var)
        c_sg = act(sSq, sSq, AF.Sqrt)
        P.wait("vector", "act", c_sg)
        c_iv = recip(sS1, sSq)
        tt(sSq, sMu, sS1, ALU.mult)
        c_mt = ts(sSq, sSq, -1.0, None, ALU.mult)
        P.wait("sync", "dve", c_mt)
        P.dma("sync", d_rowb.ap()[2:3, :], sS1, slot="rb1")
        crr = P.dma("sync", d_rowb.ap()[3:4, :], sSq, slot="rb1")
        P.wait("sync", "rb1", crr)
        P.dma("sync", BRS[0:16, 0:128],
              d_rowb.ap()[2:3, :].rearrange("a (b c) -> (a b) c", b=16), slot="rb2")
        crl2 = P.dma("sync", BRS2[0:16, 0:128],
                     d_rowb.ap()[3:4, :].rearrange("a (b c) -> (a b) c", b=16), slot="rb2")
        P.wait("tensor", "rb2", crl2)
        c_ti = transp(BRS[0:16, 0:128], PB[5][:, 0:16], np_=16)
        c_tm = transp(BRS2[0:16, 0:128], PB[6][:, 0:16], np_=16)
        P.wait("vector", "pe", c_tm)
        copy_ev(COLS[:, 0:16], PB[5][:, 0:16])
        c_cc = copy_ev(COLS[:, 16:32], PB[6][:, 0:16])
        P.wait("sync", "dve", c_cc)
        P.dma("sync", d_hstatc.ap()[h * 128:(h + 1) * 128, :], COLS[:, 0:32], slot="bst")
        P.dma("sync", d_hstatr.ap()[h * 4 + 1:h * 4 + 2, :], sS1, slot="bst")
        cst = P.dma("sync", d_hstatr.ap()[h * 4 + 2:h * 4 + 3, :], sSq, slot="bst")
        for w in range(NW):
            c_ib = mm(ONES1[:, 0:64], rowA(0)[:, w * 512:(w + 1) * 512],
                      PB[4][0:64, :], True, True, inc="pe")
            P.wait("vector", "pe", c_ib)
            tt(KPAD[0:64, w * 512:(w + 1) * 512], RB[0:64, w * 512:(w + 1) * 512],
               PB[4][0:64, :], ALU.mult)
            c_rg = tt(XVPAD[0:64, w * 512:(w + 1) * 512], RB[64:128, w * 512:(w + 1) * 512],
                      PB[4][0:64, :], ALU.mult)
            P.wait("tensor", "dve", c_rg)
        copy_ev(KPAD[64:65, 0:RPC], sSq)
        c_rg2 = copy_ev(XVPAD[64:65, 0:RPC], sSq)
        P.wait("sync", "dve", c_rg2)
        P.dma("sync", d_hraug.ap()[h * 130:h * 130 + 65, :], KPAD[0:65, 0:RPC],
              slot="bst")
        crg3 = P.dma("sync", d_hraug.ap()[h * 130 + 65:h * 130 + 130, :],
                     XVPAD[0:65, 0:RPC], slot="bst")
        P.wait("sync", "bst", crg3)
        P.wait("vector", "bst", crg3)
        P.wait("scalar", "bst", crg3)
    P.barrier()

    dsr = P.dbg_out("statr", [H * 4, RPC])
    if dsr is not None:
        P.dma("sync", dsr, d_hstatr.ap())
        P.dma("sync", P.dbg_out("statc", [H * 128, 32]), d_hstatc.ap())
        P.dma("sync", P.dbg_out("raug", [H * 130, RPC]), d_hraug.ap())
        P.dma("sync", P.dbg_out("band", [H * NIC * 128, 192]), d_hband.ap())
        P.wait_now("sync", "qs")

    if upto == "p10":
        for ic in range(NIC):
            P.dma("sync", out[ic * 128:(ic + 1) * 128, :],
                  XT[:, ic * 1024:(ic + 1) * 1024])
        P.wait_now("sync", "qs")
        return P

    VWALL = pool(0, 4224)
    RAUGA = pool(4224, 2048)
    RAUGB = pool(6272, 2048)
    WFARC = pool(8320, 2048)
    FMK = pool(10368, 2560)
    FE = pool(12928, 512)
    BEXP = pool(13440, 192)
    YH = pool(13440, 128)
    PBT1 = pool(13632, 128)
    PBT2 = pool(13760, 128)
    VB1 = pool(13888, 132)
    VB2 = pool(14020, 132)
    YT = pool(14152, 128)
    ZN = pool(14280, 16)
    STC = pool(14296, 32)
    ZSC = pool(14328, 8)

    cfm = P.dma("sync", WFARC[0:65, :], wfar_i, slot="fml")
    cfm = P.dma("sync", FMK, fmask_i2, slot="fml")
    for jt in range(32):
        copy_ev(VWALL[:, jt * 132 + 128:jt * 132 + 129], ONESC[:, :])
    copy_ev(VB1[:, 128:129], ONESC[:, :])
    c_on = copy_ev(VB2[:, 128:129], ONESC[:, :])
    P.wait("tensor", "fml", cfm)
    P.wait("tensor", "dve", c_on)

    def p11_build(r):
        yev = {0: ("dve", 0), 1: ("dve", 0), 2: ("dve", 0), 3: ("dve", 0)}
        frd = {0: ("act", 0), 1: ("act", 0)}
        n_jt = 16 + 16 * r
        for h in range(H):
            P.wait("sync", "pe", P.ct("pe"))
            P.wait("sync", "dve", P.ct("dve"))
            cra = P.dma("sync", RAUGA[0:65, :], d_hraug.ap()[h * 130:h * 130 + 65, :],
                        slot="ra")
            if r == 1:
                cra = P.dma("sync", RAUGB[0:65, :],
                            d_hraug.ap()[h * 130 + 65:h * 130 + 130, :], slot="ra")
            cstc = P.dma("sync", STC, d_hstatc.ap()[h * 128:(h + 1) * 128, :], slot="ra")
            for jt in range(n_jt):
                cvw = P.dma("sync", VWALL[:, jt * 132:jt * 132 + 128],
                            d_hv1all.ap()[jt * 128:(jt + 1) * 128, h * 128:(h + 1) * 128],
                            slot="vw")
            cnr = P.dma("sync", PBT1[0:16, 0:128],
                        d_hstatr.ap()[h * 4:h * 4 + 1, :]
                        .rearrange("a (b c) -> (a b) c", b=16), slot="nr")
            P.wait("tensor", "nr", cnr)
            P.wait("tensor", "dve", P.ct("dve"))
            c_tn = transp(PBT1[0:16, 0:128], PB[2][:, 0:16], np_=16)
            P.wait("vector", "pe", c_tn)
            P.wait("vector", "ra", cstc)
            copy_ev(ZN, PB[2][:, 0:16])
            tt(ZN, ZN, STC[:, 0:16], ALU.mult)
            c_z1 = tt(ZN, ZN, STC[:, 16:32], ALU.add)
            P.wait("scalar", "dve", c_z1)
            c_zx = act(ZN, ZN, AF.Exp)
            P.wait("tensor", "ra", P.ct("ra"))
            P.wait("tensor", "vw", cvw)
            P.wait("vector", "act", c_zx)
            prev_cty = 0
            prev_yst = P.ct("yst")
            for itl in range(4):
                gs128 = 16 * r + itl * 4
                def emit_pvs(jt, c_e):
                    P.wait("tensor", "act", c_e)
                    for iq in range(4):
                        if jt == 0:
                            sem2, cnt2 = yev[iq]
                            P.wait("tensor", sem2, cnt2)
                        mm(FE[:, iq * 128:(iq + 1) * 128],
                           VWALL[:, jt * 132:jt * 132 + 129],
                           PB[3 + iq][:, 0:129], jt == 0, False,
                           inc="pe" if iq == 3 else None)

                pend = None
                for jt in range(gs128 + 4):
                    half = RAUGA if jt < 16 else RAUGB
                    js = (jt % 16) * 128
                    bk = jt % 2
                    sem, cnt = frd[bk]
                    P.wait("tensor", sem, cnt)
                    c_f = mm(WFARC[0:65, js:js + 128],
                             half[0:65, itl * 512:(itl + 1) * 512],
                             PB[bk][:, :], True, True, inc="pe")
                    if pend is not None:
                        emit_pvs(jt - 1, pend)
                    pe_now = P.ct("pe")
                    k = jt - gs128 + 1
                    if k >= 0:
                        P.wait("vector", "pe", pe_now)
                        c_mk = stt(FE, PB[bk][:, :], 1.0, FMK[:, k * 512:(k + 1) * 512],
                                   ALU.mult, ALU.add)
                        frd[bk] = ("dve", c_mk)
                        P.wait("scalar", "dve", c_mk)
                        pend = act(FE, FE, AF.Exp)
                    else:
                        P.wait("scalar", "pe", pe_now)
                        pend = act(FE, PB[bk][:, :], AF.Exp)
                        frd[bk] = ("act", pend)
                emit_pvs(gs128 + 3, pend)
                for iq in range(4):
                    ic = itl * 4 + iq
                    P.wait("sync", "pe", prev_cty)
                    cbb = P.dma("sync", BEXP,
                                d_hband.ap()[(h * NIC + ic) * 128:(h * NIC + ic + 1) * 128, :],
                                slot="bb")
                    gw0 = r * 2048 + ic * 128 - 64
                    if gw0 < 0:
                        P.dma("sync", VB1[0:64, 0:128],
                              d_hv1all.ap()[0:64, h * 128:(h + 1) * 128], slot="vb")
                        P.dma("sync", VB1[64:128, 0:128],
                              d_hv1all.ap()[0:64, h * 128:(h + 1) * 128], slot="vb")
                    else:
                        P.dma("sync", VB1[:, 0:128],
                              d_hv1all.ap()[gw0:gw0 + 128, h * 128:(h + 1) * 128],
                              slot="vb")
                    cvb = P.dma("sync", VB2[0:64, 0:128],
                                d_hv1all.ap()[gw0 + 128:gw0 + 192, h * 128:(h + 1) * 128],
                                slot="vb")
                    P.wait("vector", "bb", cbb)
                    c_sc = ts(BEXP, BEXP, STC[:, ic:ic + 1], STC[:, 16 + ic:17 + ic],
                              ALU.mult, ALU.add)
                    P.wait("scalar", "dve", c_sc)
                    c_be = act(BEXP, BEXP, AF.Exp)
                    P.wait("tensor", "act", c_be)
                    P.wait("tensor", "dve", P.ct("dve"))
                    c_t1 = transp(BEXP[:, 0:128], PB[2][:, 0:128])
                    c_t2 = transp(BEXP[:, 128:192], PB[7][0:64, 0:128], inc="pe")
                    P.wait("vector", "pe", c_t2)
                    copy_ev(PBT1, PB[2][:, 0:128])
                    c_pb = copy_ev(PBT2[0:64, :], PB[7][0:64, 0:128])
                    P.wait("tensor", "dve", c_pb)
                    P.wait("tensor", "vb", cvb)
                    mm(PBT1, VB1[:, 0:129], PB[3 + iq][:, 0:129], False, False)
                    c_last = mm(PBT2[0:64, :], VB2[0:64, 0:129], PB[3 + iq][:, 0:129],
                                False, True, inc="pe")
                    P.wait("vector", "pe", c_last)
                    tt(ZSC[:, 0:1], PB[3 + iq][:, 128:129], ZN[:, ic:ic + 1], ALU.add)
                    recip(ZSC[:, 1:2], ZSC[:, 0:1])
                    c_y = ts(YH, PB[3 + iq][:, 0:128], ZSC[:, 1:2], None, ALU.mult)
                    yev[iq] = ("dve", c_y)
                    P.wait("tensor", "dve", c_y)
                    c_ty = transp(YH, PB[2][:, 0:128], inc="pe")
                    prev_cty = c_ty
                    P.wait("vector", "pe", c_ty)
                    P.wait("vector", "yst", prev_yst)
                    c_yt = copy_ev(YT, PB[2][:, 0:128])
                    P.wait("sync", "dve", c_yt)
                    prev_yst = P.dma("sync",
                                     d_hy.ap()[h * 128:(h + 1) * 128,
                                               ic * 128:(ic + 1) * 128],
                                     YT, slot="yst")

    P.parity(lambda: p11_build(0), lambda: p11_build(1))
    P.barrier()

    dy = P.dbg_out("y1", [H * 128, RPC])
    if dy is not None:
        P.dma("sync", dy, d_hy.ap())
        P.wait_now("sync", "qs")

    if upto == "p11":
        for ic in range(NIC):
            P.dma("sync", out[ic * 128:(ic + 1) * 128, :],
                  XT[:, ic * 1024:(ic + 1) * 1024])
        P.wait_now("sync", "qs")
        return P

    W8S = P.sb("W8S", 128, 1024, I8)
    WTQ = P.sb("WTQ", 128, 128, BF16)
    XQG = [P.sb("XQG0", 128, 1024, BF16), P.sb("XQG1", 128, 1024, BF16)]
    WTile = [P.sb("WT0", 128, 512, BF16), P.sb("WT1", 128, 512, BF16)]

    WSB = pool(8192, 1024)
    FBB = pool(9216, 1024)
    LQ = pool(0, 2048)
    LR = pool(2048, 2048)
    LC = pool(4096, 2048)
    c_rw = P.dma("sync", POOL[0:1, 6144:7168], wscale, slot="wol")
    c_rw = P.dma("sync", POOL[0:1, 7168:8192], fbias, slot="wol")
    P.wait("tensor", "wol", c_rw)
    P.wait("vector", "wol", c_rw)
    bcast_row(POOL[0:1, 6144:7168], WSB, 1024, PB[6])
    c_ws = ts(WSB, WSB, XSB[:, 14:15], None, ALU.mult)
    bcast_row(POOL[0:1, 7168:8192], FBB, 1024, PB[6])
    for ic in range(NIC):
        c_fb = tt(XT[:, ic * 1024:(ic + 1) * 1024], XT[:, ic * 1024:(ic + 1) * 1024],
                  FBB, ALU.add)

    prev_w8 = {0: 0, 1: 0}
    prev_wtq = 0
    for nb in range(8):
        cw8 = P.dma("sync", W8S[:, :], W8[nb * 128:(nb + 1) * 128, :], slot="w8l")
        P.wait("vector", "w8l", cw8)
        for kb in range(8):
            bk = kb % 2
            c_cv = copy_ev(POOL[:, 10368:10496], W8S[:, kb * 128:(kb + 1) * 128])
            P.wait("tensor", "dve", c_cv)
            P.wait("tensor", "act", prev_w8[bk])
            c_tp = transp(POOL[:, 10368:10496], PB[bk][:, 0:128])
            P.wait("scalar", "pe", c_tp)
            P.wait("scalar", "w8s", prev_wtq)
            c_ev = act(WTQ[:, :], PB[bk][:, 0:128], AF.Identity)
            prev_w8[bk] = c_ev
            P.wait("sync", "act", c_ev)
            prev_wtq = P.dma("sync",
                             d_hw8t.ap()[kb * 128:(kb + 1) * 128,
                                         nb * 128:(nb + 1) * 128],
                             WTQ[:, :], slot="w8s")
            P.wait("vector", "pe", c_tp)
    P.barrier()

    RWOALL = pool(0, 8192)
    WOSTG = pool(10240, 128)
    prev_rw = 0
    for dc in range(8):
        for eb in range(8):
            bk = (dc * 8 + eb) % 2
            cwo = P.dma("sync", WOSTG,
                        WO1[eb * 128:(eb + 1) * 128, dc * 128:(dc + 1) * 128],
                        slot="wol")
            P.wait("tensor", "wol", cwo)
            P.wait("tensor", "dve", prev_rw)
            c_tp = transp(WOSTG, PB[bk][:, 0:128])
            P.wait("vector", "pe", c_tp)
            prev_rw = copy_ev(RWOALL[:, dc * 1024 + eb * 128:dc * 1024 + eb * 128 + 128],
                              PB[bk][:, 0:128])
            P.wait("sync", "dve", prev_rw)
    P.wait("tensor", "dve", prev_rw)
    LQ4 = pool(10368, 512)
    LR4 = pool(10880, 512)
    LC4 = pool(11392, 512)
    prev_q4 = [0, 0]
    qi_ctr = [0]

    def ladder_quarter():
        qi = qi_ctr[0]
        qi_ctr[0] += 1
        fc, q = qi // 4, qi % 4
        src = XNT[:, fc * RPC + q * 512:fc * RPC + (q + 1) * 512]
        ts(LQ4, src, XSB[:, 15:16], 128.5, ALU.mult, ALU.add)
        copy_ev(LR4, LQ4)
        for b in range(7, -1, -1):
            ts(LC4, LR4, float(2 ** b), float(-(2 ** b)), ALU.is_ge, ALU.mult)
            tt(LR4, LR4, LC4, ALU.add)
        tt(LQ4, LQ4, LR4, ALU.subtract)
        ts(LQ4, LQ4, -128.0, None, ALU.add)
        sl = qi % 2
        P.wait("vector", f"xqs{sl}", prev_q4[sl])
        c_cv = copy_ev(WTile[sl][:, :], LQ4)
        P.wait("sync", "dve", c_cv)
        prev_q4[sl] = P.dma(
            "sync", d_hxq.ap()[fc * 128:(fc + 1) * 128, q * 512:(q + 1) * 512],
            WTile[sl][:, :], slot=f"xqs{sl}")

    YLG = [pool(12288, 1024), pool(13312, 1024)]
    prev_yl = [0, 0]
    prev_ev = {0: 0, 1: 0}
    for icc in range(NIC):
        buf = YLG[icc % 2]
        P.wait("sync", "pe", prev_yl[icc % 2])
        for dc in range(8):
            cyl = P.dma("sync", buf[:, dc * 128:(dc + 1) * 128],
                        d_hy.ap()[dc * 128:(dc + 1) * 128,
                                  icc * 128:(icc + 1) * 128], slot=f"yl{icc % 2}")
        P.wait("tensor", f"yl{icc % 2}", P.ct(f"yl{icc % 2}"))
        for eh in range(2):
            bk = (icc * 2 + eh) % 2
            P.wait("tensor", "dve", prev_ev[bk])
            for dc in range(8):
                mm(buf[:, dc * 128:(dc + 1) * 128],
                   RWOALL[:, dc * 1024 + eh * 512:dc * 1024 + (eh + 1) * 512],
                   PB[bk][:, :], dc == 0, dc == 7, inc="pe" if dc == 7 else None)
            c_mm = P.ct("pe")
            P.wait("vector", "pe", c_mm)
            xsl = XT[:, icc * 1024 + eh * 512:icc * 1024 + (eh + 1) * 512]
            prev_ev[bk] = tt(xsl, xsl, PB[bk][:, :], ALU.add)
            prev_yl[icc % 2] = c_mm
            ladder_quarter()
    P.barrier()

    TMP12 = pool(0, 512)
    prev_fq = {i: 0 for i in range(8)}
    prev_ld = [0, 0]
    for icb in range(2):
        for nh in range(2):
            for kc in range(8):
                bk = kc % 2
                P.wait("sync", "pe", prev_ld[bk])
                cwt = P.dma("sync", WTile[bk][:, :],
                            d_hw8t.ap()[kc * 128:(kc + 1) * 128,
                                        nh * 512:(nh + 1) * 512], slot=f"fwl{bk}")
                cxq = P.dma("sync", XQG[bk][:, :],
                            d_hxq.ap()[kc * 128:(kc + 1) * 128,
                                       icb * 1024:(icb + 1) * 1024], slot=f"fwl{bk}")
                P.wait("tensor", f"fwl{bk}", cxq)
                if kc == 0:
                    for iq in range(8):
                        P.wait("tensor", "dve", prev_fq[iq])
                for iq in range(8):
                    mm(XQG[bk][:, iq * 128:(iq + 1) * 128], WTile[bk][:, :],
                       PB[iq][:, :], kc == 0, kc == 7,
                       inc="pe" if iq == 7 else None)
                prev_ld[bk] = P.ct("pe")
            c_mm = P.ct("pe")
            P.wait("vector", "pe", c_mm)
            for iq in range(8):
                icc = icb * 8 + iq
                tt(TMP12, PB[iq][:, :], WSB[:, nh * 512:(nh + 1) * 512], ALU.mult)
                xsl = XT[:, icc * 1024 + nh * 512:icc * 1024 + (nh + 1) * 512]
                prev_fq[iq] = tt(xsl, xsl, TMP12, ALU.add)
    P.barrier()

    for ic in range(NIC):
        P.dma("sync", out[ic * 128:(ic + 1) * 128, :],
              XT[:, ic * 1024:(ic + 1) * 1024])
    P.wait_now("sync", "qs")
    return P


def _prep_inputs(ins):
    def col8(v):
        return np.ascontiguousarray(np.asarray(v, np.float32).reshape(8, 128).T)

    base = {
        "Wq": ins["L0_attn_Wq"], "bq": col8(ins["L0_attn_bq"]),
        "Wk": ins["L0_attn_Wk"], "bk": col8(ins["L0_attn_bk"]),
        "Wv": ins["L0_attn_Wv"], "bv": np.asarray(ins["L0_attn_bv"], np.float32).reshape(1, D),
        "U1": ins["L0_ffn_U1"], "V1": ins["L0_ffn_V1"],
        "b1": np.ascontiguousarray(np.asarray(ins["L0_ffn_b1"], np.float32).reshape(32, 128).T),
        "U2": ins["L0_ffn_U2"], "V2": ins["L0_ffn_V2"],
        "b2": np.asarray(ins["L0_ffn_b2"], np.float32).reshape(1, D),
        "WQ1": ins["L1_attn_WQ"], "WK1": ins["L1_attn_WK"],
        "WV1": ins["L1_attn_WV"], "WO1": ins["L1_attn_WO"],
        "lnw": col8(ins["L1_attn_ln_w"]), "lnb": col8(ins["L1_attn_ln_b"]),
        "A1": ins["L1_attn_A"],
        "knT": np.ascontiguousarray(np.asarray(ins["L1_attn_k_null"], np.float32).reshape(H, DH).T),
        "W8": ins["L1_ffn_weight_int8"],
        "wscale": np.asarray(ins["L1_ffn_weight_scale"], np.float32).reshape(1, D),
        "fbias": np.asarray(ins["L1_ffn_bias"], np.float32).reshape(1, D),
    }
    return base


def kernel(**inputs):
    ins = {k: np.asarray(v) for k, v in inputs.items()}
    x = ins["L0_attn_minibatch"]
    base = _prep_inputs(ins)
    cst = {0: host_consts(0), 1: host_consts(1)}

    Pr = build_program()
    nc = Pr.finalize()

    in_maps = []
    for c in range(8):
        b, r = c // 2, c % 2
        m = dict(base)
        m["x"] = np.ascontiguousarray(x[b, r * RPC:(r + 1) * RPC, :])
        m.update(cst[r])
        in_maps.append(m)
    res = run_bass_kernel_spmd(nc, in_maps, core_ids=list(range(8)))
    o = np.empty((B, T, D), np.float32)
    for c in range(8):
        b, r = c // 2, c % 2
        o[b, r * RPC:(r + 1) * RPC, :] = res.results[c]["out"]
    return o


# version 05
# speedup vs baseline: 1.0129x; 1.0129x over previous
import math
from contextlib import ExitStack

import numpy as np
import concourse.bass as bass
import concourse.mybir as mybir
from concourse.bass_utils import run_bass_kernel_spmd

F32 = mybir.dt.float32
BF16 = mybir.dt.bfloat16
I8 = mybir.dt.int8
AF = mybir.ActivationFunctionType
ALU = mybir.AluOpType
AX = mybir.AxisListType

B, T, D = 4, 4096, 1024
RPC = T // 2
NIC = RPC // 128
NFC = D // 128
NW = RPC // 512
H, DH, BLK = 8, 128, 64
NJT = T // 128
RANK, DFF = 256, 4096
EPS_LN = 1e-5

ENGS = ("sync", "scalar", "vector", "tensor", "gpsimd")


def host_consts(parity):
    i_glob = np.arange(RPC, dtype=np.float64) + parity * RPC

    c = {}
    c["ident"] = np.eye(128, dtype=np.float32)
    c["identh"] = (0.5 * np.eye(128)).astype(np.float32)
    import ml_dtypes
    c["identb"] = np.eye(128).astype(ml_dtypes.bfloat16)
    c["onescol"] = np.ones((128, 1), np.float32)
    c["ones1"] = np.ones((1, 128), np.float32)

    half = DH // 2
    inv = 1.0 / 10000.0 ** (np.arange(half, dtype=np.float64) / half)
    ang = i_glob[None, :] * inv[:, None]
    cos, sin = np.cos(ang), np.sin(ang)
    c["ctab"] = np.concatenate([cos, cos], 0).astype(np.float32)
    c["stab"] = np.concatenate([-sin, sin], 0).astype(np.float32)

    t = np.linspace(-1.0, 1.0, BLK)
    W = np.stack([np.ones_like(t), t], 1)
    Wa = W / (np.linalg.norm(W, axis=0, keepdims=True) + 1e-6)

    wtr = np.zeros((128, NIC * 64), np.float64)
    for ic in range(NIC):
        for p in range(128):
            il = ic * 128 + p
            nl, a = il // 64, il % 64
            for kq in range(2):
                cl = 2 * nl + kq
                wtr[p, ic * 64 + cl] = Wa[a, kq]
    c["wtrows"] = wtr.astype(np.float32)

    wtT = np.zeros((64, RPC), np.float64)
    for il in range(RPC):
        nl, a = il // 64, il % 64
        wtT[2 * nl, il] = Wa[a, 0]
        wtT[2 * nl + 1, il] = Wa[a, 1]
    c["wtT"] = wtT.astype(np.float32)

    wf = np.zeros((65, 16 * 128), np.float64)
    for tt_ in range(16):
        for jj in range(128):
            blkk, a = jj // 64, jj % 64
            for kk in range(2):
                wf[4 * tt_ + 2 * blkk + kk, tt_ * 128 + jj] = Wa[a, kk]
        wf[64, tt_ * 128:tt_ * 128 + 128] = 1.0
    c["wfar"] = wf.astype(np.float32)

    u = Wa.sum(0)
    d2 = (Wa ** 2).sum(0)
    c01 = (Wa[:, 0] * Wa[:, 1]).sum()
    P1 = np.vstack([np.zeros(2), np.cumsum(Wa, 0)])
    PG = np.vstack([np.zeros(2), np.cumsum(Wa ** 2, 0)])
    PC = np.concatenate([[0.0], np.cumsum(Wa[:, 0] * Wa[:, 1])])
    Mu = np.zeros((128, RPC), np.float64)
    Gm = np.zeros((128, RPC), np.float64)
    Px = np.zeros((128, RPC), np.float64)
    ig = i_glob.astype(np.int64)
    n_i, a_i = ig // 64, ig % 64
    for cp in range(128):
        m, kk = cp // 2, cp % 2
        full = (m <= n_i - 2).astype(np.float64)
        bnd = (m == n_i - 1).astype(np.float64)
        Mu[cp] = u[kk] * full + P1[a_i, kk] * bnd
        Gm[cp] = d2[kk] * full + PG[a_i, kk] * bnd
        if kk == 0:
            Px[cp] = 2.0 * c01 * full + 2.0 * PC[a_i] * bnd
    c["muT"] = Mu.astype(np.float32)
    c["gmT"] = Gm.astype(np.float32)
    c["pxT"] = Px.astype(np.float32)
    c["ninv"] = (1.0 / (ig + 2.0))[None, :].astype(np.float32)
    cm = np.zeros((128, 5 * 512), np.float32)
    fi = np.arange(512)
    for o in range(4):
        keep = (fi[None, :] - 128 * o) >= np.arange(128)[:, None]
        cm[:, o * 512:(o + 1) * 512] = np.where(keep, 0.0, -1e30)
    c["cmask"] = cm
    bm = np.zeros((128, 4 * 192), np.float32)
    lj = np.arange(192)[None, :]
    p_ = np.arange(128)[:, None]
    inband = ((p_ - lj + 64 >= 0) & (p_ - lj + 64 <= 64))
    ok0 = inband & (lj >= 64 - 0) if False else inband
    bm[:, 0:192] = inband.astype(np.float32)
    bm[:, 192:384] = np.where(inband, 0.0, -1e30)
    in0 = inband & ((lj - 64 + 0) >= 0) if parity == 0 else inband
    in0 = inband & (lj >= 64) if parity == 0 else inband
    bm[:, 384:576] = in0.astype(np.float32)
    bm[:, 576:768] = np.where(in0, 0.0, -1e30)
    c["bmask"] = bm
    fm = np.zeros((128, 5 * 512), np.float32)
    fi = np.arange(512)[None, :]
    for k in range(5):
        ojt = -64 + 128 * k
        band = (p_ - fi + ojt) >= 0
        fm[:, k * 512:(k + 1) * 512] = np.where(band, -1e30, 0.0)
    c["fmask"] = fm
    return c


class Prog:
    def __init__(self, dbg_names=()):
        self.nc = bass.Bass("TRN2", target_bir_lowering=False)
        self.stack = ExitStack()
        self.tape = {e: [] for e in ENGS}
        self._sub = None
        self.cnt = {}
        self.sems = {}
        self.dbg_names = set(dbg_names)
        self.dbg = {}
        self.slot_eng = {}
        self.sfx = ""
        self.pcnt = {"E": {}, "O": {}}
        self.parity_cache = {}
        for s in ("qs", "qa", "qg", "pe", "act", "dve", "gps", "cc", "sy"):
            self.sems[s] = self.stack.enter_context(self.nc.semaphore(s))
            self.cnt[s] = 0

    def sb(self, name, p, w, dtype=F32):
        return self.stack.enter_context(self.nc.sbuf_tensor(name, [p, w], dtype))

    def psum(self, name):
        return self.stack.enter_context(self.nc.psum_tensor(name, [128, 512], F32))

    def dram(self, name, shape, dtype=F32, kind="Internal", addr_space="Local"):
        return self.nc.dram_tensor(name, list(shape), dtype, kind=kind,
                                   addr_space=addr_space)

    def dbg_out(self, name, shape):
        if name in self.dbg_names:
            self.dbg[name] = self.dram("dbg_" + name, shape, F32, kind="ExternalOutput").ap()
            return self.dbg[name]
        return None

    def key(self, sem):
        if self.sfx and sem not in ("sy", "cc"):
            return sem + self.sfx
        return sem

    def ct(self, sem):
        return self.cnt.get(self.key(sem), 0)

    def _sem(self, name):
        if name not in self.sems:
            self.sems[name] = self.stack.enter_context(self.nc.semaphore(name))
        return self.sems[name]

    def _emit(self, eng, fn):
        (self._sub if self._sub is not None else self.tape)[eng].append(fn)

    ENG_SEM = {"vector": "dve", "scalar": "act", "gpsimd": "gps"}

    def op(self, eng, fn, inc=None, n=1):
        es = self.ENG_SEM.get(eng)
        if es is not None:
            if inc is None:
                inc = es
            if inc == es:
                self.wait(eng, es, self.ct(es))
        if inc is None:
            self._emit(eng, fn)
            return None
        k = self.key(inc)
        def g(e, fn=fn, s=self._sem(k), n=n):
            fn(e).then_inc(s, n)
        self._emit(eng, g)
        self.cnt[k] = self.cnt.get(k, 0) + n
        return self.cnt[k]

    def dma(self, eng, out, in_, slot=None, **kw):
        sem = slot if slot is not None else {"sync": "qs", "scalar": "qa", "gpsimd": "qg"}[eng]
        sem = self.key(sem)
        self.slot_eng[sem] = eng
        def g(e, out=out, in_=in_, s=self._sem(sem), kw=kw):
            e.dma_start(out=out, in_=in_, **kw).then_inc(s, 16)
        self._emit(eng, g)
        self.cnt[sem] = self.cnt.get(sem, 0) + 16
        return self.cnt[sem]

    def wait(self, eng, sem, n):
        def g(e, s=self._sem(self.key(sem)), n=n):
            e.wait_ge(s, n)
        self._emit(eng, g)

    def wait_now(self, eng, sem):
        self.wait(eng, sem, self.ct(sem))

    def barrier(self, engines=ENGS):
        for sem, eng in [("qs", "sync"), ("qa", "scalar"), ("qg", "gpsimd")] + \
                list(self.slot_eng.items()):
            if eng in engines:
                self.wait(eng, sem, self.cnt.get(sem, 0))
        for e, es in (("vector", "dve"), ("scalar", "act"),
                      ("gpsimd", "gps"), ("tensor", "pe")):
            if e in engines:
                self.wait(e, es, self.cnt.get(es, 0))
        for e in engines:
            def g(eng, s=self.sems["sy"]):
                eng.nop().then_inc(s, 1)
            self._emit(e, g)
            self.cnt["sy"] += 1
        tgt = self.cnt["sy"]
        for e in engines:
            self.wait(e, "sy", tgt)

    def parity(self, fn_even, fn_odd):
        base_cnt = dict(self.cnt)
        base_slots = dict(self.slot_eng)
        tapes = []
        for sfx, fn in (("E", fn_even), ("O", fn_odd)):
            self.sfx = sfx
            self.cnt = dict(base_cnt)
            self.cnt.update(self.pcnt[sfx])
            self.slot_eng = dict(base_slots)
            self._sub = {e: [] for e in ENGS}
            fn()
            for e in ENGS:
                es = self.ENG_SEM.get(e)
                if es is not None:
                    self.wait(e, es, self.ct(es))
                for sem, eng2 in list(self.slot_eng.items()):
                    if eng2 == e and sem.endswith(sfx):
                        def gg(eng_, s=self._sem(sem), n=self.cnt.get(sem, 0)):
                            eng_.wait_ge(s, n)
                        self._emit(e, gg)
                def gsy(eng_, s=self.sems["sy"]):
                    eng_.nop().then_inc(s, 1)
                self._emit(e, gsy)
            self.pcnt[sfx] = {k: v for k, v in self.cnt.items() if k.endswith(sfx)}
            tapes.append(self._sub)
        self._sub = None
        self.sfx = ""
        self.cnt = dict(base_cnt)
        self.slot_eng = dict(base_slots)
        self.cnt["sy"] = base_cnt.get("sy", 0) + len(ENGS)
        tgt = self.cnt["sy"]
        for e in ENGS:
            t0, t1 = tapes[0][e], tapes[1][e]
            def mk(t0=t0, t1=t1, ename=e):
                def run(eng):
                    if ename not in self.parity_cache:
                        self.parity_cache[ename] = eng.partition_id() % 2
                    par = self.parity_cache[ename]
                    for body in eng.Switch(par, 2):
                        for f in (t0 if body == 0 else t1):
                            f(eng)
                return run
            self.tape[e].append(mk())
        for e in ENGS:
            self.wait(e, "sy", tgt)

    def finalize(self):
        nc = self.nc
        with nc.Block() as block:
            @block.sync
            def _(sync):
                for f in self.tape["sync"]:
                    f(sync)

            @block.scalar
            def _(scalar):
                for f in self.tape["scalar"]:
                    f(scalar)

            @block.vector
            def _(vector):
                for f in self.tape["vector"]:
                    f(vector)

            @block.tensor
            def _(tensor):
                for f in self.tape["tensor"]:
                    f(tensor)

            @block.gpsimd
            def _(gpsimd):
                for f in self.tape["gpsimd"]:
                    f(gpsimd)
        self.stack.close()
        return nc


def build_program(upto="all", dbg_names=(), p5_trunc=0):
    P = Prog(dbg_names)

    def einp(name, shape, dtype=F32):
        return P.dram(name, shape, dtype, kind="ExternalInput").ap()

    x_in = einp("x", [RPC, D])
    Wq = einp("Wq", [D, D]); bq = einp("bq", [128, 8])
    Wk = einp("Wk", [D, D]); bk = einp("bk", [128, 8])
    Wv = einp("Wv", [D, D]); bv = einp("bv", [1, D])
    U1 = einp("U1", [D, RANK]); V1 = einp("V1", [RANK, DFF]); b1 = einp("b1", [128, 32])
    U2 = einp("U2", [DFF, RANK]); V2 = einp("V2", [RANK, D]); b2 = einp("b2", [1, D])
    WQ1 = einp("WQ1", [D, D]); WK1 = einp("WK1", [D, D])
    WV1 = einp("WV1", [D, D]); WO1 = einp("WO1", [D, D])
    lnw = einp("lnw", [128, 8]); lnb = einp("lnb", [128, 8])
    A1 = einp("A1", [H, 128, 128])
    knT = einp("knT", [128, H])
    W8 = einp("W8", [D, D], I8)
    wscale = einp("wscale", [1, D])
    fbias = einp("fbias", [1, D])
    ident = einp("ident", [128, 128]); identh = einp("identh", [128, 128])
    identb = einp("identb", [128, 128], BF16)
    onescol = einp("onescol", [128, 1]); ones1 = einp("ones1", [1, 128])
    ctab_i = einp("ctab", [128, RPC]); stab_i = einp("stab", [128, RPC])
    wtrows_i = einp("wtrows", [128, NIC * 64])
    wtT_i = einp("wtT", [64, RPC])
    wfar_i = einp("wfar", [65, 16 * 128])
    muT_i = einp("muT", [128, RPC]); gmT_i = einp("gmT", [128, RPC]); pxT_i = einp("pxT", [128, RPC])
    ninv_i = einp("ninv", [1, RPC])
    cmask_i = einp("cmask", [128, 5 * 512])

    out = P.dram("out", [RPC, D], F32, kind="ExternalOutput").ap()

    d_hq = P.dram("hq", [NFC * 128, RPC])
    d_hkown = P.dram("hkown", [NFC * 128, RPC])
    d_hvown = P.dram("hvown", [RPC, D])
    d_e1kout = P.dram("e1kout", [8 * NFC * 128, RPC])
    d_e1vout = P.dram("e1vout", [8 * RPC, D])
    d_hkall = P.dram("hkall", [2 * NFC * 128, T // 2])
    d_hvall = P.dram("hvall", [T, D])
    d_hraw = P.dram("hraw", [104 * 128, 512])
    d_e2in = P.dram("e2in", [128, 64])
    d_e2out = P.dram("e2out", [2 * 128, 64])
    d_hq1 = P.dram("hq1", [H * 128, RPC])
    d_hk1 = P.dram("hk1", [H * 128, RPC])
    d_e3in = P.dram("e3in", [2305, 1024])
    d_e3outp = P.dram("e3outp", [2 * 2304, 1024])
    d_e3max = P.dram("e3max", [8, 1024], addr_space="Shared")
    d_hv1all = P.dram("hv1all", [T, D])
    d_hy = P.dram("hy", [H * 128, RPC])
    d_hxq = P.dram("hxq", [NFC * 128, RPC], BF16)

    XT = P.sb("XT", 128, NIC * 1024)
    XNT = P.sb("XNT", 128, NFC * RPC)
    IDT = P.sb("IDT", 128, 128)
    IDTH = P.sb("IDTH", 128, 128)
    ONESC = P.sb("ONESC", 128, 1)
    ONES1 = P.sb("ONES1", 1, 128)
    BQ = P.sb("BQ", 128, 8); BK = P.sb("BK", 128, 8)
    B1 = P.sb("B1", 128, 32)
    LNW = P.sb("LNW", 128, 8); LNB = P.sb("LNB", 128, 8)
    KNT = P.sb("KNT", 128, H)
    M0 = P.sb("M0", 128, NJT)
    S0 = P.sb("S0", 128, NJT)
    EB0 = P.sb("EB0", 128, NJT)
    PW = 14336
    POOL = P.sb("POOL", 128, PW)
    PB = [P.psum(f"pb{i}") for i in range(8)]

    def pool(o, w):
        assert o + w <= PW, (o, w)
        return POOL[:, o:o + w]

    def prow(o, w):
        assert o + w <= PW, (o, w)
        return POOL[0:1, o:o + w]

    def mm(lhsT, rhs, outp, start, stop, inc=None, n=1):
        return P.op("tensor", lambda e, a=lhsT, b=rhs, o=outp, st=start, sp=stop:
                    e.matmul(o, a, b, start=st, stop=sp), inc=inc, n=n)

    def transp(inp, outp, inc="pe", half=False, np_=128):
        idt = IDTH if half else IDT
        return P.op("tensor", lambda e, i=inp, o=outp, d=idt, n_=np_:
                    e.transpose(o, i, d[0:n_, 0:n_]), inc=inc)

    def act(outp, inp, func, bias=0.0, scale=1.0, accum=None, inc="act"):
        return P.op("scalar", lambda e, o=outp, i=inp, f=func, b=bias, s=scale, a=accum:
                    e.activation(o, i, f, bias=b, scale=s, accum_out=a), inc=inc)

    def tt(outp, a, b_, opx, inc="dve", eng="vector"):
        return P.op(eng, lambda e, o=outp, x=a, y=b_, op=opx:
                    e.tensor_tensor(o, x, y, op), inc=inc)

    def ts(outp, a, s1, s2, op0, op1=ALU.bypass, inc="dve", eng="vector"):
        return P.op(eng, lambda e, o=outp, x=a, u=s1, v=s2, p0=op0, p1=op1:
                    e.tensor_scalar(o, x, u, v, p0, p1), inc=inc)

    def stt(outp, in0, scal, in1, op0, op1, inc="dve", eng="vector"):
        return P.op(eng, lambda e, o=outp, x=in0, s=scal, y=in1, p0=op0, p1=op1:
                    e.scalar_tensor_tensor(o, x, s, y, p0, p1), inc=inc)

    def rsum(outp, inp, inc="dve", eng="vector"):
        return P.op(eng, lambda e, o=outp, i=inp: e.reduce_sum(o, i, axis=AX.X), inc=inc)

    def rmax(outp, inp, absval=False, inc="dve", eng="vector"):
        return P.op(eng, lambda e, o=outp, i=inp, a=absval:
                    e.reduce_max(o, i, axis=AX.X, apply_absolute_value=a), inc=inc)

    def recip(outp, inp, inc="dve"):
        return P.op("vector", lambda e, o=outp, i=inp: e.reciprocal(o, i), inc=inc)

    def copy_ev(outp, inp, inc="dve", eng="vector"):
        return P.op(eng, lambda e, o=outp, i=inp: e.tensor_copy(o, i), inc=inc)

    def bcast_row(row_ap, dest_sb, width, bank):
        for w0 in range(0, width, 512):
            w = min(512, width - w0)
            c1 = mm(ONES1[:, :], row_ap[:, w0:w0 + w], bank[:, 0:w], True, True, inc="pe")
            P.wait("vector", "pe", c1)
            cd = copy_ev(dest_sb[:, w0:w0 + w], bank[:, 0:w])
            P.wait("tensor", "dve", cd)

    for ap_, t in [(ident, IDT), (identh, IDTH), (onescol, ONESC), (ones1, ONES1),
                   (bq, BQ), (bk, BK), (b1, B1), (lnw, LNW), (lnb, LNB), (knT, KNT)]:
        P.dma("sync", t[:, :], ap_)
    for ic in range(NIC):
        P.dma("sync", XT[:, ic * 1024:(ic + 1) * 1024],
              x_in[ic * 128:(ic + 1) * 128, :])
    P.wait_now("tensor", "qs")
    P.wait_now("vector", "qs")
    P.wait_now("scalar", "qs")
    P.barrier()

    def ln_into_xnt():
        mcol, s2col = pool(0, NIC), pool(16, NIC)
        msqc, varc, rstdc = pool(32, NIC), pool(48, NIC), pool(64, NIC)
        sq, xn_t = pool(80, 1024), pool(1104, 1024)
        prev_r2 = [0]
        for ic in range(NIC):
            xc = XT[:, ic * 1024:(ic + 1) * 1024]
            rsum(mcol[:, ic:ic + 1], xc)
            P.wait("scalar", "dve", prev_r2[0])
            c_sq = act(sq, xc, AF.Square)
            P.wait("vector", "act", c_sq)
            prev_r2[0] = rsum(s2col[:, ic:ic + 1], sq)
        c_m = ts(mcol, mcol, 1.0 / 1024.0, None, ALU.mult)
        P.wait("scalar", "dve", c_m)
        c_vv = act(msqc, mcol, AF.Square)
        P.wait("vector", "act", c_vv)
        ts(s2col, s2col, 1.0 / 1024.0, None, ALU.mult)
        ts(msqc, msqc, -1.0, None, ALU.mult)
        tt(varc, s2col, msqc, ALU.add)
        c_v = ts(varc, varc, EPS_LN, None, ALU.add)
        P.wait("scalar", "dve", c_v)
        c_sa = act(rstdc, varc, AF.Sqrt)
        P.wait("vector", "act", c_sa)
        c_rc = recip(rstdc, rstdc)
        dbg_st = P.dbg_out("stats0", [128, 10])
        if dbg_st is not None:
            P.wait("sync", "dve", c_rc)
            P.dma("sync", dbg_st[:, 0:2], mcol[:, 14:16])
            P.dma("sync", dbg_st[:, 2:4], varc[:, 14:16])
            P.dma("sync", dbg_st[:, 4:6], rstdc[:, 14:16])
            P.dma("sync", dbg_st[:, 6:8], s2col[:, 14:16])
            P.dma("sync", dbg_st[:, 8:10], msqc[:, 14:16])
        tpose_dve = []
        for ic in range(NIC):
            xc = XT[:, ic * 1024:(ic + 1) * 1024]
            c_xn = ts(xn_t, xc, mcol[:, ic:ic + 1], rstdc[:, ic:ic + 1],
                      ALU.subtract, ALU.mult)
            P.wait("tensor", "dve", c_xn)
            for fc in range(NFC):
                bank = PB[6 + (fc % 2)]
                gi = ic * NFC + fc
                if gi >= 2:
                    P.wait("tensor", "dve", tpose_dve[gi - 2])
                c_t = transp(xn_t[:, fc * 128:(fc + 1) * 128], bank[:, 0:128])
                P.wait("vector", "pe", c_t)
                cd = copy_ev(XNT[:, fc * RPC + ic * 128: fc * RPC + (ic + 1) * 128],
                             bank[:, 0:128])
                tpose_dve.append(cd)

    ln_into_xnt()
    dxn = P.dbg_out("xnT", [NFC * 128, RPC])
    if dxn is not None:
        P.wait_now("sync", "dve")
        for fc in range(NFC):
            P.dma("sync", dxn[fc * 128:(fc + 1) * 128, :],
                  XNT[:, fc * RPC:(fc + 1) * RPC])
    P.barrier()

    if upto == "ln":
        for ic in range(NIC):
            P.dma("sync", out[ic * 128:(ic + 1) * 128, :],
                  XT[:, ic * 1024:(ic + 1) * 1024])
        P.wait_now("sync", "qs")
        return P

    wring = [pool(128 * i, 128) for i in range(4)]
    evacs = [pool(512, 512), pool(1024, 512)]
    vtile0 = pool(1536, 1024)
    vw = [pool(2560, 1024), pool(3584, 1024)]
    BVB = pool(4608, 1024)
    bvrow = prow(5632, 1024)
    c_bv = P.dma("sync", bvrow, bv, slot="rowld")
    P.wait("tensor", "rowld", c_bv)
    bcast_row(bvrow, BVB, 1024, PB[7])

    def proj_fmaj(W_ap, bias_sb, dest):
        mm_mark = {}
        ev_dma = []
        prev_evac = [0]
        k = 0
        for do in range(NFC):
            for di in range(NFC):
                slot = k % 4
                if k >= 4:
                    P.wait("sync", "pe", mm_mark[k - 4])
                wc = P.dma("sync", wring[slot], W_ap[di * 128:(di + 1) * 128,
                                                     do * 128:(do + 1) * 128],
                           slot=f"wr{slot}")
                P.wait("tensor", f"wr{slot}", wc)
                if di == 0:
                    P.wait("tensor", "act", prev_evac[0])
                for iw in range(NW):
                    mm(wring[slot], XNT[:, di * RPC + iw * 512: di * RPC + (iw + 1) * 512],
                       PB[iw][:, :], di == 0, di == NFC - 1,
                       inc="pe" if iw == NW - 1 else None)
                mm_mark[k] = P.ct("pe")
                k += 1
            pec = P.ct("pe")
            for iw in range(NW):
                ev = evacs[iw % 2]
                P.wait("scalar", "pe", pec)
                if len(ev_dma) >= 2:
                    P.wait("scalar", f"ev{iw % 2}", ev_dma[-2][1])
                ca = act(ev, PB[iw][:, :], AF.Identity, bias=bias_sb[:, do:do + 1])
                P.wait("scalar", "act", ca)
                prev_evac[0] = ca
                ev_dma.append((iw % 2,
                    P.dma("scalar", dest[do * 128:(do + 1) * 128, iw * 512:(iw + 1) * 512], ev,
                          slot=f"ev{iw % 2}")))
        P.wait_now("sync", "ev0")
        P.wait_now("sync", "ev1")

    proj_fmaj(Wq, BQ, d_hq.ap())
    proj_fmaj(Wk, BK, d_hkown.ap())

    mm_mark = {}
    ev_dma = []
    prev_stt = [0]
    for ic in range(NIC):
        for di in range(NFC):
            kk = ic * NFC + di
            slot = kk % 2
            if kk >= 2:
                P.wait("sync", "pe", mm_mark[kk - 2])
            wc = P.dma("sync", vw[slot], Wv[di * 128:(di + 1) * 128, :], slot=f"vw{slot}")
            P.wait("tensor", f"vw{slot}", wc)
            if di == 0:
                P.wait("tensor", "dve", prev_stt[0])
            for hf in range(2):
                mm(XNT[:, di * RPC + ic * 128: di * RPC + (ic + 1) * 128],
                   vw[slot][:, hf * 512:(hf + 1) * 512], PB[4 + hf][:, :],
                   di == 0, di == NFC - 1, inc="pe" if hf == 1 else None)
            mm_mark[kk] = P.ct("pe")
        P.wait_now("vector", "pe")
        if len(ev_dma) >= 1:
            P.wait("vector", "vst", ev_dma[-1])
        for hf in range(2):
            cs = stt(vtile0[:, hf * 512:(hf + 1) * 512], PB[4 + hf][:, :], 1.0,
                     BVB[:, hf * 512:(hf + 1) * 512], ALU.mult, ALU.add)
        prev_stt[0] = cs
        P.wait_now("sync", "dve")
        ev_dma.append(P.dma("sync", d_hvown.ap()[ic * 128:(ic + 1) * 128, :], vtile0,
                            slot="vst"))
    P.wait_now("sync", "vst")

    dq = P.dbg_out("q0", [NFC * 128, RPC])
    if dq is not None:
        P.dma("sync", dq, d_hq.ap())
    dk = P.dbg_out("k0", [NFC * 128, RPC])
    if dk is not None:
        P.dma("sync", dk, d_hkown.ap())
    dv = P.dbg_out("v0", [RPC, D])
    if dv is not None:
        P.dma("sync", dv, d_hvown.ap())
    P.barrier()

    B2B = pool(11776, 1024)
    b2row = prow(12800, 1024)
    c_b2 = P.dma("sync", b2row, b2, slot="rowld")
    P.wait("tensor", "rowld", c_b2)
    bcast_row(b2row, B2B, 1024, PB[7])

    u1big = pool(0, 2048)
    PF = [pool(2048, RPC), pool(4096, RPC)]
    for di in range(NFC):
        c_u1 = P.dma("sync", u1big[:, di * 256:(di + 1) * 256],
                     U1[di * 128:(di + 1) * 128, :], slot="u1")
    P.wait("tensor", "u1", c_u1)
    for rc in range(2):
        for iw in range(NW):
            for di in range(NFC):
                mm(u1big[:, di * 256 + rc * 128: di * 256 + rc * 128 + 128],
                   XNT[:, di * RPC + iw * 512: di * RPC + (iw + 1) * 512],
                   PB[rc * 4 + iw][:, :], di == 0, di == NFC - 1,
                   inc="pe" if di == NFC - 1 else None)
    pec = P.ct("pe")
    P.wait("vector", "pe", pec)
    for rc in range(2):
        for iw in range(NW):
            copy_ev(PF[rc][:, iw * 512:(iw + 1) * 512], PB[rc * 4 + iw][:, :])
    c_pf = P.ct("dve")
    P.wait("tensor", "dve", c_pf)
    for ic in range(NIC):
        tt(XT[:, ic * 1024:(ic + 1) * 1024], XT[:, ic * 1024:(ic + 1) * 1024],
           B2B[:, :], ALU.add)

    QF = [pool(6144, RPC), pool(8192, RPC)]
    htile = pool(10240, 512)
    v1s = [pool(10752, 128), pool(10880, 128), pool(11008, 128), pool(11136, 128)]
    u2s = [pool(11264, 256), pool(11520, 256)]
    wq_marks = []
    for iw in range(NW):
        for fc in range(32):
            it = iw * 32 + fc
            sv = [v1s[(2 * it) % 4], v1s[(2 * it + 1) % 4]]
            su = u2s[it % 2]
            if it >= 2:
                P.wait("sync", "pe", wq_marks[it - 2])
            sl4 = (2 * it) % 4
            c_a = P.dma("sync", sv[0], V1[0:128, fc * 128:(fc + 1) * 128], slot=f"hv{sl4}")
            c_b = P.dma("sync", sv[1], V1[128:256, fc * 128:(fc + 1) * 128], slot=f"hv{(sl4 + 1) % 4}")
            c_w = P.dma("sync", su, U2[fc * 128:(fc + 1) * 128, :], slot=f"hu{it % 2}")
            P.wait("tensor", f"hv{sl4}", c_a)
            P.wait("tensor", f"hv{(sl4 + 1) % 4}", c_b)
            P.wait("tensor", f"hu{it % 2}", c_w)
            for rc in range(2):
                mm(sv[rc], PF[rc][:, iw * 512:(iw + 1) * 512], PB[0][:, :],
                   rc == 0, rc == 1, inc="pe" if rc == 1 else None)
            c_h = P.ct("pe")
            P.wait("scalar", "pe", c_h)
            c_g = act(htile, PB[0][:, :], AF.Gelu_apprx_tanh, bias=B1[:, fc:fc + 1])
            P.wait("tensor", "act", c_g)
            for rc in range(2):
                mm(su[:, rc * 128:(rc + 1) * 128], htile, PB[1 + rc][:, :],
                   fc == 0, fc == 31, inc="pe" if rc == 1 else None)
            wq_marks.append(P.ct("pe"))
        P.wait_now("vector", "pe")
        for rc in range(2):
            copy_ev(QF[rc][:, iw * 512:(iw + 1) * 512], PB[1 + rc][:, :])
        P.wait("tensor", "dve", P.ct("dve"))
    v2a = pool(0, 1024)
    v2b = pool(1024, 1024)
    c1 = P.dma("sync", v2a, V2[0:128, :], slot="v2l")
    c2 = P.dma("sync", v2b, V2[128:256, :], slot="v2l")
    P.wait("tensor", "v2l", c2)
    for ic in range(NIC):
        for hf in range(2):
            mm(QF[0][:, ic * 128:(ic + 1) * 128], v2a[:, hf * 512:(hf + 1) * 512],
               PB[4 + hf][:, :], True, False)
            mm(QF[1][:, ic * 128:(ic + 1) * 128], v2b[:, hf * 512:(hf + 1) * 512],
               PB[4 + hf][:, :], False, True, inc="pe" if hf == 1 else None)
        P.wait_now("vector", "pe")
        for hf in range(2):
            tt(XT[:, ic * 1024 + hf * 512: ic * 1024 + (hf + 1) * 512],
               XT[:, ic * 1024 + hf * 512: ic * 1024 + (hf + 1) * 512],
               PB[4 + hf][:, :], ALU.add)
        P.wait("tensor", "dve", P.ct("dve"))
    P.barrier()

    if upto == "p3":
        for ic in range(NIC):
            P.dma("sync", out[ic * 128:(ic + 1) * 128, :],
                  XT[:, ic * 1024:(ic + 1) * 1024])
        P.wait_now("sync", "qs")
        return P

    PAIRS = [[0, 1], [2, 3], [4, 5], [6, 7]]

    def collective(kind, op, ins_ap, outs_ap, groups=None):
        gr = groups if groups is not None else [list(range(8))]
        def g(e, k=kind, o=op, i=ins_ap, u=outs_ap, s=P.sems["cc"], gr=gr):
            e.collective_compute(k, o, replica_groups=gr,
                                 ins=[i], outs=[u]).then_inc(s, 1)
        P._emit("gpsimd", g)
        P.cnt["cc"] = P.cnt.get("cc", 0) + 1
        return P.cnt["cc"]

    c_kag = collective("AllGather", ALU.bypass, d_hkown.ap(), d_hkall.ap(),
                       groups=PAIRS)
    c_vag = collective("AllGather", ALU.bypass, d_hvown.ap(), d_hvall.ap(),
                       groups=PAIRS)
    P.wait("gpsimd", "cc", c_vag)
    P.barrier()

    if upto == "p4":
        for ic in range(NIC):
            P.dma("sync", out[ic * 128:(ic + 1) * 128, :],
                  XT[:, ic * 1024:(ic + 1) * 1024])
        P.wait_now("sync", "qs")
        return P

    NWIN = 8

    def l0_windows(parity):
        r = range(4 * parity, 4 * parity + 4)
        if p5_trunc:
            r = list(r)[:1]
        return r

    def l0_tiles(parity):
        outl = []
        lin = 0
        for giw in l0_windows(parity):
            for jt in range(4 * giw + 4):
                outl.append((giw, jt, lin))
                lin += 1
        return outl

    def p5_stats(parity):
        def self_cnt_rw(lin):
            return P.ct(f"rw{lin % 3}")
        CMASK = pool(8736, 2560)
        ccm = P.dma("sync", CMASK, cmask_i, slot="cml")
        P.wait("vector", "cml", ccm)
        qwin = pool(0, 4096)
        kslot = [pool(4096, 1024), pool(5120, 1024)]
        raws = [pool(6144, 512), pool(6656, 512), pool(7168, 512)]
        wmax = pool(7680, 256)
        P.op("gpsimd", lambda e, w=wmax: e.memset(w, -1e30), inc="gps")
        P.op("gpsimd", lambda e, w=M0[:, :]: e.memset(w, -1e30), inc="gps")
        P.op("gpsimd", lambda e, w=S0[:, :]: e.memset(w, 0.0), inc="gps")
        P.wait_now("vector", "gps")
        P.wait_now("sync", "gps")
        tiles = l0_tiles(parity)
        k_marks = {}
        raw_dma = {}
        raw_act = {}
        raw_rmax = {}
        for giw in l0_windows(parity):
            iwl = giw - 4 * parity
            cq = P.dma("sync", qwin.rearrange("p (a m) -> p a m", a=NFC),
                       d_hq.ap()[:, iwl * 512:(iwl + 1) * 512]
                       .rearrange("(a p) m -> p a m", p=128), slot="qw")
            P.wait("tensor", "qw", cq)
            wslot = iwl
            for gjt in range(4 * giw + 4):
                giw_, jt, lin = [t for t in tiles if t[0] == giw and t[1] == gjt][0]
                ks = kslot[lin % 2]
                if lin >= 2:
                    P.wait("sync", "pe", k_marks[lin - 2])
                half, jl = jt // 16, jt % 16
                ck = P.dma("sync", ks.rearrange("p (a m) -> p a m", a=NFC),
                           d_hkall.ap()[half * 1024:(half + 1) * 1024,
                                        jl * 128:(jl + 1) * 128]
                           .rearrange("(a p) m -> p a m", p=128), slot=f"kl{lin % 2}")
                P.wait("tensor", f"kl{lin % 2}", ck)
                bank = PB[lin % 2]
                if lin >= 2:
                    P.wait("tensor", "dve", raw_act[lin - 2])
                for fc in range(NFC):
                    mm(ks[:, fc * 128:(fc + 1) * 128], qwin[:, fc * 512:(fc + 1) * 512],
                       bank[:, :], fc == 0, fc == NFC - 1,
                       inc="pe" if fc == NFC - 1 else None)
                k_marks[lin] = P.ct("pe")
                if p5_trunc == 7:
                    raw_act[lin] = P.ct("act")
                    raw_dma[lin] = P.ct("qs")
                    raw_rmax[lin] = P.ct("dve")
                    continue
                rt = raws[lin % 3]
                if lin >= 3:
                    P.wait("vector", f"rw{lin % 3}", raw_dma[lin - 3])
                P.wait("vector", "pe", k_marks[lin])
                mo = jt - 4 * giw + 4 if jt < 4 * giw else jt - 4 * giw
                mo = (jt - 4 * giw) if jt >= 4 * giw else 4
                ca = stt(rt, bank[:, :], 1.0, CMASK[:, mo * 512:(mo + 1) * 512],
                         ALU.mult, ALU.add)
                raw_act[lin] = ca
                P.wait("sync", "dve", ca)
                raw_dma[lin] = P.dma("sync", d_hraw.ap()[lin * 128:(lin + 1) * 128, :], rt,
                                     slot=f"rw{lin % 3}")
                raw_rmax[lin] = rmax(wmax[:, jt * 8 + wslot: jt * 8 + wslot + 1], rt)
        P.wait_now("sync", "dve")
        for jt in range(NJT):
            if parity == 0 and jt >= 16:
                continue
            rmax(M0[:, jt:jt + 1], wmax[:, jt * 8:(jt + 1) * 8])
        c_all = P.ct("dve")
        if p5_trunc == 3:
            return
        wsum = pool(7936, 256)
        burn = pool(8192, 512)
        negM = pool(8704, 32)
        P.op("gpsimd", lambda e, w=wsum: e.memset(w, 0.0), inc="gps")
        P.wait_now("vector", "gps")
        c_nm = ts(negM, M0[:, :], -1.0, None, ALU.mult)
        P.wait("scalar", "dve", c_nm)
        P.wait_now("scalar", "gps")
        rd_marks = {}
        for sl in range(3):
            if P.ct(f"rw{sl}"):
                P.wait("sync", f"rw{sl}", P.ct(f"rw{sl}"))
        P.wait_now("sync", "dve")
        for giw in l0_windows(parity):
            iwl = giw - 4 * parity
            wslot = iwl
            for gjt in range(4 * giw + 4):
                giw_, jt, lin = [t for t in tiles if t[0] == giw and t[1] == gjt][0]
                rt = raws[lin % 3]
                if lin >= 3:
                    P.wait("sync", "act", rd_marks[lin - 3])
                crd = P.dma("sync", rt, d_hraw.ap()[lin * 128:(lin + 1) * 128, :],
                            slot=f"rw{lin % 3}")
                P.wait("scalar", f"rw{lin % 3}", crd)
                rd_marks[lin] = act(burn, rt, AF.Exp, bias=negM[:, jt:jt + 1],
                                    accum=wsum[:, jt * 8 + wslot: jt * 8 + wslot + 1])
        c_exp = P.ct("act")
        P.wait("vector", "act", c_exp)
        if p5_trunc == 2:
            return
        for jt in range(NJT):
            if parity == 0 and jt >= 16:
                continue
            rsum(S0[:, jt:jt + 1], wsum[:, jt * 8:(jt + 1) * 8])

    P.parity(lambda: p5_stats(0), lambda: p5_stats(1))
    P.barrier()

    if upto == "p5":
        for ic in range(NIC):
            P.dma("sync", out[ic * 128:(ic + 1) * 128, :],
                  XT[:, ic * 1024:(ic + 1) * 1024])
        P.wait_now("sync", "qs")
        return P

    P.dma("gpsimd", d_e2in.ap()[:, 0:32], M0[:, :])
    cqg = P.dma("gpsimd", d_e2in.ap()[:, 32:64], S0[:, :])
    P.wait("gpsimd", "qg", cqg)

    def g_e2(eng):
        eng.collective_compute("AllGather", ALU.bypass,
                               replica_groups=[[0, 1], [2, 3], [4, 5], [6, 7]],
                               ins=[d_e2in.ap()], outs=[d_e2out.ap()]
                               ).then_inc(P.sems["cc"], 1)
    P._emit("gpsimd", g_e2)
    P.cnt["cc"] = P.cnt.get("cc", 0) + 1
    P.wait("gpsimd", "cc", P.cnt["cc"])
    Ma = pool(8736, 32); Sa = pool(8768, 32)
    Mb = pool(8800, 32); Sb = pool(8832, 32)
    tca = pool(8864, 32); tcb = pool(8896, 32)
    P.dma("gpsimd", Ma, d_e2out.ap()[0:128, 0:32])
    P.dma("gpsimd", Sa, d_e2out.ap()[0:128, 32:64])
    P.dma("gpsimd", Mb, d_e2out.ap()[128:256, 0:32])
    ce2 = P.dma("gpsimd", Sb, d_e2out.ap()[128:256, 32:64])
    P.wait("vector", "qg", ce2)
    tt(M0[:, :], Ma, Mb, ALU.max)
    c1 = ts(M0[:, :], M0[:, :], -1.0, None, ALU.mult)
    tt(tca, Ma, M0[:, :], ALU.add)
    c2 = tt(tcb, Mb, M0[:, :], ALU.add)
    P.wait("scalar", "dve", c2)
    ca1 = act(tca, tca, AF.Exp)
    ca2 = act(tcb, tcb, AF.Exp)
    P.wait("vector", "act", ca2)
    tt(tca, tca, Sa, ALU.mult)
    tt(tcb, tcb, Sb, ALU.mult)
    tt(S0[:, :], tca, tcb, ALU.add)
    c3 = ts(S0[:, :], S0[:, :], 32.0, None, ALU.mult)
    P.wait("scalar", "dve", c3)
    c4 = act(EB0[:, :], S0[:, :], AF.Ln)
    P.wait("vector", "act", c4)
    c5 = ts(EB0[:, :], EB0[:, :], -1.0, None, ALU.mult)
    ceb = tt(EB0[:, :], EB0[:, :], M0[:, :], ALU.add)
    P.wait("scalar", "dve", ceb)
    P.wait("tensor", "dve", ceb)
    P.barrier()

    if upto == "p6":
        for ic in range(NIC):
            P.dma("sync", out[ic * 128:(ic + 1) * 128, :],
                  XT[:, ic * 1024:(ic + 1) * 1024])
        P.wait_now("sync", "qs")
        return P

    def p7_rows(parity):
        vslot = [pool(4096, 1024), pool(5120, 1024)]
        praw = [pool(6144, 512), pool(6656, 512)]
        ptile = [pool(7168, 512), pool(7680, 512)]
        readF = pool(0, 4096)
        tiles = l0_tiles(parity)
        pv_marks = {}
        exp_marks = {}
        for giw in l0_windows(parity):
            iwl = giw - 4 * parity
            J = 4 * giw + 4
            for gjt in range(J):
                giw_, jt, lin = [t for t in tiles if t[0] == giw and t[1] == gjt][0]
                sl = lin % 2
                if lin >= 2:
                    P.wait("sync", "pe", pv_marks[lin - 2])
                crd = P.dma("sync", praw[sl], d_hraw.ap()[lin * 128:(lin + 1) * 128, :],
                            slot=f"p7r{sl}")
                cv = P.dma("sync", vslot[sl].rearrange("p (a m) -> p a m", a=NFC),
                           d_hvall.ap()[jt * 128:(jt + 1) * 128, :]
                           .rearrange("p (a m) -> p a m", a=NFC), slot=f"p7v{sl}")
                P.wait("scalar", f"p7r{sl}", crd)
                P.wait("tensor", f"p7v{sl}", cv)
                if lin >= 2:
                    P.wait("scalar", "pe", pv_marks[lin - 2])
                ce = act(ptile[sl], praw[sl], AF.Exp, bias=EB0[:, jt:jt + 1])
                exp_marks[lin] = ce
                P.wait("tensor", "act", ce)
                for dc in range(NFC):
                    mm(vslot[sl][:, dc * 128:(dc + 1) * 128], ptile[sl],
                       PB[dc][:, :], gjt == 0, gjt == J - 1,
                       inc="pe" if dc == NFC - 1 else None)
                pv_marks[lin] = P.ct("pe")
            P.wait_now("vector", "pe")
            for dc in range(NFC):
                stt(readF[:, dc * 512:(dc + 1) * 512], PB[dc][:, :], 1.0,
                    XNT[:, dc * RPC + iwl * 512: dc * RPC + (iwl + 1) * 512],
                    ALU.mult, ALU.add)
            cev = P.ct("dve")
            P.wait("tensor", "dve", cev)
            tmarks = []
            for dc in range(NFC):
                for sub in range(4):
                    gi = dc * 4 + sub
                    bank = PB[gi % 2]
                    if gi >= 2:
                        P.wait("tensor", "dve", tmarks[gi - 2])
                    c_t = transp(readF[:, dc * 512 + sub * 128: dc * 512 + (sub + 1) * 128],
                                 bank[:, 0:128])
                    P.wait("vector", "pe", c_t)
                    ic = iwl * 4 + sub
                    cd = tt(XT[:, ic * 1024 + dc * 128: ic * 1024 + (dc + 1) * 128],
                            XT[:, ic * 1024 + dc * 128: ic * 1024 + (dc + 1) * 128],
                            bank[:, 0:128], ALU.add)
                    tmarks.append(cd)
            P.wait("tensor", "dve", P.ct("dve"))

    P.parity(lambda: p7_rows(0), lambda: p7_rows(1))
    P.barrier()

    M2C = P.sb("M2C", 128, NIC)
    R2C = P.sb("R2C", 128, NIC)
    NR2M2 = P.sb("NR2M2", 128, NIC)
    XSB = P.sb("XSB", 128, 16)

    def ln1_into_xnt():
        mcol, s2col = pool(0, NIC), pool(16, NIC)
        msqc, varc, rstdc = pool(32, NIC), pool(48, NIC), pool(64, NIC)
        sq, xn_t = pool(80, 1024), pool(1104, 1024)
        amax = pool(2128, NIC)
        prev_r2 = [0]
        for ic in range(NIC):
            xc = XT[:, ic * 1024:(ic + 1) * 1024]
            rsum(mcol[:, ic:ic + 1], xc)
            P.wait("scalar", "dve", prev_r2[0])
            c_sq = act(sq, xc, AF.Square)
            P.wait("vector", "act", c_sq)
            prev_r2[0] = rsum(s2col[:, ic:ic + 1], sq)
        c_m = ts(mcol, mcol, 1.0 / 1024.0, None, ALU.mult)
        P.wait("scalar", "dve", c_m)
        c_vv = act(msqc, mcol, AF.Square)
        P.wait("vector", "act", c_vv)
        ts(s2col, s2col, 1.0 / 1024.0, None, ALU.mult)
        ts(msqc, msqc, -1.0, None, ALU.mult)
        tt(varc, s2col, msqc, ALU.add)
        c_v = ts(varc, varc, EPS_LN, None, ALU.add)
        P.wait("scalar", "dve", c_v)
        c_sa = act(rstdc, varc, AF.Sqrt)
        P.wait("vector", "act", c_sa)
        recip(rstdc, rstdc)
        tpose_dve = []
        prev2 = [0]
        for ic in range(NIC):
            xc = XT[:, ic * 1024:(ic + 1) * 1024]
            c_xn = ts(xn_t, xc, mcol[:, ic:ic + 1], rstdc[:, ic:ic + 1],
                      ALU.subtract, ALU.mult)
            rsum(M2C[:, ic:ic + 1], xn_t)
            rmax(amax[:, ic:ic + 1], xn_t, absval=True)
            P.wait("scalar", "dve", c_xn)
            c_s2 = act(sq, xn_t, AF.Square)
            P.wait("vector", "act", c_s2)
            prev2[0] = rsum(R2C[:, ic:ic + 1], sq)
            P.wait("tensor", "dve", prev2[0])
            for fc in range(NFC):
                bank = PB[6 + (fc % 2)]
                gi = ic * NFC + fc
                if gi >= 2:
                    P.wait("tensor", "dve", tpose_dve[gi - 2])
                c_t = transp(xn_t[:, fc * 128:(fc + 1) * 128], bank[:, 0:128])
                P.wait("vector", "pe", c_t)
                cd = copy_ev(XNT[:, fc * RPC + ic * 128: fc * RPC + (ic + 1) * 128],
                             bank[:, 0:128])
                tpose_dve.append(cd)
        c1 = ts(M2C[:, :], M2C[:, :], 1.0 / 1024.0, None, ALU.mult)
        P.wait("scalar", "dve", c1)
        c2 = act(msqc, M2C[:, :], AF.Square)
        P.wait("vector", "act", c2)
        ts(R2C[:, :], R2C[:, :], 1.0 / 1024.0, None, ALU.mult)
        ts(msqc, msqc, -1.0, None, ALU.mult)
        tt(varc, R2C[:, :], msqc, ALU.add)
        c3 = ts(varc, varc, EPS_LN, None, ALU.add)
        P.wait("scalar", "dve", c3)
        c4 = act(R2C[:, :], varc, AF.Sqrt)
        P.wait("vector", "act", c4)
        recip(R2C[:, :], R2C[:, :])
        tt(NR2M2[:, :], R2C[:, :], M2C[:, :], ALU.mult)
        ts(NR2M2[:, :], NR2M2[:, :], -1.0, None, ALU.mult)
        c5 = rmax(XSB[:, 0:1], amax)
        P.wait("tensor", "dve", P.ct("dve"))
        c_tm = transp(XSB[:, 0:1], PB[6][0:1, 0:128])
        P.wait("scalar", "pe", c_tm)
        c_ev = act(POOL[0:1, 3000:3128], PB[6][0:1, 0:128], AF.Identity)
        P.wait("vector", "act", c_ev)
        c6 = rmax(XSB[0:1, 1:2], POOL[0:1, 3000:3128])
        P.wait("sync", "dve", c6)

    ln1_into_xnt()
    P.barrier()
    dx1 = P.dbg_out("xn1T", [NFC * 128, RPC])
    if dx1 is not None:
        for fc in range(NFC):
            P.dma("sync", dx1[fc * 128:(fc + 1) * 128, :],
                  XNT[:, fc * RPC:(fc + 1) * RPC])
        P.wait_now("sync", "qs")

    if upto == "p8":
        for ic in range(NIC):
            P.dma("sync", out[ic * 128:(ic + 1) * 128, :],
                  XT[:, ic * 1024:(ic + 1) * 1024])
        P.wait_now("sync", "qs")
        return P

    CTABp = pool(0, RPC)
    STABp = pool(2048, RPC)
    c_ct = P.dma("sync", CTABp, ctab_i, slot="rowld")
    c_ct = P.dma("sync", STABp, stab_i, slot="rowld")
    P.wait("vector", "rowld", c_ct)
    qraw = pool(4096, RPC)
    qswap = pool(6144, RPC)
    qrope = pool(8192, RPC)
    wring2 = [pool(10240, 128), pool(10368, 128)]
    tpsum = PB[6]

    def qk_proj(W_ap, dest_dram, swslot):
        for h in range(H):
            for di in range(NFC):
                sl = di % 2
                if di >= 2:
                    P.wait("sync", "pe", wt_marks[sl])
                cw = P.dma("sync", wring2[sl], W_ap[h * 128:(h + 1) * 128,
                                                    di * 128:(di + 1) * 128],
                           slot=f"w2{sl}")
                P.wait("tensor", f"w2{sl}", cw)
                P.wait("tensor", "dve", wt_last[0])
                c_t = transp(wring2[sl], tpsum[:, 0:128])
                P.wait("vector", "pe", c_t)
                wt_last[0] = copy_ev(wring2[sl], tpsum[:, 0:128])
                P.wait("tensor", "dve", wt_last[0])
                for iw in range(NW):
                    mm(wring2[sl], XNT[:, di * RPC + iw * 512: di * RPC + (iw + 1) * 512],
                       PB[iw][:, :], di == 0, di == NFC - 1,
                       inc="pe" if iw == NW - 1 else None)
                wt_marks[sl] = P.ct("pe")
            cmm = P.ct("pe")
            P.wait("scalar", "pe", cmm)
            for iw in range(NW):
                act(qraw[:, iw * 512:(iw + 1) * 512], PB[iw][:, :], AF.Identity)
            cev = P.ct("act")
            P.wait("gpsimd", "act", cev)
            cs1 = P.dma("gpsimd", qswap[0:64, :], qraw[64:128, :], slot=swslot)
            cs2 = P.dma("gpsimd", qswap[64:128, :], qraw[0:64, :], slot=swslot)
            P.wait("vector", swslot, cs2)
            P.wait("vector", "act", cev)
            tt(qrope, qraw, CTABp, ALU.mult)
            tt(qraw, qswap, STABp, ALU.mult)
            c_rope = tt(qrope, qrope, qraw, ALU.add)
            P.wait("scalar", "dve", c_rope)
            c_sq = act(qswap, qrope, AF.Square)
            P.wait("tensor", "act", c_sq)
            for iw in range(NW):
                mm(ONESC[:, :], qswap[:, iw * 512:(iw + 1) * 512],
                   PB[iw][0:1, :], True, True, inc="pe" if iw == NW - 1 else None)
            cnm = P.ct("pe")
            P.wait("scalar", "pe", cnm)
            for iw in range(NW):
                act(qraw[0:1, iw * 512:(iw + 1) * 512], PB[iw][0:1, :], AF.Sqrt)
            cno = P.ct("act")
            P.wait("vector", "act", cno)
            ts(qraw[0:1, :], qraw[0:1, :], 1e-6, None, ALU.add)
            c_rc = recip(qraw[0:1, :], qraw[0:1, :])
            P.wait("tensor", "dve", c_rc)
            for iw in range(NW):
                mm(ONES1[:, :], qraw[0:1, iw * 512:(iw + 1) * 512],
                   PB[iw][:, :], True, True, inc="pe" if iw == NW - 1 else None)
            cbb = P.ct("pe")
            P.wait("vector", "pe", cbb)
            for iw in range(NW):
                tt(qrope[:, iw * 512:(iw + 1) * 512], qrope[:, iw * 512:(iw + 1) * 512],
                   PB[iw][:, :], ALU.mult)
            chat = P.ct("dve")
            P.wait("sync", "dve", chat)
            cst = P.dma("sync", dest_dram[h * 128:(h + 1) * 128, :], qrope, slot="qkst")
            qk_st.append(cst)
            P.wait("tensor", "act", cev)
            P.wait("tensor", "dve", chat)
            P.wait("vector", "qkst", cst)

    wt_marks = {0: 0, 1: 0}
    wt_evac = {0: 0, 1: 0}
    wt_last = [0]
    wt_tp = {0: 0, 1: 0}
    qk_st = []
    qk_proj(WQ1, d_hq1.ap(), "sw0")
    qk_proj(WK1, d_hk1.ap(), "sw1")
    P.barrier()

    dq1 = P.dbg_out("q1", [H * 128, RPC])
    if dq1 is not None:
        P.dma("sync", dq1, d_hq1.ap())
        P.wait_now("sync", "qs")
    dk1 = P.dbg_out("k1", [H * 128, RPC])
    if dk1 is not None:
        P.dma("sync", dk1, d_hk1.ap())
        P.wait_now("sync", "qs")

    d_hv1own = P.dram("hv1own", [RPC, D])
    wvt = pool(10496, 1024)
    wvrow = prow(11520, 1024)
    wbrow = prow(12544, 1024)
    wbbc = pool(4096, 128)
    bpbc = pool(4224, 128)
    vtile = pool(4352, 128)
    t1c = pool(4480, 128)
    for do in range(NFC):
        for di in range(NFC):
            sl = di % 2
            P.wait("sync", "pe", wt_tp[sl])
            cw = P.dma("sync", wring2[sl], WV1[do * 128:(do + 1) * 128,
                                               di * 128:(di + 1) * 128],
                       slot=f"w2{sl}")
            P.wait("tensor", f"w2{sl}", cw)
            P.wait("tensor", "dve", wt_last[0])
            c_t = transp(wring2[sl], tpsum[:, 0:128])
            wt_tp[sl] = c_t
            P.wait("vector", "pe", c_t)
            cpl = copy_ev(wvt[:, di * 128:(di + 1) * 128], tpsum[:, 0:128])
            wt_last[0] = cpl
            P.wait("tensor", "dve", cpl)
            cbm = mm(LNB[:, di:di + 1], wvt[:, di * 128:(di + 1) * 128],
                     PB[7][0:1, 0:128], di == 0, di == NFC - 1, inc="pe")
            P.wait("vector", "pe", cbm)
            cs = ts(wvt[:, di * 128:(di + 1) * 128], wvt[:, di * 128:(di + 1) * 128],
                    LNW[:, di:di + 1], None, ALU.mult)
            P.wait("tensor", "dve", cs)
            cwb = mm(ONESC[:, :], wvt[:, di * 128:(di + 1) * 128],
                     PB[5][0:1, 0:128], di == 0, di == NFC - 1, inc="pe")
        P.wait("scalar", "pe", P.ct("pe"))
        act(wvrow[:, do * 128:(do + 1) * 128], PB[7][0:1, 0:128], AF.Identity)
        cwv = act(wbrow[:, do * 128:(do + 1) * 128], PB[5][0:1, 0:128], AF.Identity)
        P.wait("tensor", "act", cwv)
        mm(ONES1[:, :], wvrow[:, do * 128:(do + 1) * 128], PB[7][:, 0:128], True, True)
        c_b = mm(ONES1[:, :], wbrow[:, do * 128:(do + 1) * 128], PB[5][:, 0:128],
                 True, True, inc="pe")
        P.wait("vector", "pe", c_b)
        copy_ev(bpbc, PB[7][:, 0:128])
        c_bb = copy_ev(wbbc, PB[5][:, 0:128])
        P.wait("tensor", "dve", c_bb)
        for ic in range(NIC):
            bank = PB[ic % 4]
            for di in range(NFC):
                mm(XNT[:, di * RPC + ic * 128: di * RPC + (ic + 1) * 128],
                   wvt[:, di * 128:(di + 1) * 128], bank[:, 0:128],
                   di == 0, di == NFC - 1, inc="pe" if di == NFC - 1 else None)
            cpv = P.ct("pe")
            P.wait("vector", "pe", cpv)
            stt(t1c, wbbc, NR2M2[:, ic:ic + 1], bpbc, ALU.mult, ALU.add)
            cvt = stt(vtile, bank[:, 0:128], R2C[:, ic:ic + 1], t1c, ALU.mult, ALU.add)
            P.wait("sync", "dve", cvt)
            cvs = P.dma("sync", d_hv1own.ap()[ic * 128:(ic + 1) * 128,
                                              do * 128:(do + 1) * 128], vtile,
                        slot="v1st")
            P.wait("vector", "v1st", cvs)
            P.wait("tensor", "dve", cvt)
    P.barrier()

    dv1 = P.dbg_out("v1", [RPC, D])
    if dv1 is not None:
        P.dma("sync", dv1, d_hv1own.ap())
        P.wait_now("sync", "qs")

    if upto == "p9":
        for ic in range(NIC):
            P.dma("sync", out[ic * 128:(ic + 1) * 128, :],
                  XT[:, ic * 1024:(ic + 1) * 1024])
        P.wait_now("sync", "qs")
        return P

    bmask_i2 = einp("bmask", [128, 4 * 192])
    fmask_i2 = einp("fmask", [128, 5 * 512])
    QFO = P.sb("QFO", 128, H * 64)
    KFALL = P.sb("KFALL", 128, H * 128)
    VNALL = P.sb("VNALL", 128, H * 128)
    KNH = P.sb("KNH", 128, H)
    VNOWN_R = P.sb("VNOWN_R", 128, H * 64)
    d_bndk = P.dram("bndk", [H * 128, 64])
    d_bndx = P.dram("bndx", [NFC * 128, 64])
    d_hband = P.dram("hband", [H * NIC * 128, 192])
    d_hw8t = P.dram("hw8t", [D, D], BF16)

    qhead = pool(0, RPC)
    qtok = pool(2048, 128)
    WTR = pool(4416, NIC * 64)
    cwt = P.dma("sync", WTR, wtrows_i, slot="rowld")
    P.wait("tensor", "rowld", cwt)
    cftmp = pool(2176, 512)
    nrmsc = pool(2688, 128)
    KFOWN = pool(2816, H * 64)
    VFOWN = pool(3328, H * 64)
    sqv = pool(3840, 64)

    coarse_pe = [0]

    def coarse_for(src_dram, dest_sb, is_v):
        last_pe = coarse_pe
        for h in range(H):
            if not is_v:
                P.wait("sync", "pe", last_pe[0])
                cl = P.dma("sync", qhead, src_dram[h * 128:(h + 1) * 128, :], slot="chl")
                P.wait("tensor", "chl", cl)
            last_ev = [P.ct("dve")]
            for ic in range(NIC):
                if is_v:
                    P.wait("sync", "pe", last_pe[0])
                    cl = P.dma("sync", qtok, src_dram[ic * 128:(ic + 1) * 128,
                                                      h * 128:(h + 1) * 128], slot="chl")
                    P.wait("tensor", "chl", cl)
                    P.wait("tensor", "dve", last_ev[0])
                    srct = qtok
                else:
                    P.wait("tensor", "dve", last_ev[0])
                    c_t = transp(qhead[:, ic * 128:(ic + 1) * 128], PB[6][:, 0:128])
                    P.wait("vector", "pe", c_t)
                    last_ev[0] = copy_ev(qtok, PB[6][:, 0:128])
                    P.wait("tensor", "dve", last_ev[0])
                    srct = qtok
                cmmv = mm(srct, WTR[:, ic * 64:(ic + 1) * 64], PB[7][:, 0:64],
                          ic == 0, ic == NIC - 1, inc="pe")
                if is_v:
                    last_pe[0] = cmmv
                    last_ev[0] = P.ct("dve")
            cqf = P.ct("pe")
            last_pe[0] = cqf
            P.wait("vector", "pe", cqf)
            cev = copy_ev(dest_sb[:, h * 64:(h + 1) * 64], PB[7][:, 0:64])
            P.wait("tensor", "dve", cev)

    coarse_for(d_hq1.ap(), QFO, False)
    coarse_for(d_hk1.ap(), KFOWN, False)
    coarse_for(d_hv1own.ap(), VFOWN, True)
    VNOWN = pool(3904, H * 64)
    P.wait("scalar", "dve", P.ct("dve"))
    for h in range(H):
        c_s = act(sqv, VFOWN[:, h * 64:(h + 1) * 64], AF.Square)
        P.wait("tensor", "act", c_s)
        c_n = mm(ONESC[:, :], sqv[:, 0:64], PB[7][0:1, 0:64], True, True, inc="pe")
        P.wait("scalar", "pe", c_n)
        c_sa = act(nrmsc[0:1, 0:64], PB[7][0:1, 0:64], AF.Sqrt)
        P.wait("vector", "act", c_sa)
        ts(nrmsc[0:1, 0:64], nrmsc[0:1, 0:64], 1e-12, None, ALU.max)
        c_r = recip(nrmsc[0:1, 0:64], nrmsc[0:1, 0:64])
        P.wait("tensor", "dve", c_r)
        c_b = mm(ONES1[:, :], nrmsc[0:1, 0:64], PB[7][:, 0:64], True, True, inc="pe")
        P.wait("vector", "pe", c_b)
        c_v = tt(VNOWN[:, h * 64:(h + 1) * 64], VFOWN[:, h * 64:(h + 1) * 64],
                 PB[7][:, 0:64], ALU.mult)
        cvr = copy_ev(VNOWN_R[:, h * 64:(h + 1) * 64], VNOWN[:, h * 64:(h + 1) * 64])
        P.wait("tensor", "dve", cvr)
    P.wait("scalar", "dve", P.ct("dve"))
    c_s = act(nrmsc[:, 0:8], KNT[:, :], AF.Square)
    P.wait("tensor", "act", c_s)
    c_n = mm(ONESC[:, :], nrmsc[:, 0:8], PB[7][0:1, 0:8], True, True, inc="pe")
    P.wait("scalar", "pe", c_n)
    c_sa = act(nrmsc[0:1, 16:24], PB[7][0:1, 0:8], AF.Sqrt)
    P.wait("vector", "act", c_sa)
    ts(nrmsc[0:1, 16:24], nrmsc[0:1, 16:24], 1e-6, None, ALU.add)
    c_r = recip(nrmsc[0:1, 16:24], nrmsc[0:1, 16:24])
    P.wait("tensor", "dve", c_r)
    c_b = mm(ONES1[:, :], nrmsc[0:1, 16:24], PB[7][:, 0:8], True, True, inc="pe")
    P.wait("vector", "pe", c_b)
    c_kn = tt(KNH[:, :], KNT[:, :], PB[7][:, 0:8], ALU.mult)
    P.wait("sync", "dve", c_kn)
    P.wait("gpsimd", "dve", c_kn)

    P.wait("gpsimd", "gps", P.ct("gps"))
    P.wait("gpsimd", "dve", P.ct("dve"))
    P.dma("gpsimd", d_e3in.ap()[0:2048, :], d_hv1own.ap())
    P.dma("gpsimd", d_e3in.ap()[2048:2112, :].rearrange("a (b c) -> (a b) c", b=16),
          d_hk1.ap()[:, RPC - 64:RPC])
    for fc in range(NFC):
        P.dma("gpsimd",
              d_e3in.ap()[2112 + fc * 8:2112 + (fc + 1) * 8, :]
              .rearrange("a (b c) -> (a b) c", b=16),
              XNT[:, (fc + 1) * RPC - 64:(fc + 1) * RPC])
    P.dma("gpsimd", d_e3in.ap()[2176:2240, :].rearrange("a (b c) -> (a b) c", b=2),
          KFOWN)
    P.dma("gpsimd", d_e3in.ap()[2240:2304, :].rearrange("a (b c) -> (a b) c", b=2),
          VNOWN)
    mrow = POOL[0:1, 5504:6528]
    cms = P.op("gpsimd", lambda e, m=mrow: e.memset(m, 0.0))
    P.wait("vector", "gps", cms)
    cmc = copy_ev(mrow[:, 0:1], XSB[0:1, 1:2])
    P.wait("gpsimd", "dve", cmc)
    ce3s = P.dma("gpsimd", d_e3in.ap()[2304:2305, :], mrow)
    P.wait("gpsimd", "qg", ce3s)
    c_e3 = collective("AllGather", ALU.bypass, d_e3in.ap()[0:2304, :],
                      d_e3outp.ap(), groups=PAIRS)
    c_e3 = collective("AllGather", ALU.bypass, d_e3in.ap()[2304:2305, :],
                      d_e3max.ap())
    P.wait("gpsimd", "cc", c_e3)

    for half in (0, 1):
        P.dma("gpsimd", d_hv1all.ap()[half * 2048:(half + 1) * 2048, :],
              d_e3outp.ap()[half * 2304:half * 2304 + 2048, :])

    def g_e3asm(eng):
        par2 = eng.partition_id() % 2
        for par in eng.Switch(par2, 2):
            pr = 1 - par
            eng.dma_start(out=d_bndk.ap(),
                          in_=d_e3outp.ap()[pr * 2304 + 2048:pr * 2304 + 2112, :]
                          .rearrange("a (b c) -> (a b) c", b=16)
                          ).then_inc(P.sems["qg"], 16)
            eng.dma_start(out=d_bndx.ap(),
                          in_=d_e3outp.ap()[pr * 2304 + 2112:pr * 2304 + 2176, :]
                          .rearrange("a (b c) -> (a b) c", b=16)
                          ).then_inc(P.sems["qg"], 16)
            for src_r, dst_off in ((par, 64 * par), (pr, 64 * (1 - par))):
                eng.dma_start(
                    out=KFALL[:, :].rearrange("p (h c) -> p h c", h=H)[:, :, dst_off:dst_off + 64],
                    in_=d_e3outp.ap()[src_r * 2304 + 2176:src_r * 2304 + 2240, :]
                    .rearrange("a (b c) -> (a b) c", b=2)
                    .rearrange("p (h c) -> p h c", h=H)
                    ).then_inc(P.sems["qg"], 16)
                eng.dma_start(
                    out=VNALL[:, :].rearrange("p (h c) -> p h c", h=H)[:, :, dst_off:dst_off + 64],
                    in_=d_e3outp.ap()[src_r * 2304 + 2240:src_r * 2304 + 2304, :]
                    .rearrange("a (b c) -> (a b) c", b=2)
                    .rearrange("p (h c) -> p h c", h=H)
                    ).then_inc(P.sems["qg"], 16)
    P._emit("gpsimd", g_e3asm)
    P.cnt["qg"] = P.cnt.get("qg", 0) + 16 * 6
    P.wait("sync", "cc", P.ct("cc"))
    for r_ in range(8):
        cmx = P.dma("sync", XSB[0:1, r_:r_ + 1],
                    d_e3max.ap()[r_:r_ + 1, 0:1], slot="mxl")
    P.wait("vector", "mxl", cmx)
    rmax(XSB[0:1, 12:13], XSB[0:1, 0:8])
    ts(XSB[0:1, 12:13], XSB[0:1, 12:13], 1.0 / 127.0, None, ALU.mult)
    ts(XSB[0:1, 12:13], XSB[0:1, 12:13], 1e-5, None, ALU.max)
    copy_ev(XSB[0:1, 13:14], XSB[0:1, 12:13])
    cxs = recip(XSB[0:1, 13:14], XSB[0:1, 13:14])
    P.wait("tensor", "dve", cxs)
    mm(ONES1[:, :], XSB[0:1, 12:14], PB[7][:, 0:2], True, True, inc="pe")
    P.wait("vector", "pe", P.ct("pe"))
    cxb = copy_ev(XSB[:, 14:16], PB[7][:, 0:2])
    P.wait("tensor", "dve", cxb)
    P.barrier()

    if upto == "p10a":
        for ic in range(NIC):
            P.dma("sync", out[ic * 128:(ic + 1) * 128, :],
                  XT[:, ic * 1024:(ic + 1) * 1024])
        P.wait_now("sync", "qs")
        return P
    d_hraug = P.dram("hraug", [H * 130, RPC])
    d_hstatc = P.dram("hstatc", [H * 128, 32])
    d_hstatr = P.dram("hstatr", [H * 4, RPC])
    d_rowb = P.dram("rowb", [4, RPC])

    QH = pool(0, RPC)
    KPAD = pool(2048, 2112)
    XVPAD = pool(4160, 2112)
    SCR = pool(6272, 2048)
    RB = pool(8320, 2048)
    STL = pool(10368, 128)
    BMK = pool(10496, 768)
    CEV = pool(11264, 128)
    ROWB = 11392
    COLS = pool(13440, 64)
    BRS = pool(13504, 192)
    BRS2 = pool(13696, 192)
    UEV = pool(13888, 64)

    def rowA(base, w=RPC):
        return POOL[base:base + 1, ROWB:ROWB + w]

    sS1 = rowA(0)
    sSq = POOL[0:1, 6272:6272 + RPC]
    sNul = POOL[0:1, 2048:2048 + RPC]
    sMu = POOL[0:1, 4160:4160 + RPC]

    cbm = P.dma("sync", BMK, bmask_i2, slot="rowld")
    P.wait("vector", "rowld", cbm)
    P.wait("scalar", "rowld", cbm)

    for h in range(H):
        cq = P.dma("sync", QH, d_hq1.ap()[h * 128:(h + 1) * 128, :], slot="l1a")
        P.dma("sync", KPAD[:, 0:64], d_bndk.ap()[h * 128:(h + 1) * 128, :], slot="l1a")
        ckp = P.dma("sync", KPAD[:, 64:2112], d_hk1.ap()[h * 128:(h + 1) * 128, :],
                    slot="l1a")
        cbx = P.dma("sync", BRS[:, 0:64], d_bndx.ap()[h * 128:(h + 1) * 128, :], slot="l1b")
        xnh = XNT[:, h * RPC:(h + 1) * RPC]
        c_s = act(SCR, xnh, AF.Square)
        P.wait("tensor", "act", c_s)
        for w in range(NW):
            mm(ONESC[:, :], SCR[:, w * 512:(w + 1) * 512], PB[w][0:1, :], True, True,
               inc="pe" if w == NW - 1 else None)
        P.wait("scalar", "l1b", cbx)
        c_s2 = act(BRS2[:, 0:64], BRS[:, 0:64], AF.Square)
        P.wait("tensor", "act", c_s2)
        c_n2 = mm(ONESC[:, :], BRS2[:, 0:64], PB[5][0:1, 0:64], True, True, inc="pe")
        P.wait("scalar", "pe", c_n2)
        bndn = POOL[0:1, 13440:13504]
        for w in range(NW):
            act(rowA(0)[:, w * 512:(w + 1) * 512], PB[w][0:1, :], AF.Sqrt)
        c_q2 = act(bndn, PB[5][0:1, 0:64], AF.Sqrt)
        P.wait("vector", "act", c_q2)
        ts(rowA(0), rowA(0), 1e-12, None, ALU.max)
        ts(bndn, bndn, 1e-12, None, ALU.max)
        recip(rowA(0), rowA(0))
        c_r2 = recip(bndn, bndn)
        P.wait("tensor", "dve", c_r2)
        for w in range(NW):
            mm(ONES1[:, :], rowA(0)[:, w * 512:(w + 1) * 512], PB[w][:, :], True, True,
               inc="pe" if w == NW - 1 else None)
        c_bc2 = mm(ONES1[:, :], bndn, PB[5][:, 0:64], True, True, inc="pe")
        P.wait("vector", "pe", c_bc2)
        for w in range(NW):
            tt(XVPAD[:, 64 + w * 512:64 + (w + 1) * 512], xnh[:, w * 512:(w + 1) * 512],
               PB[w][:, :], ALU.mult)
        c_xv = tt(XVPAD[:, 0:64], BRS[:, 0:64], PB[5][:, 0:64], ALU.mult)
        P.wait("tensor", "dve", c_xv)
        ca_ = P.dma("sync", STL, A1[h, :, :], slot="l1c")
        P.wait("tensor", "l1c", ca_)
        c_t = transp(STL, PB[6][:, 0:128])
        P.wait("vector", "pe", c_t)
        tt(STL, STL, PB[6][:, 0:128], ALU.subtract)
        c_st = ts(STL, STL, 0.5, None, ALU.mult)
        P.wait("tensor", "dve", c_st)
        kfh = KFALL[:, h * 128:(h + 1) * 128]
        vnh = VNALL[:, h * 128:(h + 1) * 128]
        mm(QFO[:, h * 64:(h + 1) * 64], kfh, PB[7][0:64, 0:128], True, False)
        c_u = mm(STL, VNOWN_R[:, h * 64:(h + 1) * 64], PB[5][:, 0:64], True, True,
                 inc="pe")
        P.wait("scalar", "pe", c_u)
        c_ue = act(UEV, PB[5][:, 0:64], AF.Identity)
        P.wait("tensor", "act", c_ue)
        c_c2 = mm(UEV, vnh, PB[7][0:64, 0:128], False, True, inc="pe")
        P.wait("scalar", "pe", c_c2)
        c_ce = act(CEV[0:64, :], PB[7][0:64, 0:128], AF.Identity)
        P.wait("tensor", "act", c_ce)
        P.wait("sync", "pe", P.ct("pe"))
        P.wait("sync", "act", P.ct("act"))
        cwt = P.dma("sync", POOL[0:64, 6272:6272 + RPC], wtT_i, slot="msk")
        P.wait("tensor", "msk", cwt)
        for w in range(NW):
            mm(CEV[0:64, :], POOL[0:64, 6272 + w * 512:6272 + (w + 1) * 512],
               PB[w][:, :], True, True, inc="pe" if w == NW - 1 else None)
        c_rp = P.ct("pe")
        P.wait("vector", "pe", c_rp)
        for w in range(NW):
            copy_ev(RB[:, w * 512:(w + 1) * 512], PB[w][:, :])
        c_rb = P.ct("dve")
        P.wait("tensor", "dve", c_rb)
        for w in range(NW):
            mm(STL, XVPAD[:, 64 + w * 512:64 + (w + 1) * 512], PB[w][:, :], True, True,
               inc="pe" if w == NW - 1 else None)
        c_w1 = P.ct("pe")
        P.wait("vector", "pe", c_w1)
        for w in range(NW):
            copy_ev(SCR[:, w * 512:(w + 1) * 512], PB[w][:, :])
        c_w1e = P.ct("dve")
        P.wait("tensor", "dve", c_w1e)
        bm_dve = {}
        prev_bst = [0]
        P.wait("tensor", "l1a", ckp)
        for icl in range(NIC):
            bank = PB[icl % 2]
            if icl >= 2:
                P.wait("tensor", "dve", bm_dve[icl - 2])
            mm(QH[:, icl * 128:(icl + 1) * 128], KPAD[:, icl * 128:icl * 128 + 192],
               bank[:, 0:192], True, False)
            c_bm = mm(SCR[:, icl * 128:(icl + 1) * 128],
                      XVPAD[:, icl * 128:icl * 128 + 192],
                      bank[:, 0:192], False, True, inc="pe")
            P.wait("vector", "pe", c_bm)
            P.wait("vector", "bst", prev_bst[0])
            mofs = 384 if (icl == 0) else 0
            tt(BRS, bank[:, 0:192], BMK[:, mofs:mofs + 192], ALU.mult)
            rsum(COLS[:, 32 + icl:33 + icl], BRS)
            P.wait("scalar", "dve", P.ct("dve"))
            c_bq = act(BRS2, BRS, AF.Square)
            P.wait("vector", "act", c_bq)
            rsum(COLS[:, 48 + icl:49 + icl], BRS2)
            c_br = stt(BRS, bank[:, 0:192], 1.0, BMK[:, mofs + 192:mofs + 384],
                       ALU.mult, ALU.add)
            bm_dve[icl] = c_br
            P.wait("sync", "dve", c_br)
            prev_bst[0] = P.dma(
                "sync", d_hband.ap()[(h * NIC + icl) * 128:(h * NIC + icl + 1) * 128, :],
                BRS, slot="bst")
        nul_act = {}
        for w in range(NW):
            if w >= 2:
                P.wait("tensor", "act", nul_act[w - 2])
            mm(KNH[:, h:h + 1], QH[:, w * 512:(w + 1) * 512], PB[4 + (w % 2)][0:1, :],
               True, True, inc="pe")
            cnl = P.ct("pe")
            P.wait("scalar", "pe", cnl)
            nul_act[w] = act(sNul[:, w * 512:(w + 1) * 512],
                             PB[4 + (w % 2)][0:1, :], AF.Identity)
        c_nl = P.ct("act")
        P.wait("sync", "act", c_nl)
        P.dma("sync", d_hstatr.ap()[h * 4 + 0:h * 4 + 1, :], sNul, slot="bst")
        P.wait("sync", "pe", P.ct("pe"))
        cmu = P.dma("sync", SCR, muT_i, slot="msk")
        P.wait("vector", "msk", cmu)
        P.wait("vector", "act", c_nl)
        c_t1 = tt(SCR, SCR, RB, ALU.mult)
        P.wait("tensor", "dve", c_t1)
        for w in range(NW):
            mm(ONESC[:, :], SCR[:, w * 512:(w + 1) * 512], PB[w][0:1, :], True, True,
               inc="pe" if w == NW - 1 else None)
        c_s1m = P.ct("pe")
        P.wait("scalar", "pe", c_s1m)
        for w in range(NW):
            act(rowA(0)[:, w * 512:(w + 1) * 512], PB[w][0:1, :], AF.Identity)
        c_s1e = P.ct("act")
        P.wait("sync", "act", c_s1e)
        P.wait("sync", "pe", P.ct("pe"))
        cgm = P.dma("sync", SCR, gmT_i, slot="msk")
        P.wait("vector", "msk", cgm)
        tt(SCR, SCR, RB, ALU.mult)
        c_g2 = tt(SCR, SCR, RB, ALU.mult)
        P.wait("tensor", "dve", c_g2)
        P.wait("tensor", "act", c_s1e)
        for w in range(NW):
            mm(ONESC[:, :], SCR[:, w * 512:(w + 1) * 512], PB[w][0:1, :], True, False,
               inc="pe" if w == NW - 1 else None)
        c_gq = P.ct("pe")
        P.wait("sync", "pe", P.ct("pe"))
        cpx = P.dma("sync", SCR, pxT_i, slot="msk")
        csh = P.dma("sync", QH[0:127, :], RB[1:128, :], slot="shb")
        P.wait("vector", "msk", cpx)
        P.wait("vector", "shb", csh)
        tt(QH[0:127, :], QH[0:127, :], RB[0:127, :], ALU.mult)
        c_x2 = tt(QH[0:127, :], QH[0:127, :], SCR[0:127, :], ALU.mult)
        P.wait("tensor", "dve", c_x2)
        for w in range(NW):
            mm(ONESC[0:127, :], QH[0:127, w * 512:(w + 1) * 512], PB[w][0:1, :],
               False, True, inc="pe" if w == NW - 1 else None)
        c_sqm = P.ct("pe")
        P.wait("scalar", "pe", c_sqm)
        for w in range(NW):
            act(sSq[:, w * 512:(w + 1) * 512], PB[w][0:1, :], AF.Identity)
        c_sqe = P.ct("act")
        cb1 = P.ct("dve")
        P.wait("tensor", "dve", cb1)
        c_tc = transp(COLS[:, 32:48], PB[5][0:16, 0:128])
        c_tc2 = transp(COLS[:, 48:64], PB[6][0:16, 0:128], inc="pe")
        P.wait("vector", "pe", c_tc2)
        P.wait("vector", "bst", P.ct("bst"))
        copy_ev(BRS[0:16, 0:128], PB[5][0:16, 0:128])
        c_fe = copy_ev(BRS2[0:16, 0:128], PB[6][0:16, 0:128])
        P.wait("sync", "dve", c_fe)
        P.dma("sync", d_rowb.ap()[0:1, :].rearrange("a (b c) -> (a b) c", b=16),
              BRS[0:16, 0:128], slot="rb1")
        crb = P.dma("sync", d_rowb.ap()[1:2, :].rearrange("a (b c) -> (a b) c", b=16),
                    BRS2[0:16, 0:128], slot="rb1")
        P.wait("gpsimd", "rb1", crb)
        P.wait("gpsimd", "act", c_sqe)
        P.dma("gpsimd", sS1, d_rowb.ap()[0:1, :], slot="rba", accum_op=ALU.add)
        crl = P.dma("gpsimd", sSq, d_rowb.ap()[1:2, :], slot="rba",
                    accum_op=ALU.add)
        P.wait("vector", "rba", crl)
        tt(sMu, sNul, sNul, ALU.mult)
        tt(sSq, sSq, sMu, ALU.add)
        c_ad = tt(sS1, sS1, sNul, ALU.add)
        P.wait("sync", "dve", c_ad)
        cnv2 = P.dma("sync", sNul, ninv_i, slot="rb2")
        P.wait("vector", "rb2", cnv2)
        tt(sMu, sS1, sNul, ALU.mult)
        tt(sSq, sSq, sNul, ALU.mult)
        tt(sNul, sMu, sMu, ALU.mult)
        ts(sNul, sNul, -1.0, None, ALU.mult)
        tt(sSq, sSq, sNul, ALU.add)
        c_var = ts(sSq, sSq, 1e-6, None, ALU.max)
        P.wait("scalar", "dve", c_var)
        c_sg = act(sSq, sSq, AF.Sqrt)
        P.wait("vector", "act", c_sg)
        c_iv = recip(sS1, sSq)
        tt(sSq, sMu, sS1, ALU.mult)
        c_mt = ts(sSq, sSq, -1.0, None, ALU.mult)
        P.wait("sync", "dve", c_mt)
        P.dma("sync", d_rowb.ap()[2:3, :], sS1, slot="rb1")
        crr = P.dma("sync", d_rowb.ap()[3:4, :], sSq, slot="rb1")
        P.wait("sync", "rb1", crr)
        P.dma("sync", BRS[0:16, 0:128],
              d_rowb.ap()[2:3, :].rearrange("a (b c) -> (a b) c", b=16), slot="rb2")
        crl2 = P.dma("sync", BRS2[0:16, 0:128],
                     d_rowb.ap()[3:4, :].rearrange("a (b c) -> (a b) c", b=16), slot="rb2")
        P.wait("tensor", "rb2", crl2)
        c_ti = transp(BRS[0:16, 0:128], PB[5][:, 0:16], np_=16)
        c_tm = transp(BRS2[0:16, 0:128], PB[6][:, 0:16], np_=16)
        P.wait("vector", "pe", c_tm)
        copy_ev(COLS[:, 0:16], PB[5][:, 0:16])
        c_cc = copy_ev(COLS[:, 16:32], PB[6][:, 0:16])
        P.wait("sync", "dve", c_cc)
        P.dma("sync", d_hstatc.ap()[h * 128:(h + 1) * 128, :], COLS[:, 0:32], slot="bst")
        P.dma("sync", d_hstatr.ap()[h * 4 + 1:h * 4 + 2, :], sS1, slot="bst")
        cst = P.dma("sync", d_hstatr.ap()[h * 4 + 2:h * 4 + 3, :], sSq, slot="bst")
        for w in range(NW):
            c_ib = mm(ONES1[:, 0:64], rowA(0)[:, w * 512:(w + 1) * 512],
                      PB[4][0:64, :], True, True, inc="pe")
            P.wait("vector", "pe", c_ib)
            tt(KPAD[0:64, w * 512:(w + 1) * 512], RB[0:64, w * 512:(w + 1) * 512],
               PB[4][0:64, :], ALU.mult)
            c_rg = tt(XVPAD[0:64, w * 512:(w + 1) * 512], RB[64:128, w * 512:(w + 1) * 512],
                      PB[4][0:64, :], ALU.mult)
            P.wait("tensor", "dve", c_rg)
        copy_ev(KPAD[64:65, 0:RPC], sSq)
        c_rg2 = copy_ev(XVPAD[64:65, 0:RPC], sSq)
        P.wait("sync", "dve", c_rg2)
        P.dma("sync", d_hraug.ap()[h * 130:h * 130 + 65, :], KPAD[0:65, 0:RPC],
              slot="bst")
        crg3 = P.dma("sync", d_hraug.ap()[h * 130 + 65:h * 130 + 130, :],
                     XVPAD[0:65, 0:RPC], slot="bst")
        P.wait("sync", "bst", crg3)
        P.wait("vector", "bst", crg3)
        P.wait("scalar", "bst", crg3)
    P.barrier()

    dsr = P.dbg_out("statr", [H * 4, RPC])
    if dsr is not None:
        P.dma("sync", dsr, d_hstatr.ap())
        P.dma("sync", P.dbg_out("statc", [H * 128, 32]), d_hstatc.ap())
        P.dma("sync", P.dbg_out("raug", [H * 130, RPC]), d_hraug.ap())
        P.dma("sync", P.dbg_out("band", [H * NIC * 128, 192]), d_hband.ap())
        P.wait_now("sync", "qs")

    if upto == "p10":
        for ic in range(NIC):
            P.dma("sync", out[ic * 128:(ic + 1) * 128, :],
                  XT[:, ic * 1024:(ic + 1) * 1024])
        P.wait_now("sync", "qs")
        return P

    VWALL = pool(0, 4224)
    RAUGA = pool(4224, 2048)
    RAUGB = pool(6272, 2048)
    WFARC = pool(8320, 2048)
    FMK = pool(10368, 2560)
    FE = pool(12928, 512)
    FE2 = pool(13440, 512)
    FEB = [FE, FE2]
    BEXP = pool(13440, 192)
    YH = pool(13440, 128)
    PBT1 = pool(13632, 128)
    PBT2 = pool(13760, 128)
    VB1 = pool(13888, 132)
    VB2 = pool(14020, 132)
    YT = pool(14152, 128)
    ZN = pool(14280, 16)
    STC = pool(14296, 32)
    ZSC = pool(14328, 8)

    cfm = P.dma("sync", WFARC[0:65, :], wfar_i, slot="fml")
    cfm = P.dma("sync", FMK, fmask_i2, slot="fml")
    for jt in range(32):
        copy_ev(VWALL[:, jt * 132 + 128:jt * 132 + 129], ONESC[:, :])
    copy_ev(VB1[:, 128:129], ONESC[:, :])
    c_on = copy_ev(VB2[:, 128:129], ONESC[:, :])
    P.wait("tensor", "fml", cfm)
    P.wait("tensor", "dve", c_on)

    def p11_build(r):
        yev = {0: ("dve", 0), 1: ("dve", 0), 2: ("dve", 0), 3: ("dve", 0)}
        frd = {0: ("act", 0), 1: ("act", 0)}
        n_jt = 16 + 16 * r
        for h in range(H):
            P.wait("sync", "pe", P.ct("pe"))
            P.wait("sync", "dve", P.ct("dve"))
            cra = P.dma("sync", RAUGA[0:65, :], d_hraug.ap()[h * 130:h * 130 + 65, :],
                        slot="ra")
            if r == 1:
                cra = P.dma("sync", RAUGB[0:65, :],
                            d_hraug.ap()[h * 130 + 65:h * 130 + 130, :], slot="ra")
            cstc = P.dma("sync", STC, d_hstatc.ap()[h * 128:(h + 1) * 128, :], slot="ra")
            for jt in range(n_jt):
                cvw = P.dma("sync", VWALL[:, jt * 132:jt * 132 + 128],
                            d_hv1all.ap()[jt * 128:(jt + 1) * 128, h * 128:(h + 1) * 128],
                            slot="vw")
            cnr = P.dma("sync", PBT1[0:16, 0:128],
                        d_hstatr.ap()[h * 4:h * 4 + 1, :]
                        .rearrange("a (b c) -> (a b) c", b=16), slot="nr")
            P.wait("tensor", "nr", cnr)
            P.wait("tensor", "dve", P.ct("dve"))
            c_tn = transp(PBT1[0:16, 0:128], PB[2][:, 0:16], np_=16)
            P.wait("vector", "pe", c_tn)
            P.wait("vector", "ra", cstc)
            copy_ev(ZN, PB[2][:, 0:16])
            tt(ZN, ZN, STC[:, 0:16], ALU.mult)
            c_z1 = tt(ZN, ZN, STC[:, 16:32], ALU.add)
            P.wait("scalar", "dve", c_z1)
            c_zx = act(ZN, ZN, AF.Exp)
            P.wait("tensor", "ra", P.ct("ra"))
            P.wait("tensor", "vw", cvw)
            P.wait("vector", "act", c_zx)
            prev_cty = 0
            prev_yst = P.ct("yst")
            for itl in range(4):
                gs128 = 16 * r + itl * 4
                def emit_pvs(jt, c_e):
                    fe = FEB[jt % 2]
                    P.wait("tensor", "act", c_e)
                    for iq in range(4):
                        if jt == 0:
                            sem2, cnt2 = yev[iq]
                            P.wait("tensor", sem2, cnt2)
                        mm(fe[:, iq * 128:(iq + 1) * 128],
                           VWALL[:, jt * 132:jt * 132 + 129],
                           PB[3 + iq][:, 0:129], jt == 0, False,
                           inc="pe" if iq == 3 else None)

                pend = None
                for jt in range(gs128 + 4):
                    half = RAUGA if jt < 16 else RAUGB
                    js = (jt % 16) * 128
                    bk = jt % 2
                    fe = FEB[jt % 2]
                    sem, cnt = frd[bk]
                    P.wait("tensor", sem, cnt)
                    c_f = mm(WFARC[0:65, js:js + 128],
                             half[0:65, itl * 512:(itl + 1) * 512],
                             PB[bk][:, :], True, True, inc="pe")
                    if pend is not None:
                        emit_pvs(jt - 1, pend)
                    k = jt - gs128 + 1
                    if k >= 0:
                        P.wait("vector", "pe", c_f)
                        c_mk = stt(fe, PB[bk][:, :], 1.0, FMK[:, k * 512:(k + 1) * 512],
                                   ALU.mult, ALU.add)
                        frd[bk] = ("dve", c_mk)
                        P.wait("scalar", "dve", c_mk)
                        pend = act(fe, fe, AF.Exp)
                    else:
                        P.wait("scalar", "pe", c_f)
                        pend = act(fe, PB[bk][:, :], AF.Exp)
                        frd[bk] = ("act", pend)
                emit_pvs(gs128 + 3, pend)
                prev_cty = P.ct("pe")
                for iq in range(4):
                    ic = itl * 4 + iq
                    P.wait("sync", "pe", prev_cty)
                    cbb = P.dma("sync", BEXP,
                                d_hband.ap()[(h * NIC + ic) * 128:(h * NIC + ic + 1) * 128, :],
                                slot="bb")
                    gw0 = r * 2048 + ic * 128 - 64
                    if gw0 < 0:
                        P.dma("sync", VB1[0:64, 0:128],
                              d_hv1all.ap()[0:64, h * 128:(h + 1) * 128], slot="vb")
                        P.dma("sync", VB1[64:128, 0:128],
                              d_hv1all.ap()[0:64, h * 128:(h + 1) * 128], slot="vb")
                    else:
                        P.dma("sync", VB1[:, 0:128],
                              d_hv1all.ap()[gw0:gw0 + 128, h * 128:(h + 1) * 128],
                              slot="vb")
                    cvb = P.dma("sync", VB2[0:64, 0:128],
                                d_hv1all.ap()[gw0 + 128:gw0 + 192, h * 128:(h + 1) * 128],
                                slot="vb")
                    P.wait("vector", "bb", cbb)
                    c_sc = ts(BEXP, BEXP, STC[:, ic:ic + 1], STC[:, 16 + ic:17 + ic],
                              ALU.mult, ALU.add)
                    P.wait("scalar", "dve", c_sc)
                    c_be = act(BEXP, BEXP, AF.Exp)
                    P.wait("tensor", "act", c_be)
                    P.wait("tensor", "dve", P.ct("dve"))
                    c_t1 = transp(BEXP[:, 0:128], PB[2][:, 0:128])
                    c_t2 = transp(BEXP[:, 128:192], PB[7][0:64, 0:128], inc="pe")
                    P.wait("vector", "pe", c_t2)
                    copy_ev(PBT1, PB[2][:, 0:128])
                    c_pb = copy_ev(PBT2[0:64, :], PB[7][0:64, 0:128])
                    P.wait("tensor", "dve", c_pb)
                    P.wait("tensor", "vb", cvb)
                    mm(PBT1, VB1[:, 0:129], PB[3 + iq][:, 0:129], False, False)
                    c_last = mm(PBT2[0:64, :], VB2[0:64, 0:129], PB[3 + iq][:, 0:129],
                                False, True, inc="pe")
                    P.wait("vector", "pe", c_last)
                    tt(ZSC[:, 0:1], PB[3 + iq][:, 128:129], ZN[:, ic:ic + 1], ALU.add)
                    recip(ZSC[:, 1:2], ZSC[:, 0:1])
                    c_y = ts(YH, PB[3 + iq][:, 0:128], ZSC[:, 1:2], None, ALU.mult)
                    yev[iq] = ("dve", c_y)
                    P.wait("tensor", "dve", c_y)
                    c_ty = transp(YH, PB[2][:, 0:128], inc="pe")
                    prev_cty = c_ty
                    P.wait("vector", "pe", c_ty)
                    P.wait("vector", "yst", prev_yst)
                    c_yt = copy_ev(YT, PB[2][:, 0:128])
                    P.wait("sync", "dve", c_yt)
                    prev_yst = P.dma("sync",
                                     d_hy.ap()[h * 128:(h + 1) * 128,
                                               ic * 128:(ic + 1) * 128],
                                     YT, slot="yst")

    P.parity(lambda: p11_build(0), lambda: p11_build(1))
    P.barrier()

    dy = P.dbg_out("y1", [H * 128, RPC])
    if dy is not None:
        P.dma("sync", dy, d_hy.ap())
        P.wait_now("sync", "qs")

    if upto == "p11":
        for ic in range(NIC):
            P.dma("sync", out[ic * 128:(ic + 1) * 128, :],
                  XT[:, ic * 1024:(ic + 1) * 1024])
        P.wait_now("sync", "qs")
        return P

    W8S = P.sb("W8S", 128, 1024, I8)
    WTQ = P.sb("WTQ", 128, 128, BF16)
    XQG = [P.sb("XQG0", 128, 1024, BF16), P.sb("XQG1", 128, 1024, BF16)]
    WTile = [P.sb("WT0", 128, 512, BF16), P.sb("WT1", 128, 512, BF16)]

    WSB = pool(8192, 1024)
    FBB = pool(9216, 1024)
    LQ = pool(0, 2048)
    LR = pool(2048, 2048)
    LC = pool(4096, 2048)
    c_rw = P.dma("sync", POOL[0:1, 6144:7168], wscale, slot="wol")
    c_rw = P.dma("sync", POOL[0:1, 7168:8192], fbias, slot="wol")
    P.wait("tensor", "wol", c_rw)
    P.wait("vector", "wol", c_rw)
    bcast_row(POOL[0:1, 6144:7168], WSB, 1024, PB[6])
    c_ws = ts(WSB, WSB, XSB[:, 14:15], None, ALU.mult)
    bcast_row(POOL[0:1, 7168:8192], FBB, 1024, PB[6])
    for ic in range(NIC):
        c_fb = tt(XT[:, ic * 1024:(ic + 1) * 1024], XT[:, ic * 1024:(ic + 1) * 1024],
                  FBB, ALU.add)
    prev_q = [0, 0]
    for fc in range(NFC):
        xc = XNT[:, fc * RPC:(fc + 1) * RPC]
        ts(LQ, xc, XSB[:, 15:16], 128.5, ALU.mult, ALU.add)
        copy_ev(LR, LQ)
        for b in range(7, -1, -1):
            ts(LC, LR, float(2 ** b), float(-(2 ** b)), ALU.is_ge, ALU.mult)
            tt(LR, LR, LC, ALU.add)
        tt(LQ, LQ, LR, ALU.subtract)
        c_qd = ts(LQ, LQ, -128.0, None, ALU.add)
        for half in range(2):
            buf = XQG[half]
            P.wait("vector", f"xqs{half}", prev_q[half])
            c_cv = copy_ev(buf[:, :], LQ[:, half * 1024:(half + 1) * 1024])
            P.wait("sync", "dve", c_cv)
            prev_q[half] = P.dma(
                "sync", d_hxq.ap()[fc * 128:(fc + 1) * 128,
                                   half * 1024:(half + 1) * 1024],
                buf[:, :], slot=f"xqs{half}")

    prev_w8 = {0: 0, 1: 0}
    prev_wtq = 0
    for nb in range(8):
        cw8 = P.dma("sync", W8S[:, :], W8[nb * 128:(nb + 1) * 128, :], slot="w8l")
        P.wait("vector", "w8l", cw8)
        for kb in range(8):
            bk = kb % 2
            c_cv = copy_ev(POOL[:, 10368:10496], W8S[:, kb * 128:(kb + 1) * 128])
            P.wait("tensor", "dve", c_cv)
            P.wait("tensor", "act", prev_w8[bk])
            c_tp = transp(POOL[:, 10368:10496], PB[bk][:, 0:128])
            P.wait("scalar", "pe", c_tp)
            P.wait("scalar", "w8s", prev_wtq)
            c_ev = act(WTQ[:, :], PB[bk][:, 0:128], AF.Identity)
            prev_w8[bk] = c_ev
            P.wait("sync", "act", c_ev)
            prev_wtq = P.dma("sync",
                             d_hw8t.ap()[kb * 128:(kb + 1) * 128,
                                         nb * 128:(nb + 1) * 128],
                             WTQ[:, :], slot="w8s")
            P.wait("vector", "pe", c_tp)
    P.barrier()

    RWOALL = pool(0, 8192)
    WOSTG = pool(10240, 128)
    prev_rw = 0
    for dc in range(8):
        for eb in range(8):
            bk = (dc * 8 + eb) % 2
            cwo = P.dma("sync", WOSTG,
                        WO1[eb * 128:(eb + 1) * 128, dc * 128:(dc + 1) * 128],
                        slot="wol")
            P.wait("tensor", "wol", cwo)
            P.wait("tensor", "dve", prev_rw)
            c_tp = transp(WOSTG, PB[bk][:, 0:128])
            P.wait("vector", "pe", c_tp)
            prev_rw = copy_ev(RWOALL[:, dc * 1024 + eb * 128:dc * 1024 + eb * 128 + 128],
                              PB[bk][:, 0:128])
            P.wait("sync", "dve", prev_rw)
    P.wait("tensor", "dve", prev_rw)
    YLG = [pool(12288, 1024), pool(13312, 1024)]
    prev_yl = [0, 0]
    prev_ev = {0: 0, 1: 0}
    for icc in range(NIC):
        buf = YLG[icc % 2]
        P.wait("sync", "pe", prev_yl[icc % 2])
        for dc in range(8):
            cyl = P.dma("sync", buf[:, dc * 128:(dc + 1) * 128],
                        d_hy.ap()[dc * 128:(dc + 1) * 128,
                                  icc * 128:(icc + 1) * 128], slot=f"yl{icc % 2}")
        P.wait("tensor", f"yl{icc % 2}", P.ct(f"yl{icc % 2}"))
        for eh in range(2):
            bk = (icc * 2 + eh) % 2
            P.wait("tensor", "dve", prev_ev[bk])
            for dc in range(8):
                mm(buf[:, dc * 128:(dc + 1) * 128],
                   RWOALL[:, dc * 1024 + eh * 512:dc * 1024 + (eh + 1) * 512],
                   PB[bk][:, :], dc == 0, dc == 7, inc="pe" if dc == 7 else None)
            c_mm = P.ct("pe")
            P.wait("vector", "pe", c_mm)
            xsl = XT[:, icc * 1024 + eh * 512:icc * 1024 + (eh + 1) * 512]
            prev_ev[bk] = tt(xsl, xsl, PB[bk][:, :], ALU.add)
            prev_yl[icc % 2] = c_mm
    P.barrier()

    TMP12 = pool(0, 512)
    prev_fq = {i: 0 for i in range(8)}
    prev_ld = [0, 0]
    for icb in range(2):
        for nh in range(2):
            for kc in range(8):
                bk = kc % 2
                P.wait("sync", "pe", prev_ld[bk])
                cwt = P.dma("sync", WTile[bk][:, :],
                            d_hw8t.ap()[kc * 128:(kc + 1) * 128,
                                        nh * 512:(nh + 1) * 512], slot=f"fwl{bk}")
                cxq = P.dma("sync", XQG[bk][:, :],
                            d_hxq.ap()[kc * 128:(kc + 1) * 128,
                                       icb * 1024:(icb + 1) * 1024], slot=f"fwl{bk}")
                P.wait("tensor", f"fwl{bk}", cxq)
                if kc == 0:
                    for iq in range(8):
                        P.wait("tensor", "dve", prev_fq[iq])
                for iq in range(8):
                    mm(XQG[bk][:, iq * 128:(iq + 1) * 128], WTile[bk][:, :],
                       PB[iq][:, :], kc == 0, kc == 7,
                       inc="pe" if iq == 7 else None)
                prev_ld[bk] = P.ct("pe")
            c_mm = P.ct("pe")
            P.wait("vector", "pe", c_mm)
            for iq in range(8):
                icc = icb * 8 + iq
                tt(TMP12, PB[iq][:, :], WSB[:, nh * 512:(nh + 1) * 512], ALU.mult)
                xsl = XT[:, icc * 1024 + nh * 512:icc * 1024 + (nh + 1) * 512]
                prev_fq[iq] = tt(xsl, xsl, TMP12, ALU.add)
    P.barrier()

    for ic in range(NIC):
        P.dma("sync", out[ic * 128:(ic + 1) * 128, :],
              XT[:, ic * 1024:(ic + 1) * 1024])
    P.wait_now("sync", "qs")
    return P


def _prep_inputs(ins):
    def col8(v):
        return np.ascontiguousarray(np.asarray(v, np.float32).reshape(8, 128).T)

    base = {
        "Wq": ins["L0_attn_Wq"], "bq": col8(ins["L0_attn_bq"]),
        "Wk": ins["L0_attn_Wk"], "bk": col8(ins["L0_attn_bk"]),
        "Wv": ins["L0_attn_Wv"], "bv": np.asarray(ins["L0_attn_bv"], np.float32).reshape(1, D),
        "U1": ins["L0_ffn_U1"], "V1": ins["L0_ffn_V1"],
        "b1": np.ascontiguousarray(np.asarray(ins["L0_ffn_b1"], np.float32).reshape(32, 128).T),
        "U2": ins["L0_ffn_U2"], "V2": ins["L0_ffn_V2"],
        "b2": np.asarray(ins["L0_ffn_b2"], np.float32).reshape(1, D),
        "WQ1": ins["L1_attn_WQ"], "WK1": ins["L1_attn_WK"],
        "WV1": ins["L1_attn_WV"], "WO1": ins["L1_attn_WO"],
        "lnw": col8(ins["L1_attn_ln_w"]), "lnb": col8(ins["L1_attn_ln_b"]),
        "A1": ins["L1_attn_A"],
        "knT": np.ascontiguousarray(np.asarray(ins["L1_attn_k_null"], np.float32).reshape(H, DH).T),
        "W8": ins["L1_ffn_weight_int8"],
        "wscale": np.asarray(ins["L1_ffn_weight_scale"], np.float32).reshape(1, D),
        "fbias": np.asarray(ins["L1_ffn_bias"], np.float32).reshape(1, D),
    }
    return base


def kernel(**inputs):
    ins = {k: np.asarray(v) for k, v in inputs.items()}
    x = ins["L0_attn_minibatch"]
    base = _prep_inputs(ins)
    cst = {0: host_consts(0), 1: host_consts(1)}

    Pr = build_program()
    nc = Pr.finalize()

    in_maps = []
    for c in range(8):
        b, r = c // 2, c % 2
        m = dict(base)
        m["x"] = np.ascontiguousarray(x[b, r * RPC:(r + 1) * RPC, :])
        m.update(cst[r])
        in_maps.append(m)
    res = run_bass_kernel_spmd(nc, in_maps, core_ids=list(range(8)))
    o = np.empty((B, T, D), np.float32)
    for c in range(8):
        b, r = c // 2, c % 2
        o[b, r * RPC:(r + 1) * RPC, :] = res.results[c]["out"]
    return o
